# version 02
# baseline (speedup 1.0000x reference)
import numpy as np
import ml_dtypes
import concourse.bass as bass
import concourse.mybir as mybir
from concourse.bass_utils import run_bass_kernel_spmd

F32 = mybir.dt.float32
BF16 = mybir.dt.bfloat16
I32 = mybir.dt.int32
AF = mybir.ActivationFunctionType
OP = mybir.AluOpType
AX = mybir.AxisListType

D = 1024
DC = 8
H = 16
DH = 64
FF = 4096
FC = 32
NO = 4096
NW = 4352
NWCH = 34
NP = 3968
PADL = 128

N_PHASES = 5
DEBUG_SCRATCH = False
RACE_DETECT = False
A1_PF_TILES = 16
A1_MAIN_TILES = 17
A1_STAGES = 63
A1_SCAN = 63
USE_BARRIER = True
SKIP_TERN = False
SKIP_PHASES = frozenset()
SKIP_WT = False


class Ctx:
    def __init__(self, nc):
        self.nc = nc
        self.sem = {}
        self.cnt = {}
        self.last_wait = {}

    def add_sem(self, name, handle):
        self.sem[name] = handle
        self.cnt[name] = 0

    def wait(self, eng, tok):
        if tok is None:
            return
        if isinstance(tok, list):
            for t in tok:
                self.wait(eng, t)
            return
        s, v = tok
        if v <= 0:
            return
        key = (id(eng), s)
        if self.last_wait.get(key, 0) >= v:
            return
        eng.wait_ge(self.sem[s], v)
        self.last_wait[key] = v

    def inc(self, instr, s, amount=1):
        instr.then_inc(self.sem[s], amount)
        self.cnt[s] += amount
        return (s, self.cnt[s])

    def tok(self, s):
        return (s, self.cnt[s])


def build_program():
    nc = bass.Bass("TRN2", target_bir_lowering=False, detect_race_conditions=RACE_DETECT)
    c = Ctx(nc)

    xw = nc.dram_tensor("xw", [D, NW], F32, kind="ExternalInput").ap()
    xp = nc.dram_tensor("xp", [D, NP], F32, kind="ExternalInput").ap()
    gate = nc.dram_tensor("gate", [128, 1], F32, kind="ExternalInput").ap()
    gater = nc.dram_tensor("gater", [128, 1], F32, kind="ExternalInput").ap()
    Wq = nc.dram_tensor("Wq", [D, D], F32, kind="ExternalInput").ap()
    Wk = nc.dram_tensor("Wk", [D, D], F32, kind="ExternalInput").ap()
    Wv = nc.dram_tensor("Wv", [D, D], F32, kind="ExternalInput").ap()
    Wo = nc.dram_tensor("Wo", [D, D], F32, kind="ExternalInput").ap()
    bq_i = nc.dram_tensor("bq_i", [128, 8], F32, kind="ExternalInput").ap()
    bk_i = nc.dram_tensor("bk_i", [128, 8], F32, kind="ExternalInput").ap()
    bv_i = nc.dram_tensor("bv_i", [1, D], F32, kind="ExternalInput").ap()
    bo_i = nc.dram_tensor("bo_i", [128, 8], F32, kind="ExternalInput").ap()
    w1 = nc.dram_tensor("w1", [FF, D], F32, kind="ExternalInput").ap()
    w2 = nc.dram_tensor("w2", [D, FF], F32, kind="ExternalInput").ap()
    cw_i = nc.dram_tensor("cw_i", [128, FC, 3], F32, kind="ExternalInput").ap()
    cb_i = nc.dram_tensor("cb_i", [128, FC], F32, kind="ExternalInput").ap()
    id_i = nc.dram_tensor("id_i", [128, 128], F32, kind="ExternalInput").ap()
    mask_i = nc.dram_tensor("mask_i", [128, H, 128], F32, kind="ExternalInput").ap()
    sd_i = nc.dram_tensor("sd_i", [128, DC, H], F32, kind="ExternalInput").ap()
    sb_i = nc.dram_tensor("sb_i", [16, DC, 128], F32, kind="ExternalInput").ap()
    o1_i = nc.dram_tensor("o1_i", [1, 128], F32, kind="ExternalInput").ap()
    oc_i = nc.dram_tensor("oc_i", [128, 1], F32, kind="ExternalInput").ap()

    yT = nc.dram_tensor("yT", [D, NO], F32, kind="ExternalOutput").ap()

    sk = "ExternalOutput" if DEBUG_SCRATCH else "Internal"
    att_scr = nc.dram_tensor("att_scr", [D, NW], F32, kind=sk).ap()
    x2_scr = nc.dram_tensor("x2_scr", [D, NW], F32, kind=sk).ap()
    xq1_scr = nc.dram_tensor("xq1_scr", [D, NW], BF16, kind=sk).ap()
    st1_scr = nc.dram_tensor("st1_scr", [1, NW], F32, kind=sk).ap()
    t1t_scr = nc.dram_tensor("t1t_scr", [D, FF], BF16, kind=sk).ap()
    t2t_scr = nc.dram_tensor("t2t_scr", [FF, D], BF16, kind=sk).ap()
    xq2_scr = nc.dram_tensor("xq2_scr", [FF, NO], BF16, kind=sk).ap()
    dbg1 = nc.dram_tensor("dbg1", [128, 8], F32, kind=sk).ap()
    dbg2 = nc.dram_tensor("dbg2", [128, 1024], F32, kind=sk).ap()
    dbg3 = nc.dram_tensor("dbg3", [128, 1024], F32, kind=sk).ap()
    dbg4 = nc.dram_tensor("dbg4", [128, 1024], F32, kind=sk).ap()
    dbg5 = nc.dram_tensor("dbg5", [128, 8, 128], F32, kind=sk).ap()
    dbg6 = nc.dram_tensor("dbg6", [16, 2, 128], F32, kind=sk).ap()
    dbg7 = nc.dram_tensor("dbg7", [128, 16, 128], F32, kind=sk).ap()
    dbg8 = nc.dram_tensor("dbg8", [128, 32, 258], F32, kind=sk).ap()
    dbg9 = nc.dram_tensor("dbg9", [1, 4, 258], F32, kind=sk).ap()
    cs2_scr = nc.dram_tensor("cs2_scr", [1, NO], F32, kind=sk).ap()

    env = {}
    env["att_v"] = att_scr.rearrange("(dc p) t -> p dc t", p=128)
    env["x2_v"] = x2_scr.rearrange("(dc p) t -> p dc t", p=128)
    env["xq1_v"] = xq1_scr.rearrange("(dc p) t -> p dc t", p=128)
    env["xw_v"] = xw.rearrange("(dc p) t -> p dc t", p=128)
    env["xp_v"] = xp.rearrange("(dc p) t -> p dc t", p=128)
    env["t1t_v"] = t1t_scr.rearrange("(dc p) f -> p dc f", p=128)
    env["t2t_v"] = t2t_scr.rearrange("(fc p) d -> p fc d", p=128)
    env["xq2_v"] = xq2_scr.rearrange("(fc p) t -> p fc t", p=128)
    env["y_v"] = yT.rearrange("(dc p) t -> p dc t", p=128)
    env["dbg5"], env["dbg6"], env["dbg7"] = None, None, None
    env["dbg8"], env["dbg9"] = None, None
    env["st1_scr"] = st1_scr
    env["cs2_scr"] = cs2_scr
    env["Wo"] = Wo

    sy, ve, sc, te, gp = nc.sync, nc.vector, nc.scalar, nc.tensor, nc.gpsimd

    import contextlib
    _es = contextlib.ExitStack()
    if True:
        ident = _es.enter_context(nc.sbuf_tensor("ident", [128, 128], F32))
        mask = _es.enter_context(nc.sbuf_tensor("mask", [128, H, 128], F32))
        selden = _es.enter_context(nc.sbuf_tensor("selden", [128, DC, H], F32))
        selbc = _es.enter_context(nc.sbuf_tensor("selbc", [16, DC, 128], F32))
        ones1 = _es.enter_context(nc.sbuf_tensor("ones1", [1, 128], F32))
        onescol = _es.enter_context(nc.sbuf_tensor("onescol", [128, 1], F32))
        gate_s = _es.enter_context(nc.sbuf_tensor("gate_s", [128, 1], F32))
        gater_s = _es.enter_context(nc.sbuf_tensor("gater_s", [128, 1], F32))
        bq_s = _es.enter_context(nc.sbuf_tensor("bq_s", [128, 8], F32))
        bk_s = _es.enter_context(nc.sbuf_tensor("bk_s", [128, 8], F32))
        bo_s = _es.enter_context(nc.sbuf_tensor("bo_s", [128, 8], F32))
        bv_row = _es.enter_context(nc.sbuf_tensor("bv_row", [1, D], F32))
        cw_s = _es.enter_context(nc.sbuf_tensor("cw_s", [128, FC, 3], F32))
        cb_s = _es.enter_context(nc.sbuf_tensor("cb_s", [128, FC], F32))
        rsw1 = _es.enter_context(nc.sbuf_tensor("rsw1", [1, 1], F32))
        rsw2 = _es.enter_context(nc.sbuf_tensor("rsw2", [1, 1], F32))
        S_sb = _es.enter_context(nc.sbuf_tensor("S_sb", [128, DC, DH], F32))
        z_sb = _es.enter_context(nc.sbuf_tensor("z_sb", [128, DC], F32))
        zeros128 = _es.enter_context(nc.sbuf_tensor("zeros128", [128, 128], F32))
        tcol = _es.enter_context(nc.sbuf_tensor("tcol", [128, 4], F32))
        rwa = _es.enter_context(nc.sbuf_tensor("rwa", [1, 544], F32))
        rwb = _es.enter_context(nc.sbuf_tensor("rwb", [1, 544], F32))
        rwc = _es.enter_context(nc.sbuf_tensor("rwc", [1, 544], F32))
        rwd = _es.enter_context(nc.sbuf_tensor("rwd", [1, 544], F32))
        ps = _es.enter_context(nc.psum_tensor("ps", [128, 8, 512], F32))
        _s1 = _es.enter_context(nc.semaphore("si"))
        _s2 = _es.enter_context(nc.semaphore("so"))
        _s3 = _es.enter_context(nc.semaphore("sp"))
        _s4 = _es.enter_context(nc.semaphore("sv"))
        _s5 = _es.enter_context(nc.semaphore("ss"))
        _s6 = _es.enter_context(nc.semaphore("sg"))
        _blk = _es.enter_context(nc.Block())
    if True:
        for nm, h in [("si", _s1), ("so", _s2), ("sp", _s3), ("sv", _s4),
                      ("ss", _s5), ("sg", _s6)]:
            c.add_sem(nm, h)

        def dsem(name):
            if name not in c.sem:
                c.add_sem(name, _es.enter_context(nc.semaphore(name)))
            return name

        def load(dst, src, sem, pre=None):
            s = dsem(sem)
            c.wait(sy, pre)
            return c.inc(sy.dma_start(out=dst, in_=src), s, 16)

        def store(dst, src, sem, pre=None):
            s = dsem(sem)
            c.wait(gp, pre)
            return c.inc(gp.dma_start(out=dst, in_=src), s, 16)

        def full_barrier():
            if not USE_BARRIER:
                return
            for eng in (te, ve, sc, gp, sy):
                for s in list(c.sem):
                    c.wait(eng, c.tok(s))

        env["full_barrier"] = full_barrier
        env["dbg5"], env["dbg6"], env["dbg7"] = dbg5, dbg6, dbg7
        env["dbg8"], env["dbg9"] = dbg8, dbg9
        env.update(dict(ps=ps, ident=ident, mask=mask, selden=selden, selbc=selbc,
                        ones1=ones1, onescol=onescol, gate_s=gate_s, gater_s=gater_s,
                        bq_s=bq_s, bk_s=bk_s, bo_s=bo_s, bv_row=bv_row, cw_s=cw_s,
                        cb_s=cb_s, rsw1=rsw1, rsw2=rsw2, S_sb=S_sb, z_sb=z_sb,
                        zeros128=zeros128, tcol=tcol, rwa=rwa, rwb=rwb, rwc=rwc,
                        rwd=rwd, load=load, store=store))

        for dst, src in [(ident[:], id_i), (mask[:], mask_i), (selden[:], sd_i),
                         (selbc[:], sb_i), (ones1[:], o1_i), (onescol[:], oc_i),
                         (gate_s[:], gate), (gater_s[:], gater), (bq_s[:], bq_i),
                         (bk_s[:], bk_i), (bo_s[:], bo_i), (bv_row[:], bv_i),
                         (cw_s[:], cw_i), (cb_s[:], cb_i)]:
            load(dst, src, "Lconst")
        tok_consts = c.tok("Lconst")
        c.wait(ve, tok_consts)
        c.wait(te, tok_consts)
        c.wait(sc, tok_consts)
        c.wait(gp, tok_consts)
        ve.memset(zeros128[:], 0.0)
        ve.memset(S_sb[:], 0.0)
        c.inc(ve.memset(z_sb[:], 0.0), "sv")

        with (
            nc.sbuf_tensor("wstage", [128, 4096], F32) as wstage,
            nc.sbuf_tensor("qy", [128, 1024], F32) as qy,
            nc.sbuf_tensor("qti", [128, 1024], I32) as qti,
            nc.sbuf_tensor("qa", [128, 1024], F32) as qa,
            nc.sbuf_tensor("qb", [128, 1024], F32) as qb,
            nc.sbuf_tensor("tern", [128, 1024], F32) as tern,
            nc.sbuf_tensor("absac", [128, 40], F32) as absac,
            nc.sbuf_tensor("abscol", [128, 2], F32) as abscol,
            nc.sbuf_tensor("swcol", [128, 2], F32) as swcol,
        ):
            t_ws_free = None
            _wlist = [] if SKIP_TERN else [(w1, 32, 1024), (w2, 8, 4096)]
            for wi, (wt, ntile, tw) in enumerate(_wlist):
                for i in range(ntile):
                    tl = load(wstage[:, 0:tw], wt[i * 128:(i + 1) * 128, :], "Lws", pre=t_ws_free)
                    col = i if wi == 0 else 32 + i
                    c.wait(ve, tl)
                    t_ws_free = c.inc(
                        ve.tensor_reduce(out=absac[:, col:col + 1], in_=wstage[:, 0:tw],
                                         axis=AX.X, op=OP.add, apply_absolute_value=True),
                        "sv")
            c.wait(ve, t_ws_free)
            ve.tensor_reduce(out=abscol[:, 0:1], in_=absac[:, 0:32], axis=AX.X, op=OP.add)
            t_red = c.inc(ve.tensor_reduce(out=abscol[:, 1:2], in_=absac[:, 32:40],
                                           axis=AX.X, op=OP.add), "sv")
            if SKIP_TERN:
                c.wait(ve, c.tok("Lconst"))
                ve.memset(rsw1[:], 1.0)
                c.inc(ve.memset(rsw2[:], 1.0), "sv")
            c.wait(te, t_red)
            t_mm = c.inc(te.matmul(ps[0:1, 0, 0:2], onescol[:], abscol[:],
                                   start=True, stop=True), "sp")
            c.wait(ve, t_mm)
            t_f = c.inc(ve.tensor_scalar(out=rwa[0:1, 0:2], in0=ps[0:1, 0, 0:2],
                                         scalar1=1.0 / (FF * D), scalar2=1e-5,
                                         op0=OP.mult, op1=OP.max), "sv")
            c.wait(ve, t_f)
            t_f = c.inc(ve.reciprocal(out=rwb[0:1, 0:2], in_=rwa[0:1, 0:2]), "sv")
            c.wait(ve, t_f)
            t_f = c.inc(ve.reciprocal(out=rwc[0:1, 0:2], in_=rwb[0:1, 0:2]), "sv")
            c.wait(ve, t_f)
            ve.tensor_copy(out=rsw1[:], in_=rwc[0:1, 0:1])
            t_rw = c.inc(ve.tensor_copy(out=rsw2[:], in_=rwc[0:1, 1:2]), "sv")
            c.wait(te, t_rw)
            t_mm = c.inc(te.matmul(ps[:, 0, 0:2], ones1[:], rwb[0:1, 0:2],
                                   start=True, stop=True), "sp")
            c.wait(ve, t_mm)
            t_sw = c.inc(ve.tensor_copy(out=swcol[:], in_=ps[:, 0, 0:2]), "sv")
            if DEBUG_SCRATCH:
                store(dbg1[:, 0:2], swcol[:], "Sdbg", pre=t_sw)
                store(dbg1[:, 2:4], abscol[:], "Sdbg", pre=t_red)
                store(dbg1[0:1, 4:6], rwa[0:1, 0:2], "Sdbg", pre=t_rw)
                store(dbg1[0:1, 6:8], rwb[0:1, 0:2], "Sdbg", pre=t_rw)

            t_spill = None
            t_ws_free2 = t_ws_free
            for wi, (wt, ntile, tw) in enumerate(_wlist):
                tsb_shape = [128, DC, FF] if wi == 0 else [128, FC, D]
                with nc.sbuf_tensor(f"tts{wi}", tsb_shape, BF16) as tts:
                    c.wait(sc, t_spill)
                    t_tern_free = None
                    tr_evac = [None] * 4
                    t_prev_round = None
                    t_ti_prev = None
                    for i in range(ntile):
                        tl = load(wstage[:, 0:tw], wt[i * 128:(i + 1) * 128, :],
                                  "Lws", pre=t_ws_free2)
                        for s4 in range(tw // 1024):
                            wsl = wstage[:, s4 * 1024:(s4 + 1) * 1024]
                            c.wait(ve, [tl, t_sw, t_tern_free, t_prev_round, t_ti_prev])
                            t_y = c.inc(ve.tensor_single_scalar(
                                out=qy[:], in_=wsl, scalar=swcol[:, wi:wi + 1],
                                op=OP.mult), "sv")
                            c.wait(sc, [t_y, t_prev_round])
                            t_ti = c.inc(sc.activation(out=qti[:], in_=qy[:],
                                                       func=AF.Identity), "ss")
                            t_ti_prev = t_ti
                            c.wait(ve, t_ti)
                            t_a = c.inc(ve.scalar_tensor_tensor(
                                out=qa[:], in0=qti[:], scalar=0.5, in1=qy[:],
                                op0=OP.add, op1=OP.is_le), "sv")
                            ve.scalar_tensor_tensor(
                                out=qb[:], in0=qti[:], scalar=0.5, in1=qy[:],
                                op0=OP.subtract, op1=OP.is_gt)
                            ve.scalar_tensor_tensor(out=qa[:], in0=qb[:], scalar=-1.0,
                                                    in1=qa[:], op0=OP.mult, op1=OP.add)
                            ve.tensor_tensor(out=qa[:], in0=qa[:], in1=qti[:], op=OP.add)
                            t_tern = c.inc(ve.tensor_scalar(out=tern[:], in0=qa[:],
                                                            scalar1=-1.0, scalar2=1.0,
                                                            op0=OP.max, op1=OP.min), "sv")
                            t_prev_round = t_tern
                            if DEBUG_SCRATCH and wi == 0 and i == 0:
                                store(dbg2[:], tern[:], "Sdbg", pre=t_tern)
                            if DEBUG_SCRATCH and wi == 0 and i == 31:
                                store(dbg3[:], tern[:], "Sdbg", pre=t_tern)
                                store(dbg4[:], qy[:], "Sdbg", pre=t_y)
                            for dcb in range(8):
                                qb_ = dcb % 4
                                c.wait(te, [t_tern, tr_evac[qb_]])
                                t_tr = c.inc(te.transpose(
                                    ps[:, 4 + qb_, 0:128],
                                    tern[:, dcb * 128:(dcb + 1) * 128], ident[:]), "sp")
                                c.wait(sc, t_tr)
                                if wi == 0:
                                    dst = tts[:, dcb, i * 128:(i + 1) * 128]
                                else:
                                    dst = tts[:, s4 * 8 + dcb, i * 128:(i + 1) * 128]
                                tr_evac[qb_] = c.inc(sc.copy(out=dst,
                                                             in_=ps[:, 4 + qb_, 0:128]), "ss")
                            t_tern_free = c.tok("sp")
                        t_ws_free2 = c.tok("sv")
                    t_spill = store(env["t1t_v"][:, :, :] if wi == 0 else env["t2t_v"][:, :, :],
                                    tts[:], f"Stts{wi}", pre=[c.tok("ss")])

        full_barrier()
        with (
            nc.sbuf_tensor("WqTh", [128, DC, D], BF16) as WqTh,
            nc.sbuf_tensor("WqTl", [128, DC, D], BF16) as WqTl,
            nc.sbuf_tensor("WkTh", [128, DC, D], BF16) as WkTh,
            nc.sbuf_tensor("WkTl", [128, DC, D], BF16) as WkTl,
            nc.sbuf_tensor("WvTh", [128, DC, D], BF16) as WvTh,
            nc.sbuf_tensor("WvTl", [128, DC, D], BF16) as WvTl,
        ):
            with nc.sbuf_tensor("wstg2", [128, 1024], F32) as wstg2:
                t_stg_free = None
                tr_evac = [None] * 4
                for wt, wdh, wdl in ([] if SKIP_WT else
                                     [(Wq, WqTh, WqTl), (Wk, WkTh, WkTl),
                                      (Wv, WvTh, WvTl)]):
                    for m in range(8):
                        tl = load(wstg2[:], wt[m * 128:(m + 1) * 128, :], "Lws2", pre=t_stg_free)
                        for kc in range(8):
                            qb_ = kc % 4
                            c.wait(te, [tl, tr_evac[qb_]])
                            t_tr = c.inc(te.transpose(ps[:, 4 + qb_, 0:128],
                                                      wstg2[:, kc * 128:(kc + 1) * 128],
                                                      ident[:]), "sp")
                            c.wait(sc, t_tr)
                            t_hi = c.inc(sc.copy(out=wdh[:, kc, m * 128:(m + 1) * 128],
                                                 in_=ps[:, 4 + qb_, 0:128]), "ss")
                            c.wait(ve, t_hi)
                            tr_evac[qb_] = c.inc(ve.tensor_tensor(
                                out=wdl[:, kc, m * 128:(m + 1) * 128],
                                in0=ps[:, 4 + qb_, 0:128],
                                in1=wdh[:, kc, m * 128:(m + 1) * 128],
                                op=OP.subtract), "sv")
                        t_stg_free = c.tok("sp")
            env.update(dict(WqTh=WqTh, WqTl=WqTl, WkTh=WkTh, WkTl=WkTl,
                            WvTh=WvTh, WvTl=WvTl))
            env["tok_wT"] = c.tok("sv")

            if N_PHASES >= 2 and 2 not in SKIP_PHASES:
                emit_a1(nc, c, env)

        if N_PHASES >= 3 and 3 not in SKIP_PHASES:
            emit_a2(nc, c, env)
        if N_PHASES >= 4 and 4 not in SKIP_PHASES:
            emit_f1(nc, c, env)
        if N_PHASES >= 5 and 5 not in SKIP_PHASES:
            emit_f2(nc, c, env)

        for s in list(c.sem):
            c.wait(sy, c.tok(s))

    _es.close()
    return nc


def emit_a1(nc, c, env):
    sy, ve, sc, te, gp = nc.sync, nc.vector, nc.scalar, nc.tensor, nc.gpsimd
    ps = env["ps"]
    WqTh, WqTl = env["WqTh"], env["WqTl"]
    WkTh, WkTl = env["WkTh"], env["WkTl"]
    WvTh, WvTl = env["WvTh"], env["WvTl"]
    xw_v, xp_v, att_v = env["xw_v"], env["xp_v"], env["att_v"]
    bq_s, bk_s, bv_row = env["bq_s"], env["bk_s"], env["bv_row"]
    ones1, ident, mask = env["ones1"], env["ident"], env["mask"]
    selden, selbc = env["selden"], env["selbc"]
    gate_s, zeros128 = env["gate_s"], env["zeros128"]
    S_sb, z_sb = env["S_sb"], env["z_sb"]
    load, store = env["load"], env["store"]
    env["full_barrier"]()

    with (
        nc.sbuf_tensor("xin", [128, DC, 256], F32) as xin,
        nc.sbuf_tensor("qf", [128, DC, 256], F32) as qf,
        nc.sbuf_tensor("kf", [128, DC, 256], F32) as kf,
        nc.sbuf_tensor("vtm", [128, 2, D], F32) as vtm,
        nc.sbuf_tensor("ktm", [128, DC, 128], F32) as ktm,
        nc.sbuf_tensor("kcum", [128, DC, 128], F32) as kcum,
        nc.sbuf_tensor("pbuf", [128, DC, 128], F32) as pbuf,
        nc.sbuf_tensor("ssb", [128, H, 128], F32) as ssb,
        nc.sbuf_tensor("denr", [16, 128], F32) as denr,
        nc.sbuf_tensor("dbc_sb", [128, DC, 128], F32) as dbc_sb,
        nc.sbuf_tensor("dinv", [16, 128], F32) as dinv,
        nc.sbuf_tensor("outst", [128, 2, DC, 128], F32) as outst,
        nc.sbuf_tensor("ph1", [128, 2, 256], F32) as ph1,
        nc.sbuf_tensor("ph2", [128, 2, 256], F32) as ph2,
        nc.sbuf_tensor("ph3", [128, 2, 256], F32) as ph3,
        nc.sbuf_tensor("ztmp", [128, DC], F32) as ztmp,
        nc.sbuf_tensor("xhh", [128, DC, 256], BF16) as xh,
        nc.sbuf_tensor("xll", [128, DC, 256], BF16) as xl,
    ):
        st = {
            "xin_free": None,
            "xh_free": None,
            "b07_free": None,
            "b45_free": None,
            "b6_free": None,
            "S_started": [False, False],
            "phi_free": [None, None],
            "outst_free": [None, None],
            "vtm_free": [None, None],
            "S_sb_ok": None,
            "ssb_free": None,
            "kcum_free": None,
            "dbc_free": None,
            "pbuf_free": None,
            "z_ok": None,
            "sc03_free": None,
        }

        def cast_hl(xbuf, tw, t_x):
            c.wait(sc, [t_x, st["xh_free"]])
            t_xh = c.inc(sc.copy(out=xh[:, :, 0:tw], in_=xbuf[:, :, 0:tw]), "ss")
            c.wait(ve, t_xh)
            return c.inc(ve.tensor_tensor(out=xl[:, :, 0:tw], in0=xbuf[:, :, 0:tw],
                                          in1=xh[:, :, 0:tw], op=OP.subtract), "sv")

        def proj_qk(wTh, wTl, bias, dstf, tw, t_x):
            t_e1 = None
            for m in range(8):
                reg = ps[:, m, 0:tw]
                for kc in range(8):
                    if kc == 0 and m == 0:
                        c.wait(te, [t_x, st["b07_free"], st["sc03_free"],
                                    st["b45_free"], st["b6_free"]])
                    te.matmul(reg, wTh[:, kc, m * 128:(m + 1) * 128],
                              xh[:, kc, 0:tw], start=(kc == 0), stop=False)
                    te.matmul(reg, wTh[:, kc, m * 128:(m + 1) * 128],
                              xl[:, kc, 0:tw], start=False, stop=False)
                    t_mm = te.matmul(reg, wTl[:, kc, m * 128:(m + 1) * 128],
                                     xh[:, kc, 0:tw], start=False, stop=(kc == 7))
                t_mm = c.inc(t_mm, "sp")
                s = m % 2
                c.wait(sc, [t_mm, st["phi_free"][s]])
                t_e1 = c.inc(sc.activation(out=ph1[:, s, 0:tw], in_=reg, func=AF.Identity,
                                           bias=bias[:, m:m + 1]), "ss")
                c.wait(gp, t_e1)
                t_mn = c.inc(gp.tensor_single_scalar(out=ph2[:, s, 0:tw],
                                                     in_=ph1[:, s, 0:tw],
                                                     scalar=0.0, op=OP.min), "sg")
                c.wait(sc, t_mn)
                t_ex = c.inc(sc.activation(out=ph3[:, s, 0:tw], in_=ph2[:, s, 0:tw],
                                           func=AF.Exp), "ss")
                c.wait(ve, t_ex)
                st["phi_free"][s] = c.inc(
                    ve.scalar_tensor_tensor(out=dstf[:, m, 0:tw], in0=ph1[:, s, 0:tw],
                                            scalar=0.0, in1=ph3[:, s, 0:tw],
                                            op0=OP.max, op1=OP.add), "sv")
            t_all_evac = c.tok("ss")
            st["b07_free"] = t_all_evac
            st["sc03_free"] = t_all_evac
            st["b45_free"] = t_all_evac
            st["b6_free"] = t_all_evac
            return c.tok("sv")

        def proj_v(cc, t_x):
            for half in range(2):
                reg = ps[:, 4 + half, 0:512]
                for kc in range(8):
                    if kc == 0 and half == 0:
                        c.wait(te, [t_x, st["b45_free"]])
                    te.matmul(reg, xh[:, kc, cc * 128:(cc + 1) * 128],
                              WvTh[:, kc, half * 512:(half + 1) * 512],
                              start=(kc == 0), stop=False)
                    te.matmul(reg, xl[:, kc, cc * 128:(cc + 1) * 128],
                              WvTh[:, kc, half * 512:(half + 1) * 512],
                              start=False, stop=False)
                    te.matmul(reg, xh[:, kc, cc * 128:(cc + 1) * 128],
                              WvTl[:, kc, half * 512:(half + 1) * 512],
                              start=False, stop=False)
                t_mm = c.inc(te.matmul(reg, ones1[0:1, 0:128],
                                       bv_row[0:1, half * 512:(half + 1) * 512],
                                       start=False, stop=True), "sp")
            c.wait(sc, [t_mm, st["vtm_free"][cc]])
            t_ev = c.inc(sc.copy(out=vtm[:, cc, :], in_=ps[:, 4:6, 0:512]), "ss")
            st["b45_free"] = t_ev
            return t_ev

        def ktm_make(cc, t_k):
            tr_ev = [None, None]
            for dc in range(8):
                bnk = 4 + (dc % 2)
                c.wait(te, [t_k, st["b45_free"]] if dc == 0 else tr_ev[dc % 2])
                t_tr = c.inc(te.transpose(ps[:, bnk, 0:128],
                                          kf[:, dc, cc * 128:(cc + 1) * 128], ident[:]), "sp")
                c.wait(sc, t_tr)
                tr_ev[dc % 2] = c.inc(sc.copy(out=ktm[:, dc, :],
                                              in_=ps[:, bnk, 0:128]), "ss")
            st["b45_free"] = c.tok("ss")
            return c.tok("ss")

        def s_update(cc, t_ktm, t_v, t_num):
            for hi, h in enumerate(list(range(0, H, 2)) + list(range(1, H, 2))):
                p0 = (h % 2) * 64
                dc = h // 2
                if hi == 0:
                    c.wait(te, [t_ktm, t_v, st["S_sb_ok"]])
                t_mm = te.matmul(ps[p0:p0 + 64, 7, dc * 64:(dc + 1) * 64],
                                 ktm[:, dc, (h % 2) * 64:(h % 2) * 64 + 64],
                                 vtm[:, cc, h * 64:(h + 1) * 64],
                                 start=(h <= 1), stop=(hi % 8 == 7),
                                 tile_position=(0, p0))
            t_su = c.inc(t_mm, "sp")
            st["vtm_free"][cc] = t_su
            sflat = bass.AP(S_sb, 0, [[512, 128], [1, 512]])
            c.wait(ve, [t_su, t_num])
            st["S_sb_ok"] = c.inc(ve.tensor_tensor(out=sflat, in0=sflat,
                                                   in1=ps[:, 7, 0:512], op=OP.add), "sv")
            return st["S_sb_ok"]

        def scan_chunk(g, cc, is_last_main, t_q, t_k, t_ktm, t_v):
            if not (A1_SCAN & 1):
                if A1_SCAN & 32:
                    s_update(cc, t_ktm, t_v, None)
                return
            c.wait(ve, [t_k, st["kcum_free"], st["z_ok"]])
            t_scan = None
            for dc in range(8):
                t_scan = ve.tensor_tensor_scan(out=kcum[:, dc, :],
                                               data0=kf[:, dc, cc * 128:(cc + 1) * 128],
                                               data1=zeros128[:], initial=z_sb[:, dc:dc + 1],
                                               op0=OP.add, op1=OP.add)
            t_scan = c.inc(t_scan, "sv")
            c.wait(ve, [t_scan, t_q, st["pbuf_free"]])
            t_p0 = c.inc(ve.tensor_tensor(out=pbuf[:], in0=qf[:, :, cc * 128:(cc + 1) * 128],
                                          in1=kcum[:], op=OP.mult), "sv")
            c.wait(ve, t_p0)
            t_z = c.inc(ve.tensor_copy(out=bass.AP(z_sb, 0, [[8, 128], [1, 8], [1, 1]]),
                                       in_=kcum[:, :, 127:128]), "sv")
            st["z_ok"] = t_z
            t_p = t_z
            for dc in range(8):
                if dc == 0:
                    c.wait(te, [t_p, st["b6_free"]])
                t_mm = te.matmul(ps[0:16, 6, 0:128], selden[:, dc, :], pbuf[:, dc, :],
                                 start=(dc == 0), stop=(dc == 7))
            t_den = c.inc(t_mm, "sp")
            st["kcum_free"] = t_scan
            st["pbuf_free"] = t_den
            if DEBUG_SCRATCH and g == 0:
                st["pbuf_free"] = store(env["dbg5"], pbuf[:], "Sdbg", pre=[t_p, t_den])
            c.wait(ve, t_den)
            t_clamp = c.inc(ve.tensor_single_scalar(out=denr[:], in_=ps[0:16, 6, 0:128],
                                                    scalar=1e-30, op=OP.max), "sv")
            c.wait(ve, t_clamp)
            t_r = c.inc(ve.reciprocal(out=dinv[:], in_=denr[:]), "sv")
            c.wait(ve, t_r)
            t_dinv = c.inc(ve.tensor_copy(out=denr[:], in_=dinv[:]), "sv")
            st["b6_free"] = t_dinv
            if DEBUG_SCRATCH and g == 0:
                store(env["dbg6"][:, 0, :], denr[:], "Sdbg", pre=t_dinv)
                store(env["dbg6"][:, 1, :], dinv[:], "Sdbg", pre=t_dinv)
            if not (A1_SCAN & 2):
                if A1_SCAN & 32:
                    s_update(cc, t_ktm, t_v, None)
                return
            _lhs = (mask[0:16, 0, :] if A1_SCAN & 512 else None)
            _rhs = (mask[0:16, 1, :] if A1_SCAN & 256 else None)
            if A1_SCAN & 32768:
                _lhs, _rhs = mask[:, 0, :], mask[:, 1, :]
            for dc in range(8):
                if dc == 0:
                    c.wait(te, [t_dinv, st["b45_free"], st["b6_free"]])
                if A1_SCAN & 16384:
                    reg_bc = ps[:, 6 if dc < 4 else 7, (dc % 4) * 128:(dc % 4 + 1) * 128]
                else:
                    reg_bc = ps[:, 4 + dc // 4, (dc % 4) * 128:(dc % 4 + 1) * 128]
                t_mm = te.matmul(reg_bc,
                                 _lhs if _lhs is not None else selbc[:, dc, :],
                                 _rhs if _rhs is not None else dinv[:],
                                 start=(dc % 4 == 0), stop=(dc % 4 == 3))
            t_bc = c.inc(t_mm, "sp")
            c.wait(sc, [t_bc, st["dbc_free"]])
            t_dbc = c.inc(sc.copy(out=dbc_sb[:],
                                  in_=bass.AP(ps, 4 * 512, [[4096, 128], [128, 8], [1, 128]])), "ss")
            c.wait(ve, t_dbc)
            t_qp = c.inc(ve.tensor_tensor(
                out=qf[:, :, cc * 128:(cc + 1) * 128],
                in0=qf[:, :, cc * 128:(cc + 1) * 128],
                in1=dbc_sb[:], op=OP.mult), "sv")
            st["dbc_free"] = t_qp
            st["b45_free"] = t_qp
            if not (A1_SCAN & 4):
                if A1_SCAN & 32:
                    s_update(cc, t_ktm, t_v, None)
                return
            for hi, h in enumerate(list(range(0, H, 2)) + list(range(1, H, 2))):
                p0 = (h % 2) * 64
                dc = h // 2
                if hi == 0:
                    c.wait(te, [t_qp, st["sc03_free"], st["b07_free"]])
                t_mm = te.matmul(ps[:, h // 4, (h % 4) * 128:(h % 4 + 1) * 128],
                                 kf[p0:p0 + 64, dc, cc * 128:(cc + 1) * 128],
                                 qf[p0:p0 + 64, dc, cc * 128:(cc + 1) * 128],
                                 start=(h % 4 == 0), stop=(h % 4 == 3))
            t_sc = c.inc(t_mm, "sp")
            c.wait(ve, [t_sc, st["ssb_free"]])
            t_ssb = c.inc(ve.tensor_tensor(
                out=ssb[:], in0=bass.AP(ps, 0, [[4096, 128], [128, 16], [1, 128]]),
                in1=mask[:], op=OP.mult), "sv")
            st["sc03_free"] = t_ssb
            if DEBUG_SCRATCH and g == 0:
                st["ssb_free"] = store(env["dbg7"], ssb[:], "Sdbg", pre=t_ssb)
            if not (A1_SCAN & 8):
                if A1_SCAN & 32:
                    s_update(cc, t_ktm, t_v, None)
                return
            for hi, h in enumerate(list(range(0, H, 2)) + list(range(1, H, 2))):
                p0 = (h % 2) * 64
                dc = h // 2
                if hi == 0:
                    c.wait(te, [t_ssb, t_qp, st["S_sb_ok"], st["b45_free"]])
                te.matmul(ps[p0:p0 + 64, 4 + dc // 4, (dc % 4) * 128:(dc % 4 + 1) * 128],
                          vtm[:, cc, h * 64:(h + 1) * 64], ssb[:, h, :],
                          start=(h % 8 <= 1), stop=False, tile_position=(0, p0))
                t_mm = te.matmul(ps[p0:p0 + 64, 4 + dc // 4, (dc % 4) * 128:(dc % 4 + 1) * 128],
                                 S_sb[p0:p0 + 64, dc, :],
                                 qf[p0:p0 + 64, dc, cc * 128:(cc + 1) * 128],
                                 start=False, stop=(h % 8 >= 6), tile_position=(p0, p0))
            t_num = c.inc(t_mm, "sp")
            st["ssb_free"] = t_num
            sl = g % 2
            c.wait(sc, [t_num, st["outst_free"][sl]])
            t_oev = c.inc(sc.copy(out=outst[:, sl, :, :],
                                  in_=bass.AP(ps, 4 * 512, [[4096, 128], [128, 8], [1, 128]])), "ss")
            st["b45_free"] = t_oev
            st["outst_free"][sl] = store(att_v[:, :, g * 128:(g + 1) * 128],
                                         outst[:, sl, :, :], f"Satt{sl}", pre=t_oev)
            s_update(cc, t_ktm, t_v, t_num)

        pf_tiles = ([(i * 256, 256) for i in range(15)] + [(3840, 128)])[:A1_PF_TILES]
        t_su = None
        for ti_, (t0, tw) in enumerate(pf_tiles):
            b = ti_ % 2
            tl = load(xin[:, :, 0:tw], xp_v[:, :, t0:t0 + tw], f"Lxin{b}", pre=st["xin_free"])
            t_hl = cast_hl(xin, tw, tl)
            t_k = proj_qk(WkTh, WkTl, bk_s, kf, tw, t_hl)
            if A1_STAGES & 2:
                c.wait(ve, t_k)
                t_k = c.inc(ve.tensor_single_scalar(out=kf[:, :, 0:tw], in_=kf[:, :, 0:tw],
                                                    scalar=gate_s[:, 0:1], op=OP.mult), "sv")
                c.wait(ve, t_k)
                t_zr = c.inc(ve.tensor_reduce(out=ztmp[:], in_=kf[:, :, 0:tw],
                                              axis=AX.X, op=OP.add), "sv")
                c.wait(ve, t_zr)
                c.inc(ve.tensor_tensor(out=z_sb[:], in0=z_sb[:], in1=ztmp[:], op=OP.add), "sv")
            for cc in range(tw // 128):
                t_v = proj_v(cc, t_hl) if A1_STAGES & 4 else None
                t_ktm = ktm_make(cc, t_k) if A1_STAGES & 8 else None
                if A1_STAGES & 16:
                    t_su = s_update(cc, t_ktm, t_v, None)
            st["xin_free"] = t_hl
            st["xh_free"] = c.tok("sp")

        for ti_ in range(A1_MAIN_TILES):
            b = ti_ % 2
            t0 = ti_ * 256
            tl = load(xin[:, :, 0:256], xw_v[:, :, t0:t0 + 256], f"Lxin{b}", pre=st["xin_free"])
            if A1_STAGES & 128:
                c.wait(ve, tl)
                ve.tensor_single_scalar(out=qf[:], in_=xin[:, :, 0:256], scalar=1.0, op=OP.max)
                t_q = t_k = c.inc(ve.tensor_single_scalar(out=kf[:], in_=xin[:, :, 0:256],
                                                          scalar=1.0, op=OP.max), "sv")
            else:
                t_hl = cast_hl(xin, 256, tl)
                t_q = proj_qk(WqTh, WqTl, bq_s, qf, 256, t_hl)
                t_k = proj_qk(WkTh, WkTl, bk_s, kf, 256, t_hl)
            if ti_ == 0:
                c.wait(ve, [t_q, t_k])
                ve.tensor_single_scalar(out=qf[:, :, 0:128], in_=qf[:, :, 0:128],
                                        scalar=gate_s[:, 0:1], op=OP.mult)
                t_q = t_k = c.inc(
                    ve.tensor_single_scalar(out=kf[:, :, 0:128], in_=kf[:, :, 0:128],
                                            scalar=gate_s[:, 0:1], op=OP.mult), "sv")
            for cc in range(1 if A1_SCAN & 65536 else 2):
                g = ti_ * 2 + cc
                t_v = proj_v(cc, t_hl)
                t_ktm = ktm_make(cc, t_k)
                scan_chunk(g, cc, g == NWCH - 1, t_q, t_k, t_ktm, t_v)
            st["xin_free"] = t_hl
            st["xh_free"] = c.tok("sp")


def row_absmax(nc, c, env, srcap, tw, t_src, bank_t, bank_r, pre_te):
    sy, ve, sc, te, gp = nc.sync, nc.vector, nc.scalar, nc.tensor, nc.gpsimd
    ps, ident, tcol = env["ps"], env["ident"], env["tcol"]
    rwa = env["rwa"]
    nblk = tw // 128
    t_red = None
    for j in range(nblk):
        c.wait(te, [t_src, pre_te] if j == 0 else t_red)
        t_tr = c.inc(te.transpose(ps[:, bank_t, 0:128], srcap[:, j * 128:(j + 1) * 128],
                                  ident[:]), "sp")
        c.wait(ve, t_tr)
        t_red = c.inc(ve.tensor_reduce(out=tcol[:, j:j + 1], in_=ps[:, bank_t, 0:128],
                                       axis=AX.X, op=OP.max), "sv")
    for j in range(nblk):
        c.wait(te, t_red if j == 0 else None)
        t_tr2 = c.inc(te.transpose(ps[0:1, bank_r, j * 128:(j + 1) * 128],
                                   tcol[:, j:j + 1], ident[:]), "sp")
    c.wait(ve, t_tr2)
    t_row = c.inc(ve.tensor_single_scalar(out=rwa[0:1, 0:tw], in_=ps[0:1, bank_r, 0:tw],
                                          scalar=1e-5, op=OP.max), "sv")
    return t_row


def emit_a2(nc, c, env):
    sy, ve, sc, te, gp = nc.sync, nc.vector, nc.scalar, nc.tensor, nc.gpsimd
    ps = env["ps"]
    Wo, att_v, xw_v = env["Wo"], env["att_v"], env["xw_v"]
    x2_v, xq1_v, st1_scr = env["x2_v"], env["xq1_v"], env["st1_scr"]
    bo_s, ones1, onescol, ident = env["bo_s"], env["ones1"], env["onescol"], env["ident"]
    gate_s, gater_s = env["gate_s"], env["gater_s"]
    rwa, rwb, rwc, rwd = env["rwa"], env["rwb"], env["rwc"], env["rwd"]
    load, store = env["load"], env["store"]
    env["full_barrier"]()

    with (
        nc.sbuf_tensor("WoTh", [128, DC, D], BF16) as WoTh,
        nc.sbuf_tensor("WoTl", [128, DC, D], BF16) as WoTl,
    ):
        with nc.sbuf_tensor("wstg3", [128, 1024], F32) as wstg3:
            t_stg_free = None
            tr_evac = [None] * 4
            for m in range(8):
                tl = load(wstg3[:], Wo[m * 128:(m + 1) * 128, :], "Lws3", pre=t_stg_free)
                for kc in range(8):
                    qb_ = kc % 4
                    c.wait(te, [tl, tr_evac[qb_]])
                    t_tr = c.inc(te.transpose(ps[:, 4 + qb_, 0:128],
                                              wstg3[:, kc * 128:(kc + 1) * 128], ident[:]), "sp")
                    c.wait(sc, t_tr)
                    t_hi = c.inc(sc.copy(out=WoTh[:, kc, m * 128:(m + 1) * 128],
                                         in_=ps[:, 4 + qb_, 0:128]), "ss")
                    c.wait(ve, t_hi)
                    tr_evac[qb_] = c.inc(ve.tensor_tensor(
                        out=WoTl[:, kc, m * 128:(m + 1) * 128],
                        in0=ps[:, 4 + qb_, 0:128],
                        in1=WoTh[:, kc, m * 128:(m + 1) * 128],
                        op=OP.subtract), "sv")
                t_stg_free = c.tok("sp")
        t_woT = c.tok("sv")

        import contextlib
        _es2 = contextlib.ExitStack()
        ath = _es2.enter_context(nc.sbuf_tensor("ath", [128, DC, 512], BF16))
        atl = _es2.enter_context(nc.sbuf_tensor("atl", [128, DC, 512], BF16))
        attl = _es2.enter_context(nc.sbuf_tensor("attl", [128, DC, 512], F32))
        xl = _es2.enter_context(nc.sbuf_tensor("xl", [128, DC, 512], F32))
        x2b = _es2.enter_context(nc.sbuf_tensor("x2b", [128, DC, 512], F32))
        sqb = _es2.enter_context(nc.sbuf_tensor("sqb", [128, 2, 512], F32))
        yb = _es2.enter_context(nc.sbuf_tensor("yb", [128, DC, 512], F32))
        tib = _es2.enter_context(nc.sbuf_tensor("tib", [128, DC, 512], I32))
        ab = _es2.enter_context(nc.sbuf_tensor("ab", [128, DC, 512], F32))
        bb = _es2.enter_context(nc.sbuf_tensor("bb", [128, DC, 512], F32))
        xq1b = _es2.enter_context(nc.sbuf_tensor("xq1b", [128, DC, 512], BF16))
        qt1 = _es2.enter_context(nc.sbuf_tensor("qt1", [128, 1, 512], F32))
        qt2 = _es2.enter_context(nc.sbuf_tensor("qt2", [128, 1, 512], F32))

        tiles = [(i * 512, 512) for i in range(8)] + [(4096, 256)]
        t_ps_free = t_woT
        t_ath_free = None
        t_bufs_free = None
        t_round_prev = None
        t_comb_prev = None
        t_st1w_prev = None
        for ti_, (t0, tw) in enumerate(tiles):
            tl1 = load(attl[:, :, 0:tw], att_v[:, :, t0:t0 + tw], "Lattl", pre=t_bufs_free)
            tl2 = load(xl[:, :, 0:tw], xw_v[:, :, t0:t0 + tw], "Lxl", pre=t_bufs_free)
            c.wait(sc, [tl1, t_ath_free])
            t_ah = c.inc(sc.copy(out=ath[:, :, 0:tw], in_=attl[:, :, 0:tw]), "ss")
            c.wait(ve, t_ah)
            t_al = c.inc(ve.tensor_tensor(out=atl[:, :, 0:tw], in0=attl[:, :, 0:tw],
                                          in1=ath[:, :, 0:tw], op=OP.subtract), "sv")
            for m in range(8):
                for kc in range(8):
                    if m == 0 and kc == 0:
                        c.wait(te, [t_al, t_ps_free])
                    te.matmul(ps[:, m, 0:tw], WoTh[:, kc, m * 128:(m + 1) * 128],
                              ath[:, kc, 0:tw], start=(kc == 0), stop=False)
                    te.matmul(ps[:, m, 0:tw], WoTh[:, kc, m * 128:(m + 1) * 128],
                              atl[:, kc, 0:tw], start=False, stop=False)
                    t_mm = te.matmul(ps[:, m, 0:tw], WoTl[:, kc, m * 128:(m + 1) * 128],
                                     ath[:, kc, 0:tw], start=False, stop=(kc == 7))
            t_mm = c.inc(t_mm, "sp")
            t_ath_free = c.tok("sp")
            psw = bass.AP(ps, 0, [[4096, 128], [512, 8], [1, tw]])
            c.wait(ve, [t_mm, tl2, t_bufs_free])
            ve.tensor_tensor(out=x2b[:, :, 0:tw], in0=psw, in1=xl[:, :, 0:tw], op=OP.add)
            t_r1 = c.inc(ve.tensor_tensor(
                out=x2b[:, :, 0:tw], in0=x2b[:, :, 0:tw],
                in1=bass.AP(bo_s, 0, [[8, 128], [1, 8], [0, tw]]), op=OP.add), "sv")
            t_sq = [None, None]
            for dc in range(8):
                sl = dc % 2
                c.wait(sc, [t_r1, t_sq[sl]])
                t_s = c.inc(sc.activation(out=sqb[:, sl, 0:tw], in_=x2b[:, dc, 0:tw],
                                          func=AF.Square), "ss")
                c.wait(te, [t_r1, t_s])
                te.matmul(ps[0:1, 0, 0:tw], onescol[:], x2b[:, dc, 0:tw],
                          start=(dc == 0), stop=(dc == 7))
                t_sq[sl] = c.inc(te.matmul(ps[0:1, 1, 0:tw], onescol[:], sqb[:, sl, 0:tw],
                                           start=(dc == 0), stop=(dc == 7)), "sp")
            c.wait(ve, t_sq[1])
            ve.tensor_single_scalar(out=rwa[0:1, 0:tw], in_=ps[0:1, 0, 0:tw],
                                    scalar=1.0 / D, op=OP.mult)
            ve.tensor_single_scalar(out=rwb[0:1, 0:tw], in_=ps[0:1, 1, 0:tw],
                                    scalar=1.0 / D, op=OP.mult)
            ve.tensor_tensor(out=rwc[0:1, 0:tw], in0=rwa[0:1, 0:tw], in1=rwa[0:1, 0:tw],
                             op=OP.mult)
            t_var = c.inc(ve.tensor_tensor(out=rwb[0:1, 0:tw], in0=rwb[0:1, 0:tw],
                                           in1=rwc[0:1, 0:tw], op=OP.subtract), "sv")
            c.wait(sc, t_var)
            t_std = c.inc(sc.activation(out=rwb[0:1, 0:tw], in_=rwb[0:1, 0:tw],
                                        func=AF.Sqrt, bias=np.float32(1e-5)), "ss")
            c.wait(ve, t_std)
            t_rstd = c.inc(ve.reciprocal(out=rwc[0:1, 0:tw], in_=rwb[0:1, 0:tw]), "sv")
            c.wait(te, [t_rstd, t_r1])
            te.matmul(ps[:, 2, 0:tw], ones1[:], rwa[0:1, 0:tw], start=True, stop=True)
            t_bc = c.inc(te.matmul(ps[:, 3, 0:tw], ones1[:], rwc[0:1, 0:tw],
                                   start=True, stop=True), "sp")
            c.wait(ve, t_bc)
            ve.tensor_tensor(out=x2b[:, :, 0:tw], in0=x2b[:, :, 0:tw],
                             in1=bass.AP(ps, 2 * 512, [[4096, 128], [0, 8], [1, tw]]),
                             op=OP.subtract)
            t_x2 = c.inc(ve.tensor_tensor(out=x2b[:, :, 0:tw], in0=x2b[:, :, 0:tw],
                                          in1=bass.AP(ps, 3 * 512, [[4096, 128], [0, 8], [1, tw]]),
                                          op=OP.mult), "sv")
            t_st_x2 = store(x2_v[:, :, t0:t0 + tw], x2b[:, :, 0:tw], "Sx2", pre=t_x2)

            x2v = bass.AP(x2b, 0, [[8 * 512, 128], [1, tw], [512, 8]])
            c.wait(ve, [t_x2, t_comb_prev])
            t_rd = c.inc(ve.tensor_reduce(out=qt1[:, 0, 0:tw], in_=x2v, op=OP.max,
                                          axis=mybir.AxisListType.X), "sv")
            c.wait(ve, t_rd)
            t_rd2 = c.inc(ve.tensor_reduce(out=qt2[:, 0, 0:tw], in_=x2v, op=OP.min,
                                           axis=mybir.AxisListType.X), "sv")
            c.wait(ve, t_rd2)
            ve.tensor_single_scalar(out=qt2[:, 0, 0:tw], in_=qt2[:, 0, 0:tw],
                                    scalar=-1.0, op=OP.mult)
            t_am = c.inc(ve.tensor_tensor(out=qt1[:, 0, 0:tw], in0=qt1[:, 0, 0:tw],
                                          in1=qt2[:, 0, 0:tw], op=OP.max), "sv")
            t_comb_prev = t_am
            t_row = row_absmax(nc, c, env, qt1[:, 0, :], tw, t_am, 4, 5, t_x2)
            c.wait(ve, [t_row, t_st1w_prev])
            t_rc = c.inc(ve.reciprocal(out=rwb[0:1, 0:tw], in_=rwa[0:1, 0:tw]), "sv")
            c.wait(ve, t_rc)
            t_st1 = c.inc(ve.tensor_single_scalar(out=rwd[0:1, 0:tw], in_=rwb[0:1, 0:tw],
                                                  scalar=127.0, op=OP.mult), "sv")
            t_st1w_prev = store(st1_scr[0:1, t0:t0 + tw], rwd[0:1, 0:tw], "Sst1", pre=t_st1)
            c.wait(te, [t_st1, t_x2])
            t_bc2 = c.inc(te.matmul(ps[:, 2, 0:tw], ones1[:], rwd[0:1, 0:tw],
                                    start=True, stop=True), "sp")
            c.wait(ve, [t_bc2, t_st_x2])
            t_y = c.inc(ve.tensor_tensor(out=yb[:, :, 0:tw], in0=x2b[:, :, 0:tw],
                                         in1=bass.AP(ps, 2 * 512, [[4096, 128], [0, 8], [1, tw]]),
                                         op=OP.mult), "sv")
            t_ps_free = t_y
            c.wait(sc, [t_y, t_round_prev])
            t_ti = c.inc(sc.activation(out=tib[:, :, 0:tw], in_=yb[:, :, 0:tw],
                                       func=AF.Identity), "ss")
            c.wait(ve, t_ti)
            t_a = c.inc(ve.scalar_tensor_tensor(out=ab[:, :, 0:tw], in0=tib[:, :, 0:tw],
                                                scalar=0.5, in1=yb[:, :, 0:tw],
                                                op0=OP.add, op1=OP.is_le), "sv")
            ve.scalar_tensor_tensor(out=bb[:, :, 0:tw], in0=tib[:, :, 0:tw],
                                    scalar=0.5, in1=yb[:, :, 0:tw],
                                    op0=OP.subtract, op1=OP.is_gt)
            ve.scalar_tensor_tensor(out=ab[:, :, 0:tw], in0=bb[:, :, 0:tw], scalar=-1.0,
                                    in1=ab[:, :, 0:tw], op0=OP.mult, op1=OP.add)
            t_xq1 = c.inc(ve.tensor_tensor(out=xq1b[:, :, 0:tw], in0=ab[:, :, 0:tw],
                                           in1=tib[:, :, 0:tw], op=OP.add), "sv")
            t_round_prev = t_xq1
            if t0 <= 127 < t0 + tw:
                j = 127 - t0
                t_xq1 = c.inc(ve.tensor_single_scalar(out=xq1b[:, :, j:j + 1],
                                                      in_=xq1b[:, :, j:j + 1],
                                                      scalar=gate_s[:, 0:1], op=OP.mult), "sv")
            if t0 <= 4224 < t0 + tw:
                j = 4224 - t0
                t_xq1 = c.inc(ve.tensor_single_scalar(out=xq1b[:, :, j:j + 1],
                                                      in_=xq1b[:, :, j:j + 1],
                                                      scalar=gater_s[:, 0:1], op=OP.mult), "sv")
            store(xq1_v[:, :, t0:t0 + tw], xq1b[:, :, 0:tw], "Sxq1", pre=t_xq1)
            t_bufs_free = [c.tok("sp"), c.tok("sv"), c.tok("ss"), c.tok("sg"), c.tok("Sx2"), c.tok("Sxq1"), c.tok("Sst1")]
        _es2.close()


def emit_f1(nc, c, env):
    sy, ve, sc, te, gp = nc.sync, nc.vector, nc.scalar, nc.tensor, nc.gpsimd
    ps = env["ps"]
    t1t_v, xq1_v = env["t1t_v"], env["xq1_v"]
    st1_scr, xq2_v, cs2_scr = env["st1_scr"], env["xq2_v"], env["cs2_scr"]
    cw_s, cb_s, ident = env["cw_s"], env["cb_s"], env["ident"]
    ones1, rsw1, rsw2 = env["ones1"], env["rsw1"], env["rsw2"]
    rwa, rwb, rwc, rwd = env["rwa"], env["rwb"], env["rwc"], env["rwd"]
    load, store = env["load"], env["store"]
    env["full_barrier"]()

    with (
        nc.sbuf_tensor("t1t", [128, DC, FF], BF16) as t1t,
        nc.sbuf_tensor("xq1w", [128, DC, 258], BF16) as xq1w,
        nc.sbuf_tensor("hb", [128, FC, 258], F32) as hb,
        nc.sbuf_tensor("cacc", [128, FC, 256], F32) as cacc,
        nc.sbuf_tensor("cs1bc", [128, 258], F32) as cs1bc,
        nc.sbuf_tensor("ti2", [128, 8, 256], I32) as ti2,
        nc.sbuf_tensor("a2", [128, 8, 256], F32) as a2,
        nc.sbuf_tensor("b2", [128, 8, 256], F32) as b2,
        nc.sbuf_tensor("xq2b", [128, FC, 256], BF16) as xq2b,
        nc.sbuf_tensor("st1l", [1, 2, 258], F32) as st1l,
    ):
        tl_t1 = load(t1t[:], t1t_v[:, :, :], "Lt1t", pre=None)
        t_ps_free = None
        t_hb_free = None
        t_cacc_free = None
        t_row_free = [None, None]
        t_group_prev = None
        t_xq2b_free = None
        t_bc_prev = None
        t_cs2w_prev = None
        t_mn_prev = None
        for ti_ in range(16):
            b = ti_ % 2
            t0 = ti_ * 256
            w0 = PADL + t0 - 1
            tl_x = load(xq1w[:], xq1_v[:, :, w0:w0 + 258], "Lxq1w",
                        pre=c.tok("sp") if ti_ >= 1 else None)
            tl_s = load(st1l[0:1, b, :], st1_scr[0:1, w0:w0 + 258], f"Lst1{b}", pre=t_row_free[b])
            c.wait(ve, [tl_s, t_bc_prev])
            t_rc = c.inc(ve.reciprocal(out=rwa[0:1, 0:258], in_=st1l[0:1, b, :]), "sv")
            c.wait(ve, t_rc)
            t_cs1 = c.inc(ve.tensor_single_scalar(out=rwa[0:1, 0:258], in_=rwa[0:1, 0:258],
                                                  scalar=rsw1[0:1, 0:1], op=OP.mult), "sv")
            t_row_free[b] = t_cs1
            if DEBUG_SCRATCH and ti_ == 0:
                store(env["dbg9"][0:1, 0, :], st1l[0:1, b, :], "Sdbg", pre=tl_s)
                t_row_free[b] = store(env["dbg9"][0:1, 1, :], rwa[0:1, 0:258], "Sdbg",
                                      pre=t_cs1)
            c.wait(te, [t_cs1, t_ps_free])
            t_bc = c.inc(te.matmul(ps[:, 0, 0:258], ones1[:], rwa[0:1, 0:258],
                                   start=True, stop=True), "sp")
            t_bc_prev = t_bc
            c.wait(ve, t_bc)
            t_cs1b = c.inc(ve.tensor_copy(out=cs1bc[:], in_=ps[:, 0, 0:258]), "sv")

            for r in range(4):
                for fc8 in range(8):
                    fc = r * 8 + fc8
                    for dc in range(8):
                        if fc8 == 0 and dc == 0:
                            c.wait(te, [tl_x, tl_t1, t_cs1b, t_hb_free]
                                   if r == 0 else t_ps_free)
                        t_mm = te.matmul(ps[:, fc8, 0:258],
                                         t1t[:, dc, fc * 128:(fc + 1) * 128],
                                         xq1w[:, dc, :],
                                         start=(dc == 0), stop=(dc == 7))
                t_round = c.inc(t_mm, "sp")
                c.wait(ve, t_round)
                t_ev = c.inc(ve.tensor_tensor(
                    out=hb[:, r * 8:(r + 1) * 8, :],
                    in0=bass.AP(ps, 0, [[4096, 128], [512, 8], [1, 258]]),
                    in1=bass.AP(cs1bc, 0, [[258, 128], [0, 8], [1, 258]]),
                    op=OP.mult), "sv")
                t_ps_free = t_ev
            t_h = t_ps_free
            if DEBUG_SCRATCH and ti_ == 0:
                t_h = store(env["dbg8"], hb[:], "Sdbg", pre=t_ps_free)
                store(env["dbg9"][0:1, 2, :], cs1bc[0:1, :], "Sdbg", pre=t_cs1b)
                store(env["dbg9"][0:1, 3, :], cs1bc[64:65, :], "Sdbg", pre=t_cs1b)

            c.wait(ve, [t_h, t_cacc_free])
            for fc in range(FC):
                ve.tensor_scalar(out=cacc[:, fc, :], in0=hb[:, fc, 1:257],
                                 scalar1=cw_s[:, fc, 1:2], scalar2=cb_s[:, fc:fc + 1],
                                 op0=OP.mult, op1=OP.add)
            for fc in range(FC):
                ve.scalar_tensor_tensor(out=cacc[:, fc, :], in0=hb[:, fc, 0:256],
                                        scalar=cw_s[:, fc, 0:1], in1=cacc[:, fc, :],
                                        op0=OP.mult, op1=OP.add)
            for fc in range(FC):
                ve.scalar_tensor_tensor(out=cacc[:, fc, :], in0=hb[:, fc, 2:258],
                                        scalar=cw_s[:, fc, 2:3], in1=cacc[:, fc, :],
                                        op0=OP.mult, op1=OP.add)
            t_cr = c.inc(ve.nop(), "sv")
            c.wait(sc, t_cr)
            t_hs = c.inc(sc.activation(out=hb[:, :, 0:256], in_=cacc[:], func=AF.Silu), "ss")
            c.wait(sc, t_hs)
            t_abs = c.inc(sc.activation(out=cacc[:], in_=hb[:, :, 0:256], func=AF.Abs), "ss")

            c.wait(ve, t_abs)
            ve.tensor_tensor(out=cacc[:, 0:16, :], in0=cacc[:, 0:16, :],
                             in1=cacc[:, 16:32, :], op=OP.max)
            ve.tensor_tensor(out=cacc[:, 0:8, :], in0=cacc[:, 0:8, :],
                             in1=cacc[:, 8:16, :], op=OP.max)
            ve.tensor_tensor(out=cacc[:, 0:4, :], in0=cacc[:, 0:4, :],
                             in1=cacc[:, 4:8, :], op=OP.max)
            ve.tensor_tensor(out=cacc[:, 0:2, :], in0=cacc[:, 0:2, :],
                             in1=cacc[:, 2:4, :], op=OP.max)
            t_am = c.inc(ve.tensor_tensor(out=cacc[:, 0, :], in0=cacc[:, 0, :],
                                          in1=cacc[:, 1, :], op=OP.max), "sv")
            t_row = row_absmax(nc, c, env, cacc[:, 0, :], 256, t_am, 1, 2, t_h)
            c.wait(ve, [t_row, t_cs2w_prev])
            t_rc = c.inc(ve.reciprocal(out=rwb[0:1, 0:256], in_=rwa[0:1, 0:256]), "sv")
            c.wait(ve, t_rc)
            t_st2 = c.inc(ve.tensor_single_scalar(out=rwd[0:1, 0:256], in_=rwb[0:1, 0:256],
                                                  scalar=127.0, op=OP.mult), "sv")
            c.wait(ve, t_st2)
            t_rc = c.inc(ve.reciprocal(out=rwb[0:1, 0:256], in_=rwd[0:1, 0:256]), "sv")
            c.wait(ve, t_rc)
            t_cs2 = c.inc(ve.tensor_single_scalar(out=rwb[0:1, 0:256], in_=rwb[0:1, 0:256],
                                                  scalar=rsw2[0:1, 0:1], op=OP.mult), "sv")
            t_cs2w_prev = store(cs2_scr[0:1, t0:t0 + 256], rwb[0:1, 0:256], "Scs2", pre=t_cs2)
            c.wait(te, t_st2)
            t_bc2 = c.inc(te.matmul(ps[:, 3, 0:256], ones1[:], rwd[0:1, 0:256],
                                    start=True, stop=True), "sp")
            c.wait(ve, t_bc2)
            t_y = c.inc(ve.tensor_tensor(out=cacc[:], in0=hb[:, :, 0:256],
                                         in1=bass.AP(ps, 3 * 512, [[4096, 128], [0, 32], [1, 256]]),
                                         op=OP.mult), "sv")
            t_ps_free = t_y
            t_hb_free = t_y
            t_last = None
            for gq in range(4):
                slc = cacc[:, gq * 8:(gq + 1) * 8, :]
                c.wait(sc, [t_y, t_group_prev])
                t_ti = c.inc(sc.activation(out=ti2[:], in_=slc, func=AF.Identity), "ss")
                c.wait(ve, t_ti)
                t_a = c.inc(ve.scalar_tensor_tensor(out=a2[:], in0=ti2[:], scalar=0.5,
                                                    in1=slc, op0=OP.add, op1=OP.is_le), "sv")
                ve.scalar_tensor_tensor(out=b2[:], in0=ti2[:], scalar=0.5,
                                        in1=slc, op0=OP.subtract, op1=OP.is_gt)
                c.wait(ve, t_xq2b_free)
                ve.scalar_tensor_tensor(out=a2[:], in0=b2[:], scalar=-1.0, in1=a2[:],
                                        op0=OP.mult, op1=OP.add)
                t_last = c.inc(ve.tensor_tensor(out=xq2b[:, gq * 8:(gq + 1) * 8, :],
                                                in0=a2[:], in1=ti2[:], op=OP.add), "sv")
                t_group_prev = t_last
            t_cacc_free = t_last
            t_st = store(xq2_v[:, :, t0:t0 + 256], xq2b[:], "Sxq2", pre=t_last)
            t_xq2b_free = t_st


def emit_f2(nc, c, env):
    sy, ve, sc, te, gp = nc.sync, nc.vector, nc.scalar, nc.tensor, nc.gpsimd
    ps = env["ps"]
    t2t_v, xq2_v, cs2_scr = env["t2t_v"], env["xq2_v"], env["cs2_scr"]
    x2_v, y_v = env["x2_v"], env["y_v"]
    ones1, onescol = env["ones1"], env["onescol"]
    rwa, rwb, rwc, rwd = env["rwa"], env["rwb"], env["rwc"], env["rwd"]
    load, store = env["load"], env["store"]
    env["full_barrier"]()

    with (
        nc.sbuf_tensor("t2t", [128, FC, D], BF16) as t2t,
        nc.sbuf_tensor("xq2l", [128, FC, 512], BF16) as xq2l,
        nc.sbuf_tensor("x2l", [128, DC, 512], F32) as x2l,
        nc.sbuf_tensor("r2b", [128, DC, 512], F32) as r2b,
        nc.sbuf_tensor("sq2", [128, 2, 512], F32) as sq2,
        nc.sbuf_tensor("outb", [128, DC, 512], F32) as outb,
        nc.sbuf_tensor("cs2l", [1, 2, 512], F32) as cs2l,
    ):
        tl_t2 = load(t2t[:], t2t_v[:, :, :], "Lt2t", pre=None)
        t_ps_free = None
        t_bufs_free = None
        t_row_free = [None, None]
        t_sy_prev = None
        for ti_ in range(8):
            b = ti_ % 2
            t0 = ti_ * 512
            tl_x = load(xq2l[:], xq2_v[:, :, t0:t0 + 512], "Lxq2l",
                        pre=c.tok("sp") if ti_ >= 1 else None)
            tl_c = load(cs2l[0:1, b, :], cs2_scr[0:1, t0:t0 + 512], f"Lcs2{b}", pre=t_row_free[b])
            tl_x2 = load(x2l[:], x2_v[:, :, PADL + t0:PADL + t0 + 512], "Lx2l", pre=t_bufs_free)
            for m in range(8):
                for fc in range(FC):
                    if m == 0 and fc == 0:
                        c.wait(te, [tl_x, tl_t2, t_ps_free])
                    t_mm = te.matmul(ps[:, m, 0:512], t2t[:, fc, m * 128:(m + 1) * 128],
                                     xq2l[:, fc, :], start=(fc == 0), stop=(fc == FC - 1))
            t_mm = c.inc(t_mm, "sp")
            psw = bass.AP(ps, 0, [[4096, 128], [512, 8], [1, 512]])
            c.wait(ve, [t_mm, tl_c, tl_x2, t_bufs_free])
            t_cp = c.inc(ve.tensor_copy(out=r2b[:], in_=psw), "sv")
            c.wait(te, t_cp)
            t_bc = c.inc(te.matmul(ps[:, 0, 0:512], ones1[:], cs2l[0:1, b, :],
                                   start=True, stop=True), "sp")
            c.wait(ve, t_bc)
            ve.tensor_tensor(out=r2b[:], in0=r2b[:],
                             in1=bass.AP(ps, 0, [[4096, 128], [0, 8], [1, 512]]),
                             op=OP.mult)
            t_r2 = c.inc(ve.tensor_tensor(out=r2b[:], in0=r2b[:], in1=x2l[:], op=OP.add), "sv")
            t_sq = [None, None]
            for dc in range(8):
                sl = dc % 2
                c.wait(sc, [t_r2, t_sq[sl]])
                t_s = c.inc(sc.activation(out=sq2[:, sl, :], in_=r2b[:, dc, :],
                                          func=AF.Square), "ss")
                c.wait(te, [t_r2, t_s])
                te.matmul(ps[0:1, 1, 0:512], onescol[:], r2b[:, dc, :],
                          start=(dc == 0), stop=(dc == 7))
                t_sq[sl] = c.inc(te.matmul(ps[0:1, 2, 0:512], onescol[:], sq2[:, sl, :],
                                           start=(dc == 0), stop=(dc == 7)), "sp")
            c.wait(ve, t_sq[1])
            ve.tensor_single_scalar(out=rwa[0:1, 0:512], in_=ps[0:1, 1, 0:512],
                                    scalar=1.0 / D, op=OP.mult)
            ve.tensor_single_scalar(out=rwb[0:1, 0:512], in_=ps[0:1, 2, 0:512],
                                    scalar=1.0 / D, op=OP.mult)
            ve.tensor_tensor(out=rwc[0:1, 0:512], in0=rwa[0:1, 0:512], in1=rwa[0:1, 0:512],
                             op=OP.mult)
            t_var = c.inc(ve.tensor_tensor(out=rwb[0:1, 0:512], in0=rwb[0:1, 0:512],
                                           in1=rwc[0:1, 0:512], op=OP.subtract), "sv")
            c.wait(sc, t_var)
            t_std = c.inc(sc.activation(out=rwb[0:1, 0:512], in_=rwb[0:1, 0:512],
                                        func=AF.Sqrt, bias=np.float32(1e-5)), "ss")
            c.wait(ve, t_std)
            t_rstd = c.inc(ve.reciprocal(out=rwc[0:1, 0:512], in_=rwb[0:1, 0:512]), "sv")
            c.wait(te, [t_rstd])
            te.matmul(ps[:, 3, 0:512], ones1[:], rwa[0:1, 0:512], start=True, stop=True)
            t_bc2 = c.inc(te.matmul(ps[:, 4, 0:512], ones1[:], rwc[0:1, 0:512],
                                    start=True, stop=True), "sp")
            c.wait(ve, [t_bc2, t_sy_prev])
            ve.tensor_tensor(out=outb[:], in0=r2b[:],
                             in1=bass.AP(ps, 3 * 512, [[4096, 128], [0, 8], [1, 512]]),
                             op=OP.subtract)
            t_out = c.inc(ve.tensor_tensor(out=outb[:], in0=outb[:],
                                           in1=bass.AP(ps, 4 * 512, [[4096, 128], [0, 8], [1, 512]]),
                                           op=OP.mult), "sv")
            t_ps_free = t_out
            t_st = store(y_v[:, :, t0:t0 + 512], outb[:], "Sy", pre=t_out)
            t_sy_prev = t_st
            t_bufs_free = [t_st, c.tok("sv"), c.tok("ss")]
            t_row_free[b] = t_out


def _stage_inputs(inputs):
    x = np.asarray(inputs["L0_attn_x"], dtype=np.float32)
    Wq_ = np.ascontiguousarray(inputs["L0_attn_Wq"], dtype=np.float32)
    Wk_ = np.ascontiguousarray(inputs["L0_attn_Wk"], dtype=np.float32)
    Wv_ = np.ascontiguousarray(inputs["L0_attn_Wv"], dtype=np.float32)
    Wo_ = np.ascontiguousarray(inputs["L0_attn_Wo"], dtype=np.float32)
    bq = np.asarray(inputs["L0_attn_bq"], dtype=np.float32)
    bk = np.asarray(inputs["L0_attn_bk"], dtype=np.float32)
    bv = np.asarray(inputs["L0_attn_bv"], dtype=np.float32)
    bo = np.asarray(inputs["L0_attn_bo"], dtype=np.float32)
    w1_ = np.ascontiguousarray(inputs["L0_ffn_w1"], dtype=np.float32)
    w2_ = np.ascontiguousarray(inputs["L0_ffn_w2"], dtype=np.float32)
    cw = np.asarray(inputs["L0_ffn_conv_w"], dtype=np.float32).reshape(FF, 3)
    cb = np.asarray(inputs["L0_ffn_conv_b"], dtype=np.float32)

    bq_s = np.ascontiguousarray(bq.reshape(8, 128).T)
    bk_s = np.ascontiguousarray(bk.reshape(8, 128).T)
    bo_s = np.ascontiguousarray(bo.reshape(8, 128).T)
    bv_r = np.ascontiguousarray(bv.reshape(1, D))
    cw_s = np.ascontiguousarray(cw.reshape(FC, 128, 3).transpose(1, 0, 2))
    cb_s = np.ascontiguousarray(cb.reshape(FC, 128).T)

    ident = np.eye(128, dtype=np.float32)
    m0 = np.triu(np.ones((128, 128), dtype=np.float32))
    maskc = np.ascontiguousarray(np.broadcast_to(m0[:, None, :], (128, H, 128)))
    selden = np.zeros((128, DC, H), dtype=np.float32)
    selbc = np.zeros((16, DC, 128), dtype=np.float32)
    for dc in range(DC):
        for p in range(128):
            h = 2 * dc + (1 if p >= 64 else 0)
            selden[p, dc, h] = 1.0
            selbc[h, dc, p] = 1.0
    ones1 = np.ones((1, 128), dtype=np.float32)
    onescol = np.ones((128, 1), dtype=np.float32)

    shared = dict(Wq=Wq_, Wk=Wk_, Wv=Wv_, Wo=Wo_, bq_i=bq_s, bk_i=bk_s, bv_i=bv_r,
                  bo_i=bo_s, w1=w1_, w2=w2_, cw_i=cw_s, cb_i=cb_s, id_i=ident,
                  mask_i=maskc, sd_i=selden, sb_i=selbc, o1_i=ones1, oc_i=onescol)

    in_maps = []
    for core in range(8):
        bi, half = core // 2, core % 2
        xb = x[bi]
        xwin = np.zeros((D, NW), dtype=np.float32)
        if half == 0:
            xwin[:, PADL:] = xb[0:4224].T
            xpre = np.zeros((D, NP), dtype=np.float32)
            g, gr = 0.0, 1.0
        else:
            xwin[:, 0:4224] = xb[3968:8192].T
            xpre = np.ascontiguousarray(xb[0:NP].T)
            g, gr = 1.0, 0.0
        m = dict(shared)
        m["xw"] = np.ascontiguousarray(xwin)
        m["xp"] = np.ascontiguousarray(xpre)
        m["gate"] = np.full((128, 1), g, dtype=np.float32)
        m["gater"] = np.full((128, 1), gr, dtype=np.float32)
        in_maps.append(m)
    return in_maps


def kernel(**inputs):
    in_maps = _stage_inputs(inputs)
    nc = build_program()
    res = run_bass_kernel_spmd(nc, in_maps, core_ids=list(range(8)))
    out = np.empty((4, 8192, D), dtype=np.float32)
    for core in range(8):
        bi, half = core // 2, core % 2
        out[bi, half * NO:(half + 1) * NO, :] = res.results[core]["yT"].T
    return out


# version 05
# speedup vs baseline: 1.0547x; 1.0547x over previous
import numpy as np
import ml_dtypes
import concourse.bass as bass
import concourse.mybir as mybir
from concourse.bass_utils import run_bass_kernel_spmd

F32 = mybir.dt.float32
BF16 = mybir.dt.bfloat16
I32 = mybir.dt.int32
AF = mybir.ActivationFunctionType
OP = mybir.AluOpType
AX = mybir.AxisListType

D = 1024
DC = 8
H = 16
DH = 64
FF = 4096
FC = 32
NO = 4096
NW = 4352
NWCH = 34
NP = 3968
PADL = 128

N_PHASES = 5
DEBUG_SCRATCH = False
RACE_DETECT = False
A1_PF_TILES = 16
A1_MAIN_TILES = 17
A1_STAGES = 63
A1_SCAN = 63
USE_BARRIER = True
SKIP_TERN = False
SKIP_PHASES = frozenset()
SKIP_WT = False


class Ctx:
    def __init__(self, nc):
        self.nc = nc
        self.sem = {}
        self.cnt = {}
        self.last_wait = {}

    def add_sem(self, name, handle):
        self.sem[name] = handle
        self.cnt[name] = 0

    def wait(self, eng, tok):
        if tok is None:
            return
        if isinstance(tok, list):
            for t in tok:
                self.wait(eng, t)
            return
        s, v = tok
        if v <= 0:
            return
        key = (id(eng), s)
        if self.last_wait.get(key, 0) >= v:
            return
        eng.wait_ge(self.sem[s], v)
        self.last_wait[key] = v

    def inc(self, instr, s, amount=1):
        instr.then_inc(self.sem[s], amount)
        self.cnt[s] += amount
        return (s, self.cnt[s])

    def tok(self, s):
        return (s, self.cnt[s])


def build_program():
    nc = bass.Bass("TRN2", target_bir_lowering=False, detect_race_conditions=RACE_DETECT)
    c = Ctx(nc)

    xw = nc.dram_tensor("xw", [D, NW], F32, kind="ExternalInput").ap()
    xp = nc.dram_tensor("xp", [D, NP], F32, kind="ExternalInput").ap()
    gate = nc.dram_tensor("gate", [128, 1], F32, kind="ExternalInput").ap()
    gater = nc.dram_tensor("gater", [128, 1], F32, kind="ExternalInput").ap()
    Wq = nc.dram_tensor("Wq", [D, D], F32, kind="ExternalInput").ap()
    Wk = nc.dram_tensor("Wk", [D, D], F32, kind="ExternalInput").ap()
    Wv = nc.dram_tensor("Wv", [D, D], F32, kind="ExternalInput").ap()
    Wo = nc.dram_tensor("Wo", [D, D], F32, kind="ExternalInput").ap()
    bq_i = nc.dram_tensor("bq_i", [128, 8], F32, kind="ExternalInput").ap()
    bk_i = nc.dram_tensor("bk_i", [128, 8], F32, kind="ExternalInput").ap()
    bv_i = nc.dram_tensor("bv_i", [1, D], F32, kind="ExternalInput").ap()
    bo_i = nc.dram_tensor("bo_i", [128, 8], F32, kind="ExternalInput").ap()
    w1 = nc.dram_tensor("w1", [FF, D], F32, kind="ExternalInput").ap()
    w2 = nc.dram_tensor("w2", [D, FF], F32, kind="ExternalInput").ap()
    cw_i = nc.dram_tensor("cw_i", [128, FC, 3], F32, kind="ExternalInput").ap()
    cb_i = nc.dram_tensor("cb_i", [128, FC], F32, kind="ExternalInput").ap()
    id_i = nc.dram_tensor("id_i", [128, 128], F32, kind="ExternalInput").ap()
    mask_i = nc.dram_tensor("mask_i", [128, H, 128], F32, kind="ExternalInput").ap()
    sd_i = nc.dram_tensor("sd_i", [128, DC, H], F32, kind="ExternalInput").ap()
    sb_i = nc.dram_tensor("sb_i", [16, DC, 128], F32, kind="ExternalInput").ap()
    o1_i = nc.dram_tensor("o1_i", [1, 128], F32, kind="ExternalInput").ap()
    oc_i = nc.dram_tensor("oc_i", [128, 1], F32, kind="ExternalInput").ap()

    yT = nc.dram_tensor("yT", [D, NO], F32, kind="ExternalOutput").ap()

    sk = "ExternalOutput" if DEBUG_SCRATCH else "Internal"
    att_scr = nc.dram_tensor("att_scr", [D, NW], F32, kind=sk).ap()
    x2_scr = nc.dram_tensor("x2_scr", [D, NW], F32, kind=sk).ap()
    xq1_scr = nc.dram_tensor("xq1_scr", [D, NW], BF16, kind=sk).ap()
    st1_scr = nc.dram_tensor("st1_scr", [1, NW], F32, kind=sk).ap()
    t1t_scr = nc.dram_tensor("t1t_scr", [D, FF], BF16, kind=sk).ap()
    t2t_scr = nc.dram_tensor("t2t_scr", [FF, D], BF16, kind=sk).ap()
    xq2_scr = nc.dram_tensor("xq2_scr", [FF, NO], BF16, kind=sk).ap()
    dbg1 = nc.dram_tensor("dbg1", [128, 8], F32, kind=sk).ap()
    dbg2 = nc.dram_tensor("dbg2", [128, 1024], F32, kind=sk).ap()
    dbg3 = nc.dram_tensor("dbg3", [128, 1024], F32, kind=sk).ap()
    dbg4 = nc.dram_tensor("dbg4", [128, 1024], F32, kind=sk).ap()
    dbg5 = nc.dram_tensor("dbg5", [128, 8, 128], F32, kind=sk).ap()
    dbg6 = nc.dram_tensor("dbg6", [16, 2, 128], F32, kind=sk).ap()
    dbg7 = nc.dram_tensor("dbg7", [128, 16, 128], F32, kind=sk).ap()
    dbg8 = nc.dram_tensor("dbg8", [128, 32, 258], F32, kind=sk).ap()
    dbg9 = nc.dram_tensor("dbg9", [1, 4, 258], F32, kind=sk).ap()
    cs2_scr = nc.dram_tensor("cs2_scr", [1, NO], F32, kind=sk).ap()

    env = {}
    env["att_v"] = att_scr.rearrange("(dc p) t -> p dc t", p=128)
    env["x2_v"] = x2_scr.rearrange("(dc p) t -> p dc t", p=128)
    env["xq1_v"] = xq1_scr.rearrange("(dc p) t -> p dc t", p=128)
    env["xw_v"] = xw.rearrange("(dc p) t -> p dc t", p=128)
    env["xp_v"] = xp.rearrange("(dc p) t -> p dc t", p=128)
    env["t1t_v"] = t1t_scr.rearrange("(dc p) f -> p dc f", p=128)
    env["t2t_v"] = t2t_scr.rearrange("(fc p) d -> p fc d", p=128)
    env["xq2_v"] = xq2_scr.rearrange("(fc p) t -> p fc t", p=128)
    env["y_v"] = yT.rearrange("(dc p) t -> p dc t", p=128)
    env["dbg5"], env["dbg6"], env["dbg7"] = None, None, None
    env["dbg8"], env["dbg9"] = None, None
    env["st1_scr"] = st1_scr
    env["cs2_scr"] = cs2_scr
    env["Wo"] = Wo

    sy, ve, sc, te, gp = nc.sync, nc.vector, nc.scalar, nc.tensor, nc.gpsimd

    import contextlib
    _es = contextlib.ExitStack()
    if True:
        ident = _es.enter_context(nc.sbuf_tensor("ident", [128, 128], F32))
        mask = _es.enter_context(nc.sbuf_tensor("mask", [128, H, 128], F32))
        selden = _es.enter_context(nc.sbuf_tensor("selden", [128, DC, H], F32))
        selbc = _es.enter_context(nc.sbuf_tensor("selbc", [16, DC, 128], F32))
        ones1 = _es.enter_context(nc.sbuf_tensor("ones1", [1, 128], F32))
        onescol = _es.enter_context(nc.sbuf_tensor("onescol", [128, 1], F32))
        gate_s = _es.enter_context(nc.sbuf_tensor("gate_s", [128, 1], F32))
        gater_s = _es.enter_context(nc.sbuf_tensor("gater_s", [128, 1], F32))
        bq_s = _es.enter_context(nc.sbuf_tensor("bq_s", [128, 8], F32))
        bk_s = _es.enter_context(nc.sbuf_tensor("bk_s", [128, 8], F32))
        bo_s = _es.enter_context(nc.sbuf_tensor("bo_s", [128, 8], F32))
        bv_row = _es.enter_context(nc.sbuf_tensor("bv_row", [1, D], F32))
        cw_s = _es.enter_context(nc.sbuf_tensor("cw_s", [128, FC, 3], F32))
        cb_s = _es.enter_context(nc.sbuf_tensor("cb_s", [128, FC], F32))
        rsw1 = _es.enter_context(nc.sbuf_tensor("rsw1", [1, 1], F32))
        rsw2 = _es.enter_context(nc.sbuf_tensor("rsw2", [1, 1], F32))
        S_sb = _es.enter_context(nc.sbuf_tensor("S_sb", [128, DC, DH], F32))
        z_sb = _es.enter_context(nc.sbuf_tensor("z_sb", [128, DC], F32))
        zeros128 = _es.enter_context(nc.sbuf_tensor("zeros128", [128, 128], F32))
        tcol = _es.enter_context(nc.sbuf_tensor("tcol", [128, 4], F32))
        rwa = _es.enter_context(nc.sbuf_tensor("rwa", [1, 544], F32))
        rwb = _es.enter_context(nc.sbuf_tensor("rwb", [1, 544], F32))
        rwc = _es.enter_context(nc.sbuf_tensor("rwc", [1, 544], F32))
        rwd = _es.enter_context(nc.sbuf_tensor("rwd", [1, 544], F32))
        ps = _es.enter_context(nc.psum_tensor("ps", [128, 8, 512], F32))
        _s1 = _es.enter_context(nc.semaphore("si"))
        _s2 = _es.enter_context(nc.semaphore("so"))
        _s3 = _es.enter_context(nc.semaphore("sp"))
        _s4 = _es.enter_context(nc.semaphore("sv"))
        _s5 = _es.enter_context(nc.semaphore("ss"))
        _s6 = _es.enter_context(nc.semaphore("sg"))
        _blk = _es.enter_context(nc.Block())
    if True:
        for nm, h in [("si", _s1), ("so", _s2), ("sp", _s3), ("sv", _s4),
                      ("ss", _s5), ("sg", _s6)]:
            c.add_sem(nm, h)

        def dsem(name):
            if name not in c.sem:
                c.add_sem(name, _es.enter_context(nc.semaphore(name)))
            return name

        def load(dst, src, sem, pre=None):
            s = dsem(sem)
            c.wait(sy, pre)
            return c.inc(sy.dma_start(out=dst, in_=src), s, 16)

        def store(dst, src, sem, pre=None):
            s = dsem(sem)
            c.wait(gp, pre)
            return c.inc(gp.dma_start(out=dst, in_=src), s, 16)

        def full_barrier():
            if not USE_BARRIER:
                return
            for eng in (te, ve, sc, gp, sy):
                for s in list(c.sem):
                    c.wait(eng, c.tok(s))

        env["full_barrier"] = full_barrier
        env["dbg5"], env["dbg6"], env["dbg7"] = dbg5, dbg6, dbg7
        env["dbg8"], env["dbg9"] = dbg8, dbg9
        env.update(dict(ps=ps, ident=ident, mask=mask, selden=selden, selbc=selbc,
                        ones1=ones1, onescol=onescol, gate_s=gate_s, gater_s=gater_s,
                        bq_s=bq_s, bk_s=bk_s, bo_s=bo_s, bv_row=bv_row, cw_s=cw_s,
                        cb_s=cb_s, rsw1=rsw1, rsw2=rsw2, S_sb=S_sb, z_sb=z_sb,
                        zeros128=zeros128, tcol=tcol, rwa=rwa, rwb=rwb, rwc=rwc,
                        rwd=rwd, load=load, store=store))

        for dst, src in [(ident[:], id_i), (mask[:], mask_i), (selden[:], sd_i),
                         (selbc[:], sb_i), (ones1[:], o1_i), (onescol[:], oc_i),
                         (gate_s[:], gate), (gater_s[:], gater), (bq_s[:], bq_i),
                         (bk_s[:], bk_i), (bo_s[:], bo_i), (bv_row[:], bv_i),
                         (cw_s[:], cw_i), (cb_s[:], cb_i)]:
            load(dst, src, "Lconst")
        tok_consts = c.tok("Lconst")
        c.wait(ve, tok_consts)
        c.wait(te, tok_consts)
        c.wait(sc, tok_consts)
        c.wait(gp, tok_consts)
        ve.memset(zeros128[:], 0.0)
        ve.memset(S_sb[:], 0.0)
        c.inc(ve.memset(z_sb[:], 0.0), "sv")

        with (
            nc.sbuf_tensor("wstage", [128, 2, 4096], F32) as wstage,
            nc.sbuf_tensor("qy", [128, 2, 1024], F32) as qy,
            nc.sbuf_tensor("qti", [128, 2, 1024], I32) as qti,
            nc.sbuf_tensor("qa", [128, 2, 1024], F32) as qa,
            nc.sbuf_tensor("qb", [128, 2, 1024], F32) as qb,
            nc.sbuf_tensor("tern", [128, 2, 1024], F32) as tern,
            nc.sbuf_tensor("absac", [128, 40], F32) as absac,
            nc.sbuf_tensor("abscol", [128, 2], F32) as abscol,
            nc.sbuf_tensor("swcol", [128, 2], F32) as swcol,
        ):
            t_ws_slot = [None, None]
            _wlist = [] if SKIP_TERN else [(w1, 32, 1024), (w2, 8, 4096)]
            gi = 0
            for wi, (wt, ntile, tw) in enumerate(_wlist):
                for i in range(ntile):
                    sl = gi % 2
                    gi += 1
                    tl = load(wstage[:, sl, 0:tw], wt[i * 128:(i + 1) * 128, :],
                              f"Lws{sl}", pre=t_ws_slot[sl])
                    col = i if wi == 0 else 32 + i
                    c.wait(ve, tl)
                    t_ws_slot[sl] = c.inc(
                        ve.tensor_reduce(out=absac[:, col:col + 1], in_=wstage[:, sl, 0:tw],
                                         axis=AX.X, op=OP.add, apply_absolute_value=True),
                        "sv")
            t_ws_free = [t_ws_slot[0], t_ws_slot[1]]
            c.wait(ve, t_ws_free)
            ve.tensor_reduce(out=abscol[:, 0:1], in_=absac[:, 0:32], axis=AX.X, op=OP.add)
            t_red = c.inc(ve.tensor_reduce(out=abscol[:, 1:2], in_=absac[:, 32:40],
                                           axis=AX.X, op=OP.add), "sv")
            if SKIP_TERN:
                c.wait(ve, c.tok("Lconst"))
                ve.memset(rsw1[:], 1.0)
                c.inc(ve.memset(rsw2[:], 1.0), "sv")
            c.wait(te, t_red)
            t_mm = c.inc(te.matmul(ps[0:1, 0, 0:2], onescol[:], abscol[:],
                                   start=True, stop=True), "sp")
            c.wait(ve, t_mm)
            t_f = c.inc(ve.tensor_scalar(out=rwa[0:1, 0:2], in0=ps[0:1, 0, 0:2],
                                         scalar1=1.0 / (FF * D), scalar2=1e-5,
                                         op0=OP.mult, op1=OP.max), "sv")
            c.wait(ve, t_f)
            t_f = c.inc(ve.reciprocal(out=rwb[0:1, 0:2], in_=rwa[0:1, 0:2]), "sv")
            c.wait(ve, t_f)
            t_f = c.inc(ve.reciprocal(out=rwc[0:1, 0:2], in_=rwb[0:1, 0:2]), "sv")
            c.wait(ve, t_f)
            ve.tensor_copy(out=rsw1[:], in_=rwc[0:1, 0:1])
            t_rw = c.inc(ve.tensor_copy(out=rsw2[:], in_=rwc[0:1, 1:2]), "sv")
            c.wait(te, t_rw)
            t_mm = c.inc(te.matmul(ps[:, 0, 0:2], ones1[:], rwb[0:1, 0:2],
                                   start=True, stop=True), "sp")
            c.wait(ve, t_mm)
            t_sw = c.inc(ve.tensor_copy(out=swcol[:], in_=ps[:, 0, 0:2]), "sv")
            if DEBUG_SCRATCH:
                store(dbg1[:, 0:2], swcol[:], "Sdbg", pre=t_sw)
                store(dbg1[:, 2:4], abscol[:], "Sdbg", pre=t_red)
                store(dbg1[0:1, 4:6], rwa[0:1, 0:2], "Sdbg", pre=t_rw)
                store(dbg1[0:1, 6:8], rwb[0:1, 0:2], "Sdbg", pre=t_rw)

            t_spill = None
            t_ws_slot2 = [t_ws_free, t_ws_free]
            t_y_slot = [None, None]
            t_ti_slot = [None, None]
            t_tern_slot = [None, None]
            gi = 0
            for wi, (wt, ntile, tw) in enumerate(_wlist):
                tsb_shape = [128, DC, FF] if wi == 0 else [128, FC, D]
                with nc.sbuf_tensor(f"tts{wi}", tsb_shape, BF16) as tts:
                    c.wait(sc, t_spill)
                    tr_evac = [None] * 4
                    for i in range(ntile):
                        wl = gi % 2
                        gi += 1
                        tl = load(wstage[:, wl, 0:tw], wt[i * 128:(i + 1) * 128, :],
                                  f"Lws{wl}", pre=t_ws_slot2[wl])
                        for s4 in range(tw // 1024):
                            sl = s4 % 2
                            wsl = wstage[:, wl, s4 * 1024:(s4 + 1) * 1024]
                            c.wait(ve, [tl, t_sw, t_ti_slot[sl], t_tern_slot[sl]])
                            t_y = c.inc(ve.tensor_single_scalar(
                                out=qy[:, sl, :], in_=wsl, scalar=swcol[:, wi:wi + 1],
                                op=OP.mult), "sv")
                            t_y_slot[sl] = t_y
                            c.wait(sc, [t_y, t_tern_slot[sl]])
                            t_ti = c.inc(sc.activation(out=qti[:, sl, :], in_=qy[:, sl, :],
                                                       func=AF.Identity), "ss")
                            t_ti_slot[sl] = t_ti
                            c.wait(ve, t_ti)
                            t_a = c.inc(ve.scalar_tensor_tensor(
                                out=qa[:, sl, :], in0=qti[:, sl, :], scalar=0.5,
                                in1=qy[:, sl, :], op0=OP.add, op1=OP.is_le), "sv")
                            ve.scalar_tensor_tensor(
                                out=qb[:, sl, :], in0=qti[:, sl, :], scalar=0.5,
                                in1=qy[:, sl, :], op0=OP.subtract, op1=OP.is_gt)
                            ve.scalar_tensor_tensor(out=qa[:, sl, :], in0=qb[:, sl, :],
                                                    scalar=-1.0, in1=qa[:, sl, :],
                                                    op0=OP.mult, op1=OP.add)
                            ve.tensor_tensor(out=qa[:, sl, :], in0=qa[:, sl, :],
                                             in1=qti[:, sl, :], op=OP.add)
                            t_tern = c.inc(ve.tensor_scalar(out=tern[:, sl, :],
                                                            in0=qa[:, sl, :],
                                                            scalar1=-1.0, scalar2=1.0,
                                                            op0=OP.max, op1=OP.min), "sv")
                            if DEBUG_SCRATCH and wi == 0 and i == 0:
                                store(dbg2[:], tern[:, sl, :], "Sdbg", pre=t_tern)
                            if DEBUG_SCRATCH and wi == 0 and i == 31:
                                store(dbg3[:], tern[:, sl, :], "Sdbg", pre=t_tern)
                                store(dbg4[:], qy[:, sl, :], "Sdbg", pre=t_y)
                            for dcb in range(8):
                                qb_ = dcb % 4
                                c.wait(te, [t_tern, tr_evac[qb_]])
                                t_tr = c.inc(te.transpose(
                                    ps[:, 4 + qb_, 0:128],
                                    tern[:, sl, dcb * 128:(dcb + 1) * 128], ident[:]), "sp")
                                c.wait(sc, t_tr)
                                if wi == 0:
                                    dst = tts[:, dcb, i * 128:(i + 1) * 128]
                                else:
                                    dst = tts[:, s4 * 8 + dcb, i * 128:(i + 1) * 128]
                                tr_evac[qb_] = c.inc(sc.copy(out=dst,
                                                             in_=ps[:, 4 + qb_, 0:128]), "ss")
                            t_tern_slot[sl] = c.tok("sp")
                        t_ws_slot2[wl] = c.tok("sv")
                    t_spill = store(env["t1t_v"][:, :, :] if wi == 0 else env["t2t_v"][:, :, :],
                                    tts[:], f"Stts{wi}", pre=[c.tok("ss")])

        full_barrier()
        with (
            nc.sbuf_tensor("WqTh", [128, DC, D], BF16) as WqTh,
            nc.sbuf_tensor("WqTl", [128, DC, D], BF16) as WqTl,
            nc.sbuf_tensor("WkTh", [128, DC, D], BF16) as WkTh,
            nc.sbuf_tensor("WkTl", [128, DC, D], BF16) as WkTl,
            nc.sbuf_tensor("WvTh", [128, DC, D], BF16) as WvTh,
            nc.sbuf_tensor("WvTl", [128, DC, D], BF16) as WvTl,
        ):
            with nc.sbuf_tensor("wstg2", [128, 1024], F32) as wstg2:
                t_stg_free = None
                tr_evac = [None] * 4
                for wt, wdh, wdl in ([] if SKIP_WT else
                                     [(Wq, WqTh, WqTl), (Wk, WkTh, WkTl),
                                      (Wv, WvTh, WvTl)]):
                    for m in range(8):
                        tl = load(wstg2[:], wt[m * 128:(m + 1) * 128, :], "Lws2", pre=t_stg_free)
                        for kc in range(8):
                            qb_ = kc % 4
                            c.wait(te, [tl, tr_evac[qb_]])
                            t_tr = c.inc(te.transpose(ps[:, 4 + qb_, 0:128],
                                                      wstg2[:, kc * 128:(kc + 1) * 128],
                                                      ident[:]), "sp")
                            c.wait(sc, t_tr)
                            t_hi = c.inc(sc.copy(out=wdh[:, kc, m * 128:(m + 1) * 128],
                                                 in_=ps[:, 4 + qb_, 0:128]), "ss")
                            c.wait(ve, t_hi)
                            tr_evac[qb_] = c.inc(ve.tensor_tensor(
                                out=wdl[:, kc, m * 128:(m + 1) * 128],
                                in0=ps[:, 4 + qb_, 0:128],
                                in1=wdh[:, kc, m * 128:(m + 1) * 128],
                                op=OP.subtract), "sv")
                        t_stg_free = c.tok("sp")
            env.update(dict(WqTh=WqTh, WqTl=WqTl, WkTh=WkTh, WkTl=WkTl,
                            WvTh=WvTh, WvTl=WvTl))
            env["tok_wT"] = c.tok("sv")

            if N_PHASES >= 2 and 2 not in SKIP_PHASES:
                emit_a1(nc, c, env)

        if N_PHASES >= 3 and 3 not in SKIP_PHASES:
            emit_a2(nc, c, env)
        if N_PHASES >= 4 and 4 not in SKIP_PHASES:
            emit_f1(nc, c, env)
        if N_PHASES >= 5 and 5 not in SKIP_PHASES:
            emit_f2(nc, c, env)

        for s in list(c.sem):
            c.wait(sy, c.tok(s))

    _es.close()
    return nc


def emit_a1(nc, c, env):
    sy, ve, sc, te, gp = nc.sync, nc.vector, nc.scalar, nc.tensor, nc.gpsimd
    ps = env["ps"]
    WqTh, WqTl = env["WqTh"], env["WqTl"]
    WkTh, WkTl = env["WkTh"], env["WkTl"]
    WvTh, WvTl = env["WvTh"], env["WvTl"]
    xw_v, xp_v, att_v = env["xw_v"], env["xp_v"], env["att_v"]
    bq_s, bk_s, bv_row = env["bq_s"], env["bk_s"], env["bv_row"]
    ones1, ident, mask = env["ones1"], env["ident"], env["mask"]
    selden, selbc = env["selden"], env["selbc"]
    gate_s, zeros128 = env["gate_s"], env["zeros128"]
    S_sb, z_sb = env["S_sb"], env["z_sb"]
    load, store = env["load"], env["store"]
    env["full_barrier"]()

    with (
        nc.sbuf_tensor("xin", [128, DC, 256], F32) as xin,
        nc.sbuf_tensor("qf", [128, DC, 256], F32) as qf,
        nc.sbuf_tensor("kf", [128, DC, 256], F32) as kf,
        nc.sbuf_tensor("vtm", [128, 2, D], F32) as vtm,
        nc.sbuf_tensor("ktm", [128, DC, 128], F32) as ktm,
        nc.sbuf_tensor("kcum", [128, DC, 128], F32) as kcum,
        nc.sbuf_tensor("pbuf", [128, DC, 128], F32) as pbuf,
        nc.sbuf_tensor("ssb", [128, H, 128], F32) as ssb,
        nc.sbuf_tensor("denr", [16, 128], F32) as denr,
        nc.sbuf_tensor("dbc_sb", [128, DC, 128], F32) as dbc_sb,
        nc.sbuf_tensor("dinv", [16, 128], F32) as dinv,
        nc.sbuf_tensor("outst", [128, 2, DC, 128], F32) as outst,
        nc.sbuf_tensor("ph1", [128, 2, 256], F32) as ph1,
        nc.sbuf_tensor("ph2", [128, 2, 256], F32) as ph2,
        nc.sbuf_tensor("ph3", [128, 2, 256], F32) as ph3,
        nc.sbuf_tensor("ztmp", [128, DC], F32) as ztmp,
        nc.sbuf_tensor("xhh", [128, DC, 256], BF16) as xh,
        nc.sbuf_tensor("xll", [128, DC, 256], BF16) as xl,
    ):
        st = {
            "xin_free": None,
            "xh_free": None,
            "b07_free": None,
            "b45_free": None,
            "b6_free": None,
            "S_started": [False, False],
            "phi_free": [None, None],
            "outst_free": [None, None],
            "vtm_free": [None, None],
            "S_sb_ok": None,
            "ssb_free": None,
            "kcum_free": None,
            "dbc_free": None,
            "pbuf_free": None,
            "z_ok": None,
            "sc03_free": None,
        }

        def cast_hl(xbuf, tw, t_x):
            c.wait(sc, [t_x, st["xh_free"]])
            t_xh = c.inc(sc.copy(out=xh[:, :, 0:tw], in_=xbuf[:, :, 0:tw]), "ss")
            c.wait(ve, t_xh)
            return c.inc(ve.tensor_tensor(out=xl[:, :, 0:tw], in0=xbuf[:, :, 0:tw],
                                          in1=xh[:, :, 0:tw], op=OP.subtract), "sv")

        def proj_qk(wTh, wTl, bias, dstf, tw, t_x):
            t_e1 = None
            for m in range(8):
                reg = ps[:, m, 0:tw]
                for kc in range(8):
                    if kc == 0 and m == 0:
                        c.wait(te, [t_x, st["b07_free"], st["sc03_free"],
                                    st["b45_free"], st["b6_free"], st["S_sb_ok"]])
                    te.matmul(reg, wTh[:, kc, m * 128:(m + 1) * 128],
                              xh[:, kc, 0:tw], start=(kc == 0), stop=False)
                    te.matmul(reg, wTh[:, kc, m * 128:(m + 1) * 128],
                              xl[:, kc, 0:tw], start=False, stop=False)
                    t_mm = te.matmul(reg, wTl[:, kc, m * 128:(m + 1) * 128],
                                     xh[:, kc, 0:tw], start=False, stop=(kc == 7))
                t_mm = c.inc(t_mm, "sp")
                s = m % 2
                c.wait(sc, [t_mm, st["phi_free"][s]])
                t_e1 = c.inc(sc.activation(out=ph1[:, s, 0:tw], in_=reg, func=AF.Identity,
                                           bias=bias[:, m:m + 1]), "ss")
                c.wait(gp, t_e1)
                t_mn = c.inc(gp.tensor_single_scalar(out=ph2[:, s, 0:tw],
                                                     in_=ph1[:, s, 0:tw],
                                                     scalar=0.0, op=OP.min), "sg")
                c.wait(sc, t_mn)
                t_ex = c.inc(sc.activation(out=ph3[:, s, 0:tw], in_=ph2[:, s, 0:tw],
                                           func=AF.Exp), "ss")
                c.wait(ve, t_ex)
                st["phi_free"][s] = c.inc(
                    ve.scalar_tensor_tensor(out=dstf[:, m, 0:tw], in0=ph1[:, s, 0:tw],
                                            scalar=0.0, in1=ph3[:, s, 0:tw],
                                            op0=OP.max, op1=OP.add), "sv")
            t_all_evac = c.tok("ss")
            st["b07_free"] = t_all_evac
            st["sc03_free"] = t_all_evac
            st["b45_free"] = t_all_evac
            st["b6_free"] = t_all_evac
            return c.tok("sv")

        def proj_v(cc, t_x):
            for half in range(2):
                reg = ps[:, 4 + half, 0:512]
                for kc in range(8):
                    if kc == 0 and half == 0:
                        c.wait(te, [t_x, st["b45_free"]])
                    te.matmul(reg, xh[:, kc, cc * 128:(cc + 1) * 128],
                              WvTh[:, kc, half * 512:(half + 1) * 512],
                              start=(kc == 0), stop=False)
                    te.matmul(reg, xl[:, kc, cc * 128:(cc + 1) * 128],
                              WvTh[:, kc, half * 512:(half + 1) * 512],
                              start=False, stop=False)
                    te.matmul(reg, xh[:, kc, cc * 128:(cc + 1) * 128],
                              WvTl[:, kc, half * 512:(half + 1) * 512],
                              start=False, stop=False)
                t_mm = c.inc(te.matmul(reg, ones1[0:1, 0:128],
                                       bv_row[0:1, half * 512:(half + 1) * 512],
                                       start=False, stop=True), "sp")
            c.wait(sc, [t_mm, st["vtm_free"][cc]])
            t_ev = c.inc(sc.copy(out=vtm[:, cc, :], in_=ps[:, 4:6, 0:512]), "ss")
            st["b45_free"] = t_ev
            return t_ev

        def ktm_make(cc, t_k):
            tr_ev = [None, None]
            for dc in range(8):
                bnk = 4 + (dc % 2)
                c.wait(te, [t_k, st["b45_free"]] if dc == 0 else tr_ev[dc % 2])
                t_tr = c.inc(te.transpose(ps[:, bnk, 0:128],
                                          kf[:, dc, cc * 128:(cc + 1) * 128], ident[:]), "sp")
                c.wait(sc, t_tr)
                tr_ev[dc % 2] = c.inc(sc.copy(out=ktm[:, dc, :],
                                              in_=ps[:, bnk, 0:128]), "ss")
            st["b45_free"] = c.tok("ss")
            return c.tok("ss")

        def s_update(cc, t_ktm, t_v, t_num):
            for hi, h in enumerate(list(range(0, H, 2)) + list(range(1, H, 2))):
                p0 = (h % 2) * 64
                dc = h // 2
                if hi == 0:
                    c.wait(te, [t_ktm, t_v, st["S_sb_ok"]])
                t_mm = te.matmul(ps[p0:p0 + 64, 7, dc * 64:(dc + 1) * 64],
                                 ktm[:, dc, (h % 2) * 64:(h % 2) * 64 + 64],
                                 vtm[:, cc, h * 64:(h + 1) * 64],
                                 start=(h <= 1), stop=(hi % 8 == 7),
                                 tile_position=(0, p0))
            t_su = c.inc(t_mm, "sp")
            st["vtm_free"][cc] = t_su
            sflat = bass.AP(S_sb, 0, [[512, 128], [1, 512]])
            c.wait(ve, [t_su, t_num])
            st["S_sb_ok"] = c.inc(ve.tensor_tensor(out=sflat, in0=sflat,
                                                   in1=ps[:, 7, 0:512], op=OP.add), "sv")
            return st["S_sb_ok"]

        def scan_chunk(g, cc, is_last_main, t_q, t_k, t_ktm, t_v):
            for hi, h in enumerate(list(range(0, H, 2)) + list(range(1, H, 2))):
                p0 = (h % 2) * 64
                dc = h // 2
                if hi == 0:
                    c.wait(te, [t_q, t_k, st["sc03_free"], st["b07_free"]])
                t_mm = te.matmul(ps[:, h // 4, (h % 4) * 128:(h % 4 + 1) * 128],
                                 kf[p0:p0 + 64, dc, cc * 128:(cc + 1) * 128],
                                 qf[p0:p0 + 64, dc, cc * 128:(cc + 1) * 128],
                                 start=(h % 4 == 0), stop=(h % 4 == 3))
            t_sc = c.inc(t_mm, "sp")
            c.wait(ve, [t_k, st["kcum_free"], st["z_ok"]])
            t_scan = None
            for dc in range(8):
                t_scan = ve.tensor_tensor_scan(out=kcum[:, dc, :],
                                               data0=kf[:, dc, cc * 128:(cc + 1) * 128],
                                               data1=zeros128[:], initial=z_sb[:, dc:dc + 1],
                                               op0=OP.add, op1=OP.add)
            t_scan = c.inc(t_scan, "sv")
            c.wait(ve, [t_scan, t_q, st["pbuf_free"]])
            t_p0 = c.inc(ve.tensor_tensor(out=pbuf[:], in0=qf[:, :, cc * 128:(cc + 1) * 128],
                                          in1=kcum[:], op=OP.mult), "sv")
            c.wait(ve, t_p0)
            t_z = c.inc(ve.tensor_copy(out=bass.AP(z_sb, 0, [[8, 128], [1, 8], [1, 1]]),
                                       in_=kcum[:, :, 127:128]), "sv")
            st["z_ok"] = t_z
            t_p = t_z
            for dc in range(8):
                if dc == 0:
                    c.wait(te, [t_p, st["b6_free"]])
                t_mm = te.matmul(ps[0:16, 6, 0:128], selden[:, dc, :], pbuf[:, dc, :],
                                 start=(dc == 0), stop=(dc == 7))
            t_den = c.inc(t_mm, "sp")
            st["kcum_free"] = t_scan
            st["pbuf_free"] = t_den
            if DEBUG_SCRATCH and g == 0:
                st["pbuf_free"] = store(env["dbg5"], pbuf[:], "Sdbg", pre=[t_p, t_den])
            c.wait(ve, t_den)
            t_clamp = c.inc(ve.tensor_single_scalar(out=denr[:], in_=ps[0:16, 6, 0:128],
                                                    scalar=1e-30, op=OP.max), "sv")
            c.wait(ve, t_clamp)
            t_r = c.inc(ve.reciprocal(out=dinv[:], in_=denr[:]), "sv")
            c.wait(ve, t_r)
            t_dinv = c.inc(ve.tensor_copy(out=denr[:], in_=dinv[:]), "sv")
            st["b6_free"] = t_dinv
            if DEBUG_SCRATCH and g == 0:
                store(env["dbg6"][:, 0, :], denr[:], "Sdbg", pre=t_dinv)
                store(env["dbg6"][:, 1, :], dinv[:], "Sdbg", pre=t_dinv)
            for dc in range(8):
                if dc == 0:
                    c.wait(te, [t_dinv, st["b45_free"]])
                t_mm = te.matmul(ps[:, 4 + dc // 4, (dc % 4) * 128:(dc % 4 + 1) * 128],
                                 selbc[:, dc, :], dinv[:],
                                 start=(dc % 4 == 0), stop=(dc % 4 == 3))
            t_bc = c.inc(t_mm, "sp")
            c.wait(sc, [t_bc, st["dbc_free"]])
            t_dbc = c.inc(sc.copy(out=dbc_sb[:],
                                  in_=bass.AP(ps, 4 * 512, [[4096, 128], [128, 8], [1, 128]])), "ss")
            st["b45_free"] = t_dbc
            c.wait(ve, [t_sc, st["ssb_free"]])
            t_ssb = c.inc(ve.tensor_tensor(
                out=ssb[:], in0=bass.AP(ps, 0, [[4096, 128], [128, 16], [1, 128]]),
                in1=mask[:], op=OP.mult), "sv")
            st["sc03_free"] = t_ssb
            if DEBUG_SCRATCH and g == 0:
                st["ssb_free"] = store(env["dbg7"], ssb[:], "Sdbg", pre=t_ssb)
            for hi, h in enumerate(list(range(0, H, 2)) + list(range(1, H, 2))):
                p0 = (h % 2) * 64
                dc = h // 2
                if hi == 0:
                    c.wait(te, [t_ssb, st["S_sb_ok"], st["b45_free"]])
                te.matmul(ps[p0:p0 + 64, 4 + dc // 4, (dc % 4) * 128:(dc % 4 + 1) * 128],
                          vtm[:, cc, h * 64:(h + 1) * 64], ssb[:, h, :],
                          start=(h % 8 <= 1), stop=False, tile_position=(0, p0))
                t_mm = te.matmul(ps[p0:p0 + 64, 4 + dc // 4, (dc % 4) * 128:(dc % 4 + 1) * 128],
                                 S_sb[p0:p0 + 64, dc, :],
                                 qf[p0:p0 + 64, dc, cc * 128:(cc + 1) * 128],
                                 start=False, stop=(h % 8 >= 6), tile_position=(p0, p0))
            t_num = c.inc(t_mm, "sp")
            st["ssb_free"] = t_num
            sl = g % 2
            c.wait(ve, [t_num, t_dbc, st["outst_free"][sl]])
            t_oev = c.inc(ve.tensor_tensor(
                out=outst[:, sl, :, :],
                in0=bass.AP(ps, 4 * 512, [[4096, 128], [128, 8], [1, 128]]),
                in1=dbc_sb[:], op=OP.mult), "sv")
            st["b45_free"] = t_oev
            st["dbc_free"] = t_oev
            st["outst_free"][sl] = store(att_v[:, :, g * 128:(g + 1) * 128],
                                         outst[:, sl, :, :], f"Satt{sl}", pre=t_oev)
            s_update(cc, t_ktm, t_v, t_num)

        pf_tiles = ([(i * 256, 256) for i in range(15)] + [(3840, 128)])[:A1_PF_TILES]
        t_su = None
        for ti_, (t0, tw) in enumerate(pf_tiles):
            b = ti_ % 2
            tl = load(xin[:, :, 0:tw], xp_v[:, :, t0:t0 + tw], f"Lxin{b}", pre=st["xin_free"])
            t_hl = cast_hl(xin, tw, tl)
            t_k = proj_qk(WkTh, WkTl, bk_s, kf, tw, t_hl)
            if A1_STAGES & 2:
                c.wait(ve, t_k)
                t_k = c.inc(ve.tensor_single_scalar(out=kf[:, :, 0:tw], in_=kf[:, :, 0:tw],
                                                    scalar=gate_s[:, 0:1], op=OP.mult), "sv")
                c.wait(ve, t_k)
                t_zr = c.inc(ve.tensor_reduce(out=ztmp[:], in_=kf[:, :, 0:tw],
                                              axis=AX.X, op=OP.add), "sv")
                c.wait(ve, t_zr)
                c.inc(ve.tensor_tensor(out=z_sb[:], in0=z_sb[:], in1=ztmp[:], op=OP.add), "sv")
            t_vs = [proj_v(cc, t_hl) if A1_STAGES & 4 else None
                    for cc in range(tw // 128)]
            for cc in range(tw // 128):
                t_ktm = ktm_make(cc, t_k) if A1_STAGES & 8 else None
                if A1_STAGES & 16:
                    t_su = s_update(cc, t_ktm, t_vs[cc], None)
            st["xin_free"] = t_hl
            st["xh_free"] = c.tok("sp")

        for ti_ in range(A1_MAIN_TILES):
            b = ti_ % 2
            t0 = ti_ * 256
            tl = load(xin[:, :, 0:256], xw_v[:, :, t0:t0 + 256], f"Lxin{b}", pre=st["xin_free"])
            if A1_STAGES & 128:
                c.wait(ve, tl)
                ve.tensor_single_scalar(out=qf[:], in_=xin[:, :, 0:256], scalar=1.0, op=OP.max)
                t_q = t_k = c.inc(ve.tensor_single_scalar(out=kf[:], in_=xin[:, :, 0:256],
                                                          scalar=1.0, op=OP.max), "sv")
            else:
                t_hl = cast_hl(xin, 256, tl)
                t_q = proj_qk(WqTh, WqTl, bq_s, qf, 256, t_hl)
                t_k = proj_qk(WkTh, WkTl, bk_s, kf, 256, t_hl)
            if ti_ == 0:
                c.wait(ve, [t_q, t_k])
                ve.tensor_single_scalar(out=qf[:, :, 0:128], in_=qf[:, :, 0:128],
                                        scalar=gate_s[:, 0:1], op=OP.mult)
                t_q = t_k = c.inc(
                    ve.tensor_single_scalar(out=kf[:, :, 0:128], in_=kf[:, :, 0:128],
                                            scalar=gate_s[:, 0:1], op=OP.mult), "sv")
            ncc = 1 if A1_SCAN & 65536 else 2
            t_vs = [proj_v(cc, t_hl) for cc in range(ncc)]
            for cc in range(ncc):
                g = ti_ * 2 + cc
                t_ktm = ktm_make(cc, t_k)
                scan_chunk(g, cc, g == NWCH - 1, t_q, t_k, t_ktm, t_vs[cc])
            st["xin_free"] = t_hl
            st["xh_free"] = c.tok("sp")


def row_absmax(nc, c, env, srcap, tw, t_src, bank_t, bank_r, pre_te):
    sy, ve, sc, te, gp = nc.sync, nc.vector, nc.scalar, nc.tensor, nc.gpsimd
    ps, ident, tcol = env["ps"], env["ident"], env["tcol"]
    rwa = env["rwa"]
    nblk = tw // 128
    t_red = None
    for j in range(nblk):
        c.wait(te, [t_src, pre_te] if j == 0 else t_red)
        t_tr = c.inc(te.transpose(ps[:, bank_t, 0:128], srcap[:, j * 128:(j + 1) * 128],
                                  ident[:]), "sp")
        c.wait(ve, t_tr)
        t_red = c.inc(ve.tensor_reduce(out=tcol[:, j:j + 1], in_=ps[:, bank_t, 0:128],
                                       axis=AX.X, op=OP.max), "sv")
    for j in range(nblk):
        c.wait(te, t_red if j == 0 else None)
        t_tr2 = c.inc(te.transpose(ps[0:1, bank_r, j * 128:(j + 1) * 128],
                                   tcol[:, j:j + 1], ident[:]), "sp")
    c.wait(ve, t_tr2)
    t_row = c.inc(ve.tensor_single_scalar(out=rwa[0:1, 0:tw], in_=ps[0:1, bank_r, 0:tw],
                                          scalar=1e-5, op=OP.max), "sv")
    return t_row


def emit_a2(nc, c, env):
    sy, ve, sc, te, gp = nc.sync, nc.vector, nc.scalar, nc.tensor, nc.gpsimd
    ps = env["ps"]
    Wo, att_v, xw_v = env["Wo"], env["att_v"], env["xw_v"]
    x2_v, xq1_v, st1_scr = env["x2_v"], env["xq1_v"], env["st1_scr"]
    bo_s, ones1, onescol, ident = env["bo_s"], env["ones1"], env["onescol"], env["ident"]
    gate_s, gater_s = env["gate_s"], env["gater_s"]
    rwa, rwb, rwc, rwd = env["rwa"], env["rwb"], env["rwc"], env["rwd"]
    load, store = env["load"], env["store"]
    env["full_barrier"]()

    with (
        nc.sbuf_tensor("WoTh", [128, DC, D], BF16) as WoTh,
        nc.sbuf_tensor("WoTl", [128, DC, D], BF16) as WoTl,
    ):
        with nc.sbuf_tensor("wstg3", [128, 1024], F32) as wstg3:
            t_stg_free = None
            tr_evac = [None] * 4
            for m in range(8):
                tl = load(wstg3[:], Wo[m * 128:(m + 1) * 128, :], "Lws3", pre=t_stg_free)
                for kc in range(8):
                    qb_ = kc % 4
                    c.wait(te, [tl, tr_evac[qb_]])
                    t_tr = c.inc(te.transpose(ps[:, 4 + qb_, 0:128],
                                              wstg3[:, kc * 128:(kc + 1) * 128], ident[:]), "sp")
                    c.wait(sc, t_tr)
                    t_hi = c.inc(sc.copy(out=WoTh[:, kc, m * 128:(m + 1) * 128],
                                         in_=ps[:, 4 + qb_, 0:128]), "ss")
                    c.wait(ve, t_hi)
                    tr_evac[qb_] = c.inc(ve.tensor_tensor(
                        out=WoTl[:, kc, m * 128:(m + 1) * 128],
                        in0=ps[:, 4 + qb_, 0:128],
                        in1=WoTh[:, kc, m * 128:(m + 1) * 128],
                        op=OP.subtract), "sv")
                t_stg_free = c.tok("sp")
        t_woT = c.tok("sv")

        import contextlib
        _es2 = contextlib.ExitStack()
        ath = _es2.enter_context(nc.sbuf_tensor("ath", [128, DC, 512], BF16))
        atl = _es2.enter_context(nc.sbuf_tensor("atl", [128, DC, 512], BF16))
        attl = _es2.enter_context(nc.sbuf_tensor("attl", [128, DC, 512], F32))
        xl = _es2.enter_context(nc.sbuf_tensor("xl", [128, DC, 512], F32))
        x2b = _es2.enter_context(nc.sbuf_tensor("x2b", [128, DC, 512], F32))
        sqb = _es2.enter_context(nc.sbuf_tensor("sqb", [128, 2, 512], F32))
        yb = _es2.enter_context(nc.sbuf_tensor("yb", [128, DC, 512], F32))
        tib = _es2.enter_context(nc.sbuf_tensor("tib", [128, DC, 512], I32))
        ab = _es2.enter_context(nc.sbuf_tensor("ab", [128, DC, 512], F32))
        bb = _es2.enter_context(nc.sbuf_tensor("bb", [128, DC, 512], F32))
        xq1b = _es2.enter_context(nc.sbuf_tensor("xq1b", [128, DC, 512], BF16))
        qt1 = _es2.enter_context(nc.sbuf_tensor("qt1", [128, 1, 512], F32))
        qt2 = _es2.enter_context(nc.sbuf_tensor("qt2", [128, 1, 512], F32))

        tiles = [(i * 512, 512) for i in range(8)] + [(4096, 256)]
        t_ps_free = t_woT
        t_ath_free = None
        t_bufs_free = None
        t_round_prev = None
        t_comb_prev = None
        t_st1w_prev = None
        for ti_, (t0, tw) in enumerate(tiles):
            tl1 = load(attl[:, :, 0:tw], att_v[:, :, t0:t0 + tw], "Lattl", pre=t_bufs_free)
            tl2 = load(xl[:, :, 0:tw], xw_v[:, :, t0:t0 + tw], "Lxl", pre=t_bufs_free)
            c.wait(sc, [tl1, t_ath_free])
            t_ah = c.inc(sc.copy(out=ath[:, :, 0:tw], in_=attl[:, :, 0:tw]), "ss")
            c.wait(ve, t_ah)
            t_al = c.inc(ve.tensor_tensor(out=atl[:, :, 0:tw], in0=attl[:, :, 0:tw],
                                          in1=ath[:, :, 0:tw], op=OP.subtract), "sv")
            for m in range(8):
                for kc in range(8):
                    if m == 0 and kc == 0:
                        c.wait(te, [t_al, t_ps_free])
                    te.matmul(ps[:, m, 0:tw], WoTh[:, kc, m * 128:(m + 1) * 128],
                              ath[:, kc, 0:tw], start=(kc == 0), stop=False)
                    te.matmul(ps[:, m, 0:tw], WoTh[:, kc, m * 128:(m + 1) * 128],
                              atl[:, kc, 0:tw], start=False, stop=False)
                    t_mm = te.matmul(ps[:, m, 0:tw], WoTl[:, kc, m * 128:(m + 1) * 128],
                                     ath[:, kc, 0:tw], start=False, stop=(kc == 7))
            t_mm = c.inc(t_mm, "sp")
            t_ath_free = c.tok("sp")
            psw = bass.AP(ps, 0, [[4096, 128], [512, 8], [1, tw]])
            c.wait(ve, [t_mm, tl2, t_bufs_free])
            ve.tensor_tensor(out=x2b[:, :, 0:tw], in0=psw, in1=xl[:, :, 0:tw], op=OP.add)
            t_r1 = c.inc(ve.tensor_tensor(
                out=x2b[:, :, 0:tw], in0=x2b[:, :, 0:tw],
                in1=bass.AP(bo_s, 0, [[8, 128], [1, 8], [0, tw]]), op=OP.add), "sv")
            t_sq = [None, None]
            for dc in range(8):
                sl = dc % 2
                c.wait(sc, [t_r1, t_sq[sl]])
                t_s = c.inc(sc.activation(out=sqb[:, sl, 0:tw], in_=x2b[:, dc, 0:tw],
                                          func=AF.Square), "ss")
                c.wait(te, [t_r1, t_s])
                te.matmul(ps[0:1, 0, 0:tw], onescol[:], x2b[:, dc, 0:tw],
                          start=(dc == 0), stop=(dc == 7))
                t_sq[sl] = c.inc(te.matmul(ps[0:1, 1, 0:tw], onescol[:], sqb[:, sl, 0:tw],
                                           start=(dc == 0), stop=(dc == 7)), "sp")
            c.wait(ve, t_sq[1])
            ve.tensor_single_scalar(out=rwa[0:1, 0:tw], in_=ps[0:1, 0, 0:tw],
                                    scalar=1.0 / D, op=OP.mult)
            ve.tensor_single_scalar(out=rwb[0:1, 0:tw], in_=ps[0:1, 1, 0:tw],
                                    scalar=1.0 / D, op=OP.mult)
            ve.tensor_tensor(out=rwc[0:1, 0:tw], in0=rwa[0:1, 0:tw], in1=rwa[0:1, 0:tw],
                             op=OP.mult)
            t_var = c.inc(ve.tensor_tensor(out=rwb[0:1, 0:tw], in0=rwb[0:1, 0:tw],
                                           in1=rwc[0:1, 0:tw], op=OP.subtract), "sv")
            c.wait(sc, t_var)
            t_std = c.inc(sc.activation(out=rwb[0:1, 0:tw], in_=rwb[0:1, 0:tw],
                                        func=AF.Sqrt, bias=np.float32(1e-5)), "ss")
            c.wait(ve, t_std)
            t_rstd = c.inc(ve.reciprocal(out=rwc[0:1, 0:tw], in_=rwb[0:1, 0:tw]), "sv")
            c.wait(te, [t_rstd, t_r1])
            te.matmul(ps[:, 2, 0:tw], ones1[:], rwa[0:1, 0:tw], start=True, stop=True)
            t_bc = c.inc(te.matmul(ps[:, 3, 0:tw], ones1[:], rwc[0:1, 0:tw],
                                   start=True, stop=True), "sp")
            c.wait(ve, t_bc)
            ve.tensor_tensor(out=x2b[:, :, 0:tw], in0=x2b[:, :, 0:tw],
                             in1=bass.AP(ps, 2 * 512, [[4096, 128], [0, 8], [1, tw]]),
                             op=OP.subtract)
            t_x2 = c.inc(ve.tensor_tensor(out=x2b[:, :, 0:tw], in0=x2b[:, :, 0:tw],
                                          in1=bass.AP(ps, 3 * 512, [[4096, 128], [0, 8], [1, tw]]),
                                          op=OP.mult), "sv")
            t_st_x2 = store(x2_v[:, :, t0:t0 + tw], x2b[:, :, 0:tw], "Sx2", pre=t_x2)

            x2v = bass.AP(x2b, 0, [[8 * 512, 128], [1, tw], [512, 8]])
            c.wait(ve, [t_x2, t_comb_prev])
            t_rd = c.inc(ve.tensor_reduce(out=qt1[:, 0, 0:tw], in_=x2v, op=OP.max,
                                          axis=mybir.AxisListType.X), "sv")
            c.wait(ve, t_rd)
            t_rd2 = c.inc(ve.tensor_reduce(out=qt2[:, 0, 0:tw], in_=x2v, op=OP.min,
                                           axis=mybir.AxisListType.X), "sv")
            c.wait(ve, t_rd2)
            ve.tensor_single_scalar(out=qt2[:, 0, 0:tw], in_=qt2[:, 0, 0:tw],
                                    scalar=-1.0, op=OP.mult)
            t_am = c.inc(ve.tensor_tensor(out=qt1[:, 0, 0:tw], in0=qt1[:, 0, 0:tw],
                                          in1=qt2[:, 0, 0:tw], op=OP.max), "sv")
            t_comb_prev = t_am
            t_row = row_absmax(nc, c, env, qt1[:, 0, :], tw, t_am, 4, 5, t_x2)
            c.wait(ve, [t_row, t_st1w_prev])
            t_rc = c.inc(ve.reciprocal(out=rwb[0:1, 0:tw], in_=rwa[0:1, 0:tw]), "sv")
            c.wait(ve, t_rc)
            t_st1 = c.inc(ve.tensor_single_scalar(out=rwd[0:1, 0:tw], in_=rwb[0:1, 0:tw],
                                                  scalar=127.0, op=OP.mult), "sv")
            t_st1w_prev = store(st1_scr[0:1, t0:t0 + tw], rwd[0:1, 0:tw], "Sst1", pre=t_st1)
            c.wait(te, [t_st1, t_x2])
            t_bc2 = c.inc(te.matmul(ps[:, 2, 0:tw], ones1[:], rwd[0:1, 0:tw],
                                    start=True, stop=True), "sp")
            c.wait(ve, [t_bc2, t_st_x2])
            t_y = c.inc(ve.tensor_tensor(out=yb[:, :, 0:tw], in0=x2b[:, :, 0:tw],
                                         in1=bass.AP(ps, 2 * 512, [[4096, 128], [0, 8], [1, tw]]),
                                         op=OP.mult), "sv")
            t_ps_free = t_y
            c.wait(sc, [t_y, t_round_prev])
            t_ti = c.inc(sc.activation(out=tib[:, :, 0:tw], in_=yb[:, :, 0:tw],
                                       func=AF.Identity), "ss")
            c.wait(ve, t_ti)
            t_a = c.inc(ve.scalar_tensor_tensor(out=ab[:, :, 0:tw], in0=tib[:, :, 0:tw],
                                                scalar=0.5, in1=yb[:, :, 0:tw],
                                                op0=OP.add, op1=OP.is_le), "sv")
            ve.scalar_tensor_tensor(out=bb[:, :, 0:tw], in0=tib[:, :, 0:tw],
                                    scalar=0.5, in1=yb[:, :, 0:tw],
                                    op0=OP.subtract, op1=OP.is_gt)
            ve.scalar_tensor_tensor(out=ab[:, :, 0:tw], in0=bb[:, :, 0:tw], scalar=-1.0,
                                    in1=ab[:, :, 0:tw], op0=OP.mult, op1=OP.add)
            t_xq1 = c.inc(ve.tensor_tensor(out=xq1b[:, :, 0:tw], in0=ab[:, :, 0:tw],
                                           in1=tib[:, :, 0:tw], op=OP.add), "sv")
            t_round_prev = t_xq1
            if t0 <= 127 < t0 + tw:
                j = 127 - t0
                t_xq1 = c.inc(ve.tensor_single_scalar(out=xq1b[:, :, j:j + 1],
                                                      in_=xq1b[:, :, j:j + 1],
                                                      scalar=gate_s[:, 0:1], op=OP.mult), "sv")
            if t0 <= 4224 < t0 + tw:
                j = 4224 - t0
                t_xq1 = c.inc(ve.tensor_single_scalar(out=xq1b[:, :, j:j + 1],
                                                      in_=xq1b[:, :, j:j + 1],
                                                      scalar=gater_s[:, 0:1], op=OP.mult), "sv")
            store(xq1_v[:, :, t0:t0 + tw], xq1b[:, :, 0:tw], "Sxq1", pre=t_xq1)
            t_bufs_free = [c.tok("sp"), c.tok("sv"), c.tok("ss"), c.tok("sg"), c.tok("Sx2"), c.tok("Sxq1"), c.tok("Sst1")]
        _es2.close()


def emit_f1(nc, c, env):
    sy, ve, sc, te, gp = nc.sync, nc.vector, nc.scalar, nc.tensor, nc.gpsimd
    ps = env["ps"]
    t1t_v, xq1_v = env["t1t_v"], env["xq1_v"]
    st1_scr, xq2_v, cs2_scr = env["st1_scr"], env["xq2_v"], env["cs2_scr"]
    cw_s, cb_s, ident = env["cw_s"], env["cb_s"], env["ident"]
    ones1, rsw1, rsw2 = env["ones1"], env["rsw1"], env["rsw2"]
    rwa, rwb, rwc, rwd = env["rwa"], env["rwb"], env["rwc"], env["rwd"]
    load, store = env["load"], env["store"]
    env["full_barrier"]()

    with (
        nc.sbuf_tensor("t1t", [128, DC, FF], BF16) as t1t,
        nc.sbuf_tensor("xq1w", [128, DC, 258], BF16) as xq1w,
        nc.sbuf_tensor("hb", [128, FC, 258], F32) as hb,
        nc.sbuf_tensor("cacc", [128, FC, 256], F32) as cacc,
        nc.sbuf_tensor("cs1bc", [128, 258], F32) as cs1bc,
        nc.sbuf_tensor("ti2", [128, 8, 256], I32) as ti2,
        nc.sbuf_tensor("a2", [128, 8, 256], F32) as a2,
        nc.sbuf_tensor("b2", [128, 8, 256], F32) as b2,
        nc.sbuf_tensor("xq2b", [128, FC, 256], BF16) as xq2b,
        nc.sbuf_tensor("st1l", [1, 2, 258], F32) as st1l,
    ):
        tl_t1 = load(t1t[:], t1t_v[:, :, :], "Lt1t", pre=None)
        t_ps_free = None
        t_cs1b_prev = None
        t_ev_pair = [None, None]
        t_hb_free = None
        t_cacc_free = None
        t_row_free = [None, None]
        t_group_prev = None
        t_xq2b_free = None
        t_bc_prev = None
        t_cs2w_prev = None
        t_mn_prev = None
        for ti_ in range(16):
            b = ti_ % 2
            t0 = ti_ * 256
            w0 = PADL + t0 - 1
            tl_x = load(xq1w[:], xq1_v[:, :, w0:w0 + 258], "Lxq1w",
                        pre=c.tok("sp") if ti_ >= 1 else None)
            tl_s = load(st1l[0:1, b, :], st1_scr[0:1, w0:w0 + 258], f"Lst1{b}", pre=t_row_free[b])
            c.wait(ve, [tl_s, t_bc_prev])
            t_rc = c.inc(ve.reciprocal(out=rwa[0:1, 0:258], in_=st1l[0:1, b, :]), "sv")
            c.wait(ve, t_rc)
            t_cs1 = c.inc(ve.tensor_single_scalar(out=rwa[0:1, 0:258], in_=rwa[0:1, 0:258],
                                                  scalar=rsw1[0:1, 0:1], op=OP.mult), "sv")
            t_row_free[b] = t_cs1
            if DEBUG_SCRATCH and ti_ == 0:
                store(env["dbg9"][0:1, 0, :], st1l[0:1, b, :], "Sdbg", pre=tl_s)
                t_row_free[b] = store(env["dbg9"][0:1, 1, :], rwa[0:1, 0:258], "Sdbg",
                                      pre=t_cs1)
            c.wait(te, [t_cs1, t_cs1b_prev])
            t_bc = c.inc(te.matmul(ps[:, 0, 0:258], ones1[:], rwa[0:1, 0:258],
                                   start=True, stop=True), "sp")
            t_bc_prev = t_bc
            c.wait(ve, t_bc)
            t_cs1b = c.inc(ve.tensor_copy(out=cs1bc[:], in_=ps[:, 0, 0:258]), "sv")
            t_cs1b_prev = t_cs1b

            for r in range(16):
                pair = r % 2
                for f2 in range(2):
                    fc = r * 2 + f2
                    bank = 4 + pair * 2 + f2
                    for dc in range(8):
                        if f2 == 0 and dc == 0:
                            c.wait(te, [tl_x, tl_t1] if r == 0 else t_ev_pair[pair])
                        t_mm = te.matmul(ps[:, bank, 0:258],
                                         t1t[:, dc, fc * 128:(fc + 1) * 128],
                                         xq1w[:, dc, :],
                                         start=(dc == 0), stop=(dc == 7))
                t_round = c.inc(t_mm, "sp")
                c.wait(ve, [t_round] + ([t_hb_free] if r == 0 else []))
                t_ev_pair[pair] = c.inc(ve.tensor_tensor(
                    out=hb[:, r * 2:(r + 1) * 2, :],
                    in0=bass.AP(ps, (4 + pair * 2) * 512, [[4096, 128], [512, 2], [1, 258]]),
                    in1=bass.AP(cs1bc, 0, [[258, 128], [0, 2], [1, 258]]),
                    op=OP.mult), "sv")
            t_ps_free = t_ev_pair[1]
            t_h = c.tok("sv")
            if DEBUG_SCRATCH and ti_ == 0:
                t_h = store(env["dbg8"], hb[:], "Sdbg", pre=t_ps_free)
                store(env["dbg9"][0:1, 2, :], cs1bc[0:1, :], "Sdbg", pre=t_cs1b)
                store(env["dbg9"][0:1, 3, :], cs1bc[64:65, :], "Sdbg", pre=t_cs1b)

            c.wait(ve, [t_h, t_cacc_free])
            for fc in range(FC):
                ve.tensor_scalar(out=cacc[:, fc, :], in0=hb[:, fc, 1:257],
                                 scalar1=cw_s[:, fc, 1:2], scalar2=cb_s[:, fc:fc + 1],
                                 op0=OP.mult, op1=OP.add)
            for fc in range(FC):
                ve.scalar_tensor_tensor(out=cacc[:, fc, :], in0=hb[:, fc, 0:256],
                                        scalar=cw_s[:, fc, 0:1], in1=cacc[:, fc, :],
                                        op0=OP.mult, op1=OP.add)
            for fc in range(FC):
                ve.scalar_tensor_tensor(out=cacc[:, fc, :], in0=hb[:, fc, 2:258],
                                        scalar=cw_s[:, fc, 2:3], in1=cacc[:, fc, :],
                                        op0=OP.mult, op1=OP.add)
            t_cr = c.inc(ve.nop(), "sv")
            c.wait(sc, t_cr)
            t_hs = c.inc(sc.activation(out=hb[:, :, 0:256], in_=cacc[:], func=AF.Silu), "ss")
            c.wait(sc, t_hs)
            t_abs = c.inc(sc.activation(out=cacc[:], in_=hb[:, :, 0:256], func=AF.Abs), "ss")

            c.wait(ve, t_abs)
            ve.tensor_tensor(out=cacc[:, 0:16, :], in0=cacc[:, 0:16, :],
                             in1=cacc[:, 16:32, :], op=OP.max)
            ve.tensor_tensor(out=cacc[:, 0:8, :], in0=cacc[:, 0:8, :],
                             in1=cacc[:, 8:16, :], op=OP.max)
            ve.tensor_tensor(out=cacc[:, 0:4, :], in0=cacc[:, 0:4, :],
                             in1=cacc[:, 4:8, :], op=OP.max)
            ve.tensor_tensor(out=cacc[:, 0:2, :], in0=cacc[:, 0:2, :],
                             in1=cacc[:, 2:4, :], op=OP.max)
            t_am = c.inc(ve.tensor_tensor(out=cacc[:, 0, :], in0=cacc[:, 0, :],
                                          in1=cacc[:, 1, :], op=OP.max), "sv")
            t_row = row_absmax(nc, c, env, cacc[:, 0, :], 256, t_am, 1, 2, t_h)
            c.wait(ve, [t_row, t_cs2w_prev])
            t_rc = c.inc(ve.reciprocal(out=rwb[0:1, 0:256], in_=rwa[0:1, 0:256]), "sv")
            c.wait(ve, t_rc)
            t_st2 = c.inc(ve.tensor_single_scalar(out=rwd[0:1, 0:256], in_=rwb[0:1, 0:256],
                                                  scalar=127.0, op=OP.mult), "sv")
            c.wait(ve, t_st2)
            t_rc = c.inc(ve.reciprocal(out=rwb[0:1, 0:256], in_=rwd[0:1, 0:256]), "sv")
            c.wait(ve, t_rc)
            t_cs2 = c.inc(ve.tensor_single_scalar(out=rwb[0:1, 0:256], in_=rwb[0:1, 0:256],
                                                  scalar=rsw2[0:1, 0:1], op=OP.mult), "sv")
            t_cs2w_prev = store(cs2_scr[0:1, t0:t0 + 256], rwb[0:1, 0:256], "Scs2", pre=t_cs2)
            c.wait(te, t_st2)
            t_bc2 = c.inc(te.matmul(ps[:, 3, 0:256], ones1[:], rwd[0:1, 0:256],
                                    start=True, stop=True), "sp")
            c.wait(ve, t_bc2)
            t_y = c.inc(ve.tensor_tensor(out=cacc[:], in0=hb[:, :, 0:256],
                                         in1=bass.AP(ps, 3 * 512, [[4096, 128], [0, 32], [1, 256]]),
                                         op=OP.mult), "sv")
            t_ps_free = t_y
            t_hb_free = t_y
            t_last = None
            for gq in range(4):
                slc = cacc[:, gq * 8:(gq + 1) * 8, :]
                c.wait(sc, [t_y, t_group_prev])
                t_ti = c.inc(sc.activation(out=ti2[:], in_=slc, func=AF.Identity), "ss")
                c.wait(ve, t_ti)
                t_a = c.inc(ve.scalar_tensor_tensor(out=a2[:], in0=ti2[:], scalar=0.5,
                                                    in1=slc, op0=OP.add, op1=OP.is_le), "sv")
                ve.scalar_tensor_tensor(out=b2[:], in0=ti2[:], scalar=0.5,
                                        in1=slc, op0=OP.subtract, op1=OP.is_gt)
                c.wait(ve, t_xq2b_free)
                ve.scalar_tensor_tensor(out=a2[:], in0=b2[:], scalar=-1.0, in1=a2[:],
                                        op0=OP.mult, op1=OP.add)
                t_last = c.inc(ve.tensor_tensor(out=xq2b[:, gq * 8:(gq + 1) * 8, :],
                                                in0=a2[:], in1=ti2[:], op=OP.add), "sv")
                t_group_prev = t_last
            t_cacc_free = t_last
            t_st = store(xq2_v[:, :, t0:t0 + 256], xq2b[:], "Sxq2", pre=t_last)
            t_xq2b_free = t_st


def emit_f2(nc, c, env):
    sy, ve, sc, te, gp = nc.sync, nc.vector, nc.scalar, nc.tensor, nc.gpsimd
    ps = env["ps"]
    t2t_v, xq2_v, cs2_scr = env["t2t_v"], env["xq2_v"], env["cs2_scr"]
    x2_v, y_v = env["x2_v"], env["y_v"]
    ones1, onescol = env["ones1"], env["onescol"]
    rwa, rwb, rwc, rwd = env["rwa"], env["rwb"], env["rwc"], env["rwd"]
    load, store = env["load"], env["store"]
    env["full_barrier"]()

    with (
        nc.sbuf_tensor("t2t", [128, FC, D], BF16) as t2t,
        nc.sbuf_tensor("xq2l", [128, FC, 512], BF16) as xq2l,
        nc.sbuf_tensor("x2l", [128, DC, 512], F32) as x2l,
        nc.sbuf_tensor("r2b", [128, DC, 512], F32) as r2b,
        nc.sbuf_tensor("sq2", [128, 2, 512], F32) as sq2,
        nc.sbuf_tensor("outb", [128, DC, 512], F32) as outb,
        nc.sbuf_tensor("cs2l", [1, 2, 512], F32) as cs2l,
    ):
        tl_t2 = load(t2t[:], t2t_v[:, :, :], "Lt2t", pre=None)
        t_ps_free = None
        t_bufs_free = None
        t_row_free = [None, None]
        t_sy_prev = None
        for ti_ in range(8):
            b = ti_ % 2
            t0 = ti_ * 512
            tl_x = load(xq2l[:], xq2_v[:, :, t0:t0 + 512], "Lxq2l",
                        pre=c.tok("sp") if ti_ >= 1 else None)
            tl_c = load(cs2l[0:1, b, :], cs2_scr[0:1, t0:t0 + 512], f"Lcs2{b}", pre=t_row_free[b])
            tl_x2 = load(x2l[:], x2_v[:, :, PADL + t0:PADL + t0 + 512], "Lx2l", pre=t_bufs_free)
            for m in range(8):
                for fc in range(FC):
                    if m == 0 and fc == 0:
                        c.wait(te, [tl_x, tl_t2, t_ps_free])
                    t_mm = te.matmul(ps[:, m, 0:512], t2t[:, fc, m * 128:(m + 1) * 128],
                                     xq2l[:, fc, :], start=(fc == 0), stop=(fc == FC - 1))
            t_mm = c.inc(t_mm, "sp")
            psw = bass.AP(ps, 0, [[4096, 128], [512, 8], [1, 512]])
            c.wait(ve, [t_mm, tl_c, tl_x2, t_bufs_free])
            t_cp = c.inc(ve.tensor_copy(out=r2b[:], in_=psw), "sv")
            c.wait(te, t_cp)
            t_bc = c.inc(te.matmul(ps[:, 0, 0:512], ones1[:], cs2l[0:1, b, :],
                                   start=True, stop=True), "sp")
            c.wait(ve, t_bc)
            ve.tensor_tensor(out=r2b[:], in0=r2b[:],
                             in1=bass.AP(ps, 0, [[4096, 128], [0, 8], [1, 512]]),
                             op=OP.mult)
            t_r2 = c.inc(ve.tensor_tensor(out=r2b[:], in0=r2b[:], in1=x2l[:], op=OP.add), "sv")
            t_sq = [None, None]
            for dc in range(8):
                sl = dc % 2
                c.wait(sc, [t_r2, t_sq[sl]])
                t_s = c.inc(sc.activation(out=sq2[:, sl, :], in_=r2b[:, dc, :],
                                          func=AF.Square), "ss")
                c.wait(te, [t_r2, t_s])
                te.matmul(ps[0:1, 1, 0:512], onescol[:], r2b[:, dc, :],
                          start=(dc == 0), stop=(dc == 7))
                t_sq[sl] = c.inc(te.matmul(ps[0:1, 2, 0:512], onescol[:], sq2[:, sl, :],
                                           start=(dc == 0), stop=(dc == 7)), "sp")
            c.wait(ve, t_sq[1])
            ve.tensor_single_scalar(out=rwa[0:1, 0:512], in_=ps[0:1, 1, 0:512],
                                    scalar=1.0 / D, op=OP.mult)
            ve.tensor_single_scalar(out=rwb[0:1, 0:512], in_=ps[0:1, 2, 0:512],
                                    scalar=1.0 / D, op=OP.mult)
            ve.tensor_tensor(out=rwc[0:1, 0:512], in0=rwa[0:1, 0:512], in1=rwa[0:1, 0:512],
                             op=OP.mult)
            t_var = c.inc(ve.tensor_tensor(out=rwb[0:1, 0:512], in0=rwb[0:1, 0:512],
                                           in1=rwc[0:1, 0:512], op=OP.subtract), "sv")
            c.wait(sc, t_var)
            t_std = c.inc(sc.activation(out=rwb[0:1, 0:512], in_=rwb[0:1, 0:512],
                                        func=AF.Sqrt, bias=np.float32(1e-5)), "ss")
            c.wait(ve, t_std)
            t_rstd = c.inc(ve.reciprocal(out=rwc[0:1, 0:512], in_=rwb[0:1, 0:512]), "sv")
            c.wait(te, [t_rstd])
            te.matmul(ps[:, 3, 0:512], ones1[:], rwa[0:1, 0:512], start=True, stop=True)
            t_bc2 = c.inc(te.matmul(ps[:, 4, 0:512], ones1[:], rwc[0:1, 0:512],
                                    start=True, stop=True), "sp")
            c.wait(ve, [t_bc2, t_sy_prev])
            ve.tensor_tensor(out=outb[:], in0=r2b[:],
                             in1=bass.AP(ps, 3 * 512, [[4096, 128], [0, 8], [1, 512]]),
                             op=OP.subtract)
            t_out = c.inc(ve.tensor_tensor(out=outb[:], in0=outb[:],
                                           in1=bass.AP(ps, 4 * 512, [[4096, 128], [0, 8], [1, 512]]),
                                           op=OP.mult), "sv")
            t_ps_free = t_out
            t_st = store(y_v[:, :, t0:t0 + 512], outb[:], "Sy", pre=t_out)
            t_sy_prev = t_st
            t_bufs_free = [t_st, c.tok("sv"), c.tok("ss")]
            t_row_free[b] = t_out


def _stage_inputs(inputs):
    x = np.asarray(inputs["L0_attn_x"], dtype=np.float32)
    Wq_ = np.ascontiguousarray(inputs["L0_attn_Wq"], dtype=np.float32)
    Wk_ = np.ascontiguousarray(inputs["L0_attn_Wk"], dtype=np.float32)
    Wv_ = np.ascontiguousarray(inputs["L0_attn_Wv"], dtype=np.float32)
    Wo_ = np.ascontiguousarray(inputs["L0_attn_Wo"], dtype=np.float32)
    bq = np.asarray(inputs["L0_attn_bq"], dtype=np.float32)
    bk = np.asarray(inputs["L0_attn_bk"], dtype=np.float32)
    bv = np.asarray(inputs["L0_attn_bv"], dtype=np.float32)
    bo = np.asarray(inputs["L0_attn_bo"], dtype=np.float32)
    w1_ = np.ascontiguousarray(inputs["L0_ffn_w1"], dtype=np.float32)
    w2_ = np.ascontiguousarray(inputs["L0_ffn_w2"], dtype=np.float32)
    cw = np.asarray(inputs["L0_ffn_conv_w"], dtype=np.float32).reshape(FF, 3)
    cb = np.asarray(inputs["L0_ffn_conv_b"], dtype=np.float32)

    bq_s = np.ascontiguousarray(bq.reshape(8, 128).T)
    bk_s = np.ascontiguousarray(bk.reshape(8, 128).T)
    bo_s = np.ascontiguousarray(bo.reshape(8, 128).T)
    bv_r = np.ascontiguousarray(bv.reshape(1, D))
    cw_s = np.ascontiguousarray(cw.reshape(FC, 128, 3).transpose(1, 0, 2))
    cb_s = np.ascontiguousarray(cb.reshape(FC, 128).T)

    ident = np.eye(128, dtype=np.float32)
    m0 = np.triu(np.ones((128, 128), dtype=np.float32))
    maskc = np.ascontiguousarray(np.broadcast_to(m0[:, None, :], (128, H, 128)))
    selden = np.zeros((128, DC, H), dtype=np.float32)
    selbc = np.zeros((16, DC, 128), dtype=np.float32)
    for dc in range(DC):
        for p in range(128):
            h = 2 * dc + (1 if p >= 64 else 0)
            selden[p, dc, h] = 1.0
            selbc[h, dc, p] = 1.0
    ones1 = np.ones((1, 128), dtype=np.float32)
    onescol = np.ones((128, 1), dtype=np.float32)

    shared = dict(Wq=Wq_, Wk=Wk_, Wv=Wv_, Wo=Wo_, bq_i=bq_s, bk_i=bk_s, bv_i=bv_r,
                  bo_i=bo_s, w1=w1_, w2=w2_, cw_i=cw_s, cb_i=cb_s, id_i=ident,
                  mask_i=maskc, sd_i=selden, sb_i=selbc, o1_i=ones1, oc_i=onescol)

    in_maps = []
    for core in range(8):
        bi, half = core // 2, core % 2
        xb = x[bi]
        xwin = np.zeros((D, NW), dtype=np.float32)
        if half == 0:
            xwin[:, PADL:] = xb[0:4224].T
            xpre = np.zeros((D, NP), dtype=np.float32)
            g, gr = 0.0, 1.0
        else:
            xwin[:, 0:4224] = xb[3968:8192].T
            xpre = np.ascontiguousarray(xb[0:NP].T)
            g, gr = 1.0, 0.0
        m = dict(shared)
        m["xw"] = np.ascontiguousarray(xwin)
        m["xp"] = np.ascontiguousarray(xpre)
        m["gate"] = np.full((128, 1), g, dtype=np.float32)
        m["gater"] = np.full((128, 1), gr, dtype=np.float32)
        in_maps.append(m)
    return in_maps


def kernel(**inputs):
    in_maps = _stage_inputs(inputs)
    nc = build_program()
    res = run_bass_kernel_spmd(nc, in_maps, core_ids=list(range(8)))
    out = np.empty((4, 8192, D), dtype=np.float32)
    for core in range(8):
        bi, half = core // 2, core % 2
        out[bi, half * NO:(half + 1) * NO, :] = res.results[core]["yT"].T
    return out


# version 06
# speedup vs baseline: 1.0664x; 1.0111x over previous
import numpy as np
import ml_dtypes
import concourse.bass as bass
import concourse.mybir as mybir
from concourse.bass_utils import run_bass_kernel_spmd

F32 = mybir.dt.float32
BF16 = mybir.dt.bfloat16
I32 = mybir.dt.int32
AF = mybir.ActivationFunctionType
OP = mybir.AluOpType
AX = mybir.AxisListType

D = 1024
DC = 8
H = 16
DH = 64
FF = 4096
FC = 32
NO = 4096
NW = 4352
NWCH = 34
NP = 3968
PADL = 128

N_PHASES = 5
DEBUG_SCRATCH = False
RACE_DETECT = False
A1_PF_TILES = 16
A1_MAIN_TILES = 17
A1_STAGES = 63
A1_SCAN = 63
USE_BARRIER = True
SKIP_TERN = False
SKIP_PHASES = frozenset()
SKIP_WT = False


class Ctx:
    def __init__(self, nc):
        self.nc = nc
        self.sem = {}
        self.cnt = {}
        self.last_wait = {}

    def add_sem(self, name, handle):
        self.sem[name] = handle
        self.cnt[name] = 0

    def wait(self, eng, tok):
        if tok is None:
            return
        if isinstance(tok, list):
            for t in tok:
                self.wait(eng, t)
            return
        s, v = tok
        if v <= 0:
            return
        key = (id(eng), s)
        if self.last_wait.get(key, 0) >= v:
            return
        eng.wait_ge(self.sem[s], v)
        self.last_wait[key] = v

    def inc(self, instr, s, amount=1):
        instr.then_inc(self.sem[s], amount)
        self.cnt[s] += amount
        return (s, self.cnt[s])

    def tok(self, s):
        return (s, self.cnt[s])


def build_program():
    nc = bass.Bass("TRN2", target_bir_lowering=False, detect_race_conditions=RACE_DETECT)
    c = Ctx(nc)

    xw = nc.dram_tensor("xw", [D, NW], F32, kind="ExternalInput").ap()
    xp = nc.dram_tensor("xp", [D, NP], F32, kind="ExternalInput").ap()
    gate = nc.dram_tensor("gate", [128, 1], F32, kind="ExternalInput").ap()
    gater = nc.dram_tensor("gater", [128, 1], F32, kind="ExternalInput").ap()
    Wq = nc.dram_tensor("Wq", [D, D], F32, kind="ExternalInput").ap()
    Wk = nc.dram_tensor("Wk", [D, D], F32, kind="ExternalInput").ap()
    Wv = nc.dram_tensor("Wv", [D, D], F32, kind="ExternalInput").ap()
    Wo = nc.dram_tensor("Wo", [D, D], F32, kind="ExternalInput").ap()
    bq_i = nc.dram_tensor("bq_i", [128, 8], F32, kind="ExternalInput").ap()
    bk_i = nc.dram_tensor("bk_i", [128, 8], F32, kind="ExternalInput").ap()
    bv_i = nc.dram_tensor("bv_i", [1, D], F32, kind="ExternalInput").ap()
    bo_i = nc.dram_tensor("bo_i", [128, 8], F32, kind="ExternalInput").ap()
    w1 = nc.dram_tensor("w1", [FF, D], F32, kind="ExternalInput").ap()
    w2 = nc.dram_tensor("w2", [D, FF], F32, kind="ExternalInput").ap()
    cw_i = nc.dram_tensor("cw_i", [128, FC, 3], F32, kind="ExternalInput").ap()
    cb_i = nc.dram_tensor("cb_i", [128, FC], F32, kind="ExternalInput").ap()
    id_i = nc.dram_tensor("id_i", [128, 128], F32, kind="ExternalInput").ap()
    mask_i = nc.dram_tensor("mask_i", [128, H, 128], F32, kind="ExternalInput").ap()
    sd_i = nc.dram_tensor("sd_i", [128, DC, H], F32, kind="ExternalInput").ap()
    sb_i = nc.dram_tensor("sb_i", [16, DC, 128], F32, kind="ExternalInput").ap()
    o1_i = nc.dram_tensor("o1_i", [1, 128], F32, kind="ExternalInput").ap()
    oc_i = nc.dram_tensor("oc_i", [128, 1], F32, kind="ExternalInput").ap()

    yT = nc.dram_tensor("yT", [D, NO], F32, kind="ExternalOutput").ap()

    sk = "ExternalOutput" if DEBUG_SCRATCH else "Internal"
    att_scr = nc.dram_tensor("att_scr", [D, NW], F32, kind=sk).ap()
    x2_scr = nc.dram_tensor("x2_scr", [D, NW], F32, kind=sk).ap()
    xq1_scr = nc.dram_tensor("xq1_scr", [D, NW], BF16, kind=sk).ap()
    st1_scr = nc.dram_tensor("st1_scr", [1, NW], F32, kind=sk).ap()
    t1t_scr = nc.dram_tensor("t1t_scr", [D, FF], BF16, kind=sk).ap()
    t2t_scr = nc.dram_tensor("t2t_scr", [FF, D], BF16, kind=sk).ap()
    xq2_scr = nc.dram_tensor("xq2_scr", [FF, NO], BF16, kind=sk).ap()
    dbg1 = nc.dram_tensor("dbg1", [128, 8], F32, kind=sk).ap()
    dbg2 = nc.dram_tensor("dbg2", [128, 1024], F32, kind=sk).ap()
    dbg3 = nc.dram_tensor("dbg3", [128, 1024], F32, kind=sk).ap()
    dbg4 = nc.dram_tensor("dbg4", [128, 1024], F32, kind=sk).ap()
    dbg5 = nc.dram_tensor("dbg5", [128, 8, 128], F32, kind=sk).ap()
    dbg6 = nc.dram_tensor("dbg6", [16, 2, 128], F32, kind=sk).ap()
    dbg7 = nc.dram_tensor("dbg7", [128, 16, 128], F32, kind=sk).ap()
    dbg8 = nc.dram_tensor("dbg8", [128, 32, 258], F32, kind=sk).ap()
    dbg9 = nc.dram_tensor("dbg9", [1, 4, 258], F32, kind=sk).ap()
    cs2_scr = nc.dram_tensor("cs2_scr", [1, NO], F32, kind=sk).ap()

    env = {}
    env["att_v"] = att_scr.rearrange("(dc p) t -> p dc t", p=128)
    env["x2_v"] = x2_scr.rearrange("(dc p) t -> p dc t", p=128)
    env["xq1_v"] = xq1_scr.rearrange("(dc p) t -> p dc t", p=128)
    env["xw_v"] = xw.rearrange("(dc p) t -> p dc t", p=128)
    env["xp_v"] = xp.rearrange("(dc p) t -> p dc t", p=128)
    env["t1t_v"] = t1t_scr.rearrange("(dc p) f -> p dc f", p=128)
    env["t2t_v"] = t2t_scr.rearrange("(fc p) d -> p fc d", p=128)
    env["xq2_v"] = xq2_scr.rearrange("(fc p) t -> p fc t", p=128)
    env["y_v"] = yT.rearrange("(dc p) t -> p dc t", p=128)
    env["dbg5"], env["dbg6"], env["dbg7"] = None, None, None
    env["dbg8"], env["dbg9"] = None, None
    env["st1_scr"] = st1_scr
    env["cs2_scr"] = cs2_scr
    env["Wo"] = Wo

    sy, ve, sc, te, gp = nc.sync, nc.vector, nc.scalar, nc.tensor, nc.gpsimd

    import contextlib
    _es = contextlib.ExitStack()
    if True:
        ident = _es.enter_context(nc.sbuf_tensor("ident", [128, 128], F32))
        mask = _es.enter_context(nc.sbuf_tensor("mask", [128, H, 128], F32))
        selden = _es.enter_context(nc.sbuf_tensor("selden", [128, DC, H], F32))
        selbc = _es.enter_context(nc.sbuf_tensor("selbc", [16, DC, 128], F32))
        ones1 = _es.enter_context(nc.sbuf_tensor("ones1", [1, 128], F32))
        onescol = _es.enter_context(nc.sbuf_tensor("onescol", [128, 1], F32))
        gate_s = _es.enter_context(nc.sbuf_tensor("gate_s", [128, 1], F32))
        gater_s = _es.enter_context(nc.sbuf_tensor("gater_s", [128, 1], F32))
        bq_s = _es.enter_context(nc.sbuf_tensor("bq_s", [128, 8], F32))
        bk_s = _es.enter_context(nc.sbuf_tensor("bk_s", [128, 8], F32))
        bo_s = _es.enter_context(nc.sbuf_tensor("bo_s", [128, 8], F32))
        bv_row = _es.enter_context(nc.sbuf_tensor("bv_row", [1, D], F32))
        cw_s = _es.enter_context(nc.sbuf_tensor("cw_s", [128, FC, 3], F32))
        cb_s = _es.enter_context(nc.sbuf_tensor("cb_s", [128, FC], F32))
        rsw1 = _es.enter_context(nc.sbuf_tensor("rsw1", [1, 1], F32))
        rsw2 = _es.enter_context(nc.sbuf_tensor("rsw2", [1, 1], F32))
        S_sb = _es.enter_context(nc.sbuf_tensor("S_sb", [128, DC, DH], F32))
        z_sb = _es.enter_context(nc.sbuf_tensor("z_sb", [128, DC], F32))
        zeros128 = _es.enter_context(nc.sbuf_tensor("zeros128", [128, 128], F32))
        tcol = _es.enter_context(nc.sbuf_tensor("tcol", [128, 4], F32))
        rwa = _es.enter_context(nc.sbuf_tensor("rwa", [1, 544], F32))
        rwb = _es.enter_context(nc.sbuf_tensor("rwb", [1, 544], F32))
        rwc = _es.enter_context(nc.sbuf_tensor("rwc", [1, 544], F32))
        rwd = _es.enter_context(nc.sbuf_tensor("rwd", [1, 544], F32))
        ps = _es.enter_context(nc.psum_tensor("ps", [128, 8, 512], F32))
        _s1 = _es.enter_context(nc.semaphore("si"))
        _s2 = _es.enter_context(nc.semaphore("so"))
        _s3 = _es.enter_context(nc.semaphore("sp"))
        _s4 = _es.enter_context(nc.semaphore("sv"))
        _s5 = _es.enter_context(nc.semaphore("ss"))
        _s6 = _es.enter_context(nc.semaphore("sg"))
        _blk = _es.enter_context(nc.Block())
    if True:
        for nm, h in [("si", _s1), ("so", _s2), ("sp", _s3), ("sv", _s4),
                      ("ss", _s5), ("sg", _s6)]:
            c.add_sem(nm, h)

        def dsem(name):
            if name not in c.sem:
                c.add_sem(name, _es.enter_context(nc.semaphore(name)))
            return name

        def load(dst, src, sem, pre=None):
            s = dsem(sem)
            c.wait(sy, pre)
            return c.inc(sy.dma_start(out=dst, in_=src), s, 16)

        def store(dst, src, sem, pre=None):
            s = dsem(sem)
            c.wait(gp, pre)
            return c.inc(gp.dma_start(out=dst, in_=src), s, 16)

        def full_barrier():
            if not USE_BARRIER:
                return
            for eng in (te, ve, sc, gp, sy):
                for s in list(c.sem):
                    c.wait(eng, c.tok(s))

        env["full_barrier"] = full_barrier
        env["dbg5"], env["dbg6"], env["dbg7"] = dbg5, dbg6, dbg7
        env["dbg8"], env["dbg9"] = dbg8, dbg9
        env.update(dict(ps=ps, ident=ident, mask=mask, selden=selden, selbc=selbc,
                        ones1=ones1, onescol=onescol, gate_s=gate_s, gater_s=gater_s,
                        bq_s=bq_s, bk_s=bk_s, bo_s=bo_s, bv_row=bv_row, cw_s=cw_s,
                        cb_s=cb_s, rsw1=rsw1, rsw2=rsw2, S_sb=S_sb, z_sb=z_sb,
                        zeros128=zeros128, tcol=tcol, rwa=rwa, rwb=rwb, rwc=rwc,
                        rwd=rwd, load=load, store=store))

        for dst, src in [(ident[:], id_i), (mask[:], mask_i), (selden[:], sd_i),
                         (selbc[:], sb_i), (ones1[:], o1_i), (onescol[:], oc_i),
                         (gate_s[:], gate), (gater_s[:], gater), (bq_s[:], bq_i),
                         (bk_s[:], bk_i), (bo_s[:], bo_i), (bv_row[:], bv_i),
                         (cw_s[:], cw_i), (cb_s[:], cb_i)]:
            load(dst, src, "Lconst")
        tok_consts = c.tok("Lconst")
        c.wait(ve, tok_consts)
        c.wait(te, tok_consts)
        c.wait(sc, tok_consts)
        c.wait(gp, tok_consts)
        ve.memset(zeros128[:], 0.0)
        ve.memset(S_sb[:], 0.0)
        c.inc(ve.memset(z_sb[:], 0.0), "sv")

        with (
            nc.sbuf_tensor("wstage", [128, 2, 4096], F32) as wstage,
            nc.sbuf_tensor("qy", [128, 2, 1024], F32) as qy,
            nc.sbuf_tensor("qti", [128, 2, 1024], I32) as qti,
            nc.sbuf_tensor("qa", [128, 2, 1024], F32) as qa,
            nc.sbuf_tensor("qb", [128, 2, 1024], F32) as qb,
            nc.sbuf_tensor("tern", [128, 2, 1024], F32) as tern,
            nc.sbuf_tensor("absac", [128, 40], F32) as absac,
            nc.sbuf_tensor("abscol", [128, 2], F32) as abscol,
            nc.sbuf_tensor("swcol", [128, 2], F32) as swcol,
        ):
            t_ws_slot = [None, None]
            _wlist = [] if SKIP_TERN else [(w1, 32, 1024), (w2, 8, 4096)]
            gi = 0
            for wi, (wt, ntile, tw) in enumerate(_wlist):
                for i in range(ntile):
                    sl = gi % 2
                    gi += 1
                    tl = load(wstage[:, sl, 0:tw], wt[i * 128:(i + 1) * 128, :],
                              f"Lws{sl}", pre=t_ws_slot[sl])
                    col = i if wi == 0 else 32 + i
                    c.wait(ve, tl)
                    t_ws_slot[sl] = c.inc(
                        ve.tensor_reduce(out=absac[:, col:col + 1], in_=wstage[:, sl, 0:tw],
                                         axis=AX.X, op=OP.add, apply_absolute_value=True),
                        "sv")
            t_ws_free = [t_ws_slot[0], t_ws_slot[1]]
            c.wait(ve, t_ws_free)
            ve.tensor_reduce(out=abscol[:, 0:1], in_=absac[:, 0:32], axis=AX.X, op=OP.add)
            t_red = c.inc(ve.tensor_reduce(out=abscol[:, 1:2], in_=absac[:, 32:40],
                                           axis=AX.X, op=OP.add), "sv")
            if SKIP_TERN:
                c.wait(ve, c.tok("Lconst"))
                ve.memset(rsw1[:], 1.0)
                c.inc(ve.memset(rsw2[:], 1.0), "sv")
            c.wait(te, t_red)
            t_mm = c.inc(te.matmul(ps[0:1, 0, 0:2], onescol[:], abscol[:],
                                   start=True, stop=True), "sp")
            c.wait(ve, t_mm)
            t_f = c.inc(ve.tensor_scalar(out=rwa[0:1, 0:2], in0=ps[0:1, 0, 0:2],
                                         scalar1=1.0 / (FF * D), scalar2=1e-5,
                                         op0=OP.mult, op1=OP.max), "sv")
            c.wait(ve, t_f)
            t_f = c.inc(ve.reciprocal(out=rwb[0:1, 0:2], in_=rwa[0:1, 0:2]), "sv")
            c.wait(ve, t_f)
            t_f = c.inc(ve.reciprocal(out=rwc[0:1, 0:2], in_=rwb[0:1, 0:2]), "sv")
            c.wait(ve, t_f)
            ve.tensor_copy(out=rsw1[:], in_=rwc[0:1, 0:1])
            t_rw = c.inc(ve.tensor_copy(out=rsw2[:], in_=rwc[0:1, 1:2]), "sv")
            c.wait(te, t_rw)
            t_mm = c.inc(te.matmul(ps[:, 0, 0:2], ones1[:], rwb[0:1, 0:2],
                                   start=True, stop=True), "sp")
            c.wait(ve, t_mm)
            t_sw = c.inc(ve.tensor_copy(out=swcol[:], in_=ps[:, 0, 0:2]), "sv")
            if DEBUG_SCRATCH:
                store(dbg1[:, 0:2], swcol[:], "Sdbg", pre=t_sw)
                store(dbg1[:, 2:4], abscol[:], "Sdbg", pre=t_red)
                store(dbg1[0:1, 4:6], rwa[0:1, 0:2], "Sdbg", pre=t_rw)
                store(dbg1[0:1, 6:8], rwb[0:1, 0:2], "Sdbg", pre=t_rw)

            t_spill = None
            t_ws_slot2 = [t_ws_free, t_ws_free]
            t_y_slot = [None, None]
            t_ti_slot = [None, None]
            t_tern_slot = [None, None]
            gi = 0
            for wi, (wt, ntile, tw) in enumerate(_wlist):
                tsb_shape = [128, DC, FF] if wi == 0 else [128, FC, D]
                with nc.sbuf_tensor(f"tts{wi}", tsb_shape, BF16) as tts:
                    c.wait(sc, t_spill)
                    tr_evac = [None] * 4
                    for i in range(ntile):
                        wl = gi % 2
                        gi += 1
                        tl = load(wstage[:, wl, 0:tw], wt[i * 128:(i + 1) * 128, :],
                                  f"Lws{wl}", pre=t_ws_slot2[wl])
                        for s4 in range(tw // 1024):
                            sl = s4 % 2
                            wsl = wstage[:, wl, s4 * 1024:(s4 + 1) * 1024]
                            c.wait(ve, [tl, t_sw, t_ti_slot[sl], t_tern_slot[sl]])
                            t_y = c.inc(ve.tensor_single_scalar(
                                out=qy[:, sl, :], in_=wsl, scalar=swcol[:, wi:wi + 1],
                                op=OP.mult), "sv")
                            t_y_slot[sl] = t_y
                            c.wait(sc, [t_y, t_tern_slot[sl]])
                            t_ti = c.inc(sc.activation(out=qti[:, sl, :], in_=qy[:, sl, :],
                                                       func=AF.Identity), "ss")
                            t_ti_slot[sl] = t_ti
                            c.wait(ve, t_ti)
                            t_a = c.inc(ve.scalar_tensor_tensor(
                                out=qa[:, sl, :], in0=qti[:, sl, :], scalar=0.5,
                                in1=qy[:, sl, :], op0=OP.add, op1=OP.is_le), "sv")
                            ve.scalar_tensor_tensor(
                                out=qb[:, sl, :], in0=qti[:, sl, :], scalar=0.5,
                                in1=qy[:, sl, :], op0=OP.subtract, op1=OP.is_gt)
                            ve.scalar_tensor_tensor(out=qa[:, sl, :], in0=qb[:, sl, :],
                                                    scalar=-1.0, in1=qa[:, sl, :],
                                                    op0=OP.mult, op1=OP.add)
                            ve.tensor_tensor(out=qa[:, sl, :], in0=qa[:, sl, :],
                                             in1=qti[:, sl, :], op=OP.add)
                            t_tern = c.inc(ve.tensor_scalar(out=tern[:, sl, :],
                                                            in0=qa[:, sl, :],
                                                            scalar1=-1.0, scalar2=1.0,
                                                            op0=OP.max, op1=OP.min), "sv")
                            if DEBUG_SCRATCH and wi == 0 and i == 0:
                                store(dbg2[:], tern[:, sl, :], "Sdbg", pre=t_tern)
                            if DEBUG_SCRATCH and wi == 0 and i == 31:
                                store(dbg3[:], tern[:, sl, :], "Sdbg", pre=t_tern)
                                store(dbg4[:], qy[:, sl, :], "Sdbg", pre=t_y)
                            for dcb in range(8):
                                qb_ = dcb % 4
                                c.wait(te, [t_tern, tr_evac[qb_]])
                                t_tr = c.inc(te.transpose(
                                    ps[:, 4 + qb_, 0:128],
                                    tern[:, sl, dcb * 128:(dcb + 1) * 128], ident[:]), "sp")
                                c.wait(sc, t_tr)
                                if wi == 0:
                                    dst = tts[:, dcb, i * 128:(i + 1) * 128]
                                else:
                                    dst = tts[:, s4 * 8 + dcb, i * 128:(i + 1) * 128]
                                tr_evac[qb_] = c.inc(sc.copy(out=dst,
                                                             in_=ps[:, 4 + qb_, 0:128]), "ss")
                            t_tern_slot[sl] = c.tok("sp")
                        t_ws_slot2[wl] = c.tok("sv")
                    t_spill = store(env["t1t_v"][:, :, :] if wi == 0 else env["t2t_v"][:, :, :],
                                    tts[:], f"Stts{wi}", pre=[c.tok("ss")])

        full_barrier()
        with (
            nc.sbuf_tensor("WqTh", [128, DC, D], BF16) as WqTh,
            nc.sbuf_tensor("WqTl", [128, DC, D], BF16) as WqTl,
            nc.sbuf_tensor("WkTh", [128, DC, D], BF16) as WkTh,
            nc.sbuf_tensor("WkTl", [128, DC, D], BF16) as WkTl,
            nc.sbuf_tensor("WvTh", [128, DC, D], BF16) as WvTh,
            nc.sbuf_tensor("WvTl", [128, DC, D], BF16) as WvTl,
        ):
            with nc.sbuf_tensor("wstg2", [128, 1024], F32) as wstg2:
                t_stg_free = None
                tr_evac = [None] * 4
                for wt, wdh, wdl in ([] if SKIP_WT else
                                     [(Wq, WqTh, WqTl), (Wk, WkTh, WkTl),
                                      (Wv, WvTh, WvTl)]):
                    for m in range(8):
                        tl = load(wstg2[:], wt[m * 128:(m + 1) * 128, :], "Lws2", pre=t_stg_free)
                        for kc in range(8):
                            qb_ = kc % 4
                            c.wait(te, [tl, tr_evac[qb_]])
                            t_tr = c.inc(te.transpose(ps[:, 4 + qb_, 0:128],
                                                      wstg2[:, kc * 128:(kc + 1) * 128],
                                                      ident[:]), "sp")
                            c.wait(sc, t_tr)
                            t_hi = c.inc(sc.copy(out=wdh[:, kc, m * 128:(m + 1) * 128],
                                                 in_=ps[:, 4 + qb_, 0:128]), "ss")
                            c.wait(ve, t_hi)
                            tr_evac[qb_] = c.inc(ve.tensor_tensor(
                                out=wdl[:, kc, m * 128:(m + 1) * 128],
                                in0=ps[:, 4 + qb_, 0:128],
                                in1=wdh[:, kc, m * 128:(m + 1) * 128],
                                op=OP.subtract), "sv")
                        t_stg_free = c.tok("sp")
            env.update(dict(WqTh=WqTh, WqTl=WqTl, WkTh=WkTh, WkTl=WkTl,
                            WvTh=WvTh, WvTl=WvTl))
            env["tok_wT"] = c.tok("sv")

            if N_PHASES >= 2 and 2 not in SKIP_PHASES:
                emit_a1(nc, c, env)

        if N_PHASES >= 3 and 3 not in SKIP_PHASES:
            emit_a2(nc, c, env)
        if N_PHASES >= 4 and 4 not in SKIP_PHASES:
            emit_f1(nc, c, env)
        if N_PHASES >= 5 and 5 not in SKIP_PHASES:
            emit_f2(nc, c, env)

        for s in list(c.sem):
            c.wait(sy, c.tok(s))

    _es.close()
    return nc


def emit_a1(nc, c, env):
    sy, ve, sc, te, gp = nc.sync, nc.vector, nc.scalar, nc.tensor, nc.gpsimd
    ps = env["ps"]
    WqTh, WqTl = env["WqTh"], env["WqTl"]
    WkTh, WkTl = env["WkTh"], env["WkTl"]
    WvTh, WvTl = env["WvTh"], env["WvTl"]
    xw_v, xp_v, att_v = env["xw_v"], env["xp_v"], env["att_v"]
    bq_s, bk_s, bv_row = env["bq_s"], env["bk_s"], env["bv_row"]
    ones1, ident, mask = env["ones1"], env["ident"], env["mask"]
    selden, selbc = env["selden"], env["selbc"]
    gate_s, zeros128 = env["gate_s"], env["zeros128"]
    S_sb, z_sb = env["S_sb"], env["z_sb"]
    load, store = env["load"], env["store"]
    env["full_barrier"]()

    with (
        nc.sbuf_tensor("xin", [128, DC, 256], F32) as xin,
        nc.sbuf_tensor("qf", [128, DC, 256], F32) as qf,
        nc.sbuf_tensor("kf", [128, DC, 256], F32) as kf,
        nc.sbuf_tensor("vtm", [128, 2, D], F32) as vtm,
        nc.sbuf_tensor("ktm", [128, DC, 128], F32) as ktm,
        nc.sbuf_tensor("kcum", [128, DC, 128], F32) as kcum,
        nc.sbuf_tensor("pbuf", [128, DC, 128], F32) as pbuf,
        nc.sbuf_tensor("ssb", [128, H, 128], F32) as ssb,
        nc.sbuf_tensor("denr", [16, 128], F32) as denr,
        nc.sbuf_tensor("dbc_sb", [128, DC, 128], F32) as dbc_sb,
        nc.sbuf_tensor("dinv", [16, 128], F32) as dinv,
        nc.sbuf_tensor("outst", [128, 2, DC, 128], F32) as outst,
        nc.sbuf_tensor("ph1", [128, 2, 256], F32) as ph1,
        nc.sbuf_tensor("ph2", [128, 2, 256], F32) as ph2,
        nc.sbuf_tensor("ph3", [128, 2, 256], F32) as ph3,
        nc.sbuf_tensor("ztmp", [128, DC], F32) as ztmp,
        nc.sbuf_tensor("xhh", [128, DC, 256], BF16) as xh,
        nc.sbuf_tensor("xll", [128, DC, 256], BF16) as xl,
    ):
        st = {
            "xin_free": None,
            "xh_free": None,
            "b07_free": None,
            "b45_free": None,
            "b6_free": None,
            "S_started": [False, False],
            "phi_free": [None, None],
            "outst_free": [None, None],
            "vtm_free": [None, None],
            "S_sb_ok": None,
            "ssb_free": None,
            "kcum_free": None,
            "dbc_free": None,
            "pbuf_free": None,
            "z_ok": None,
            "sc03_free": None,
        }

        def cast_hl(xbuf, tw, t_x):
            c.wait(sc, [t_x, st["xh_free"]])
            t_xh = c.inc(sc.copy(out=xh[:, :, 0:tw], in_=xbuf[:, :, 0:tw]), "ss")
            c.wait(ve, t_xh)
            return c.inc(ve.tensor_tensor(out=xl[:, :, 0:tw], in0=xbuf[:, :, 0:tw],
                                          in1=xh[:, :, 0:tw], op=OP.subtract), "sv")

        def proj_qk(wTh, wTl, bias, dstf, tw, t_x):
            t_e1 = None
            for m in range(8):
                reg = ps[:, m, 0:tw]
                for kc in range(8):
                    if kc == 0 and m == 0:
                        c.wait(te, [t_x, st["b07_free"], st["sc03_free"],
                                    st["b45_free"], st["b6_free"], st["S_sb_ok"]])
                    te.matmul(reg, wTh[:, kc, m * 128:(m + 1) * 128],
                              xh[:, kc, 0:tw], start=(kc == 0), stop=False)
                    te.matmul(reg, wTh[:, kc, m * 128:(m + 1) * 128],
                              xl[:, kc, 0:tw], start=False, stop=False)
                    t_mm = te.matmul(reg, wTl[:, kc, m * 128:(m + 1) * 128],
                                     xh[:, kc, 0:tw], start=False, stop=(kc == 7))
                t_mm = c.inc(t_mm, "sp")
                s = m % 2
                c.wait(sc, [t_mm, st["phi_free"][s]])
                t_e1 = c.inc(sc.activation(out=ph1[:, s, 0:tw], in_=reg, func=AF.Identity,
                                           bias=bias[:, m:m + 1]), "ss")
                c.wait(gp, t_e1)
                t_mn = c.inc(gp.tensor_single_scalar(out=ph2[:, s, 0:tw],
                                                     in_=ph1[:, s, 0:tw],
                                                     scalar=0.0, op=OP.min), "sg")
                c.wait(sc, t_mn)
                t_ex = c.inc(sc.activation(out=ph3[:, s, 0:tw], in_=ph2[:, s, 0:tw],
                                           func=AF.Exp), "ss")
                c.wait(ve, t_ex)
                st["phi_free"][s] = c.inc(
                    ve.scalar_tensor_tensor(out=dstf[:, m, 0:tw], in0=ph1[:, s, 0:tw],
                                            scalar=0.0, in1=ph3[:, s, 0:tw],
                                            op0=OP.max, op1=OP.add), "sv")
            t_all_evac = c.tok("ss")
            st["b07_free"] = t_all_evac
            st["sc03_free"] = t_all_evac
            st["b45_free"] = t_all_evac
            st["b6_free"] = t_all_evac
            return c.tok("sv")

        def proj_v(cc, t_x):
            for half in range(2):
                reg = ps[:, 4 + half, 0:512]
                for kc in range(8):
                    if kc == 0 and half == 0:
                        c.wait(te, [t_x, st["b45_free"]])
                    te.matmul(reg, xh[:, kc, cc * 128:(cc + 1) * 128],
                              WvTh[:, kc, half * 512:(half + 1) * 512],
                              start=(kc == 0), stop=False)
                    te.matmul(reg, xl[:, kc, cc * 128:(cc + 1) * 128],
                              WvTh[:, kc, half * 512:(half + 1) * 512],
                              start=False, stop=False)
                    te.matmul(reg, xh[:, kc, cc * 128:(cc + 1) * 128],
                              WvTl[:, kc, half * 512:(half + 1) * 512],
                              start=False, stop=False)
                t_mm = c.inc(te.matmul(reg, ones1[0:1, 0:128],
                                       bv_row[0:1, half * 512:(half + 1) * 512],
                                       start=False, stop=True), "sp")
            c.wait(sc, [t_mm, st["vtm_free"][cc]])
            t_ev = c.inc(sc.copy(out=vtm[:, cc, :], in_=ps[:, 4:6, 0:512]), "ss")
            st["b45_free"] = t_ev
            return t_ev

        def ktm_make(cc, t_k):
            tr_ev = [None, None]
            for dc in range(8):
                bnk = 4 + (dc % 2)
                c.wait(te, [t_k, st["b45_free"]] if dc == 0 else tr_ev[dc % 2])
                t_tr = c.inc(te.transpose(ps[:, bnk, 0:128],
                                          kf[:, dc, cc * 128:(cc + 1) * 128], ident[:]), "sp")
                c.wait(sc, t_tr)
                tr_ev[dc % 2] = c.inc(sc.copy(out=ktm[:, dc, :],
                                              in_=ps[:, bnk, 0:128]), "ss")
            st["b45_free"] = c.tok("ss")
            return c.tok("ss")

        def s_update(cc, t_ktm, t_v, t_num):
            for hi, h in enumerate(list(range(0, H, 2)) + list(range(1, H, 2))):
                p0 = (h % 2) * 64
                dc = h // 2
                if hi == 0:
                    c.wait(te, [t_ktm, t_v, st["S_sb_ok"]])
                t_mm = te.matmul(ps[p0:p0 + 64, 7, dc * 64:(dc + 1) * 64],
                                 ktm[:, dc, (h % 2) * 64:(h % 2) * 64 + 64],
                                 vtm[:, cc, h * 64:(h + 1) * 64],
                                 start=(h <= 1), stop=(hi % 8 == 7),
                                 tile_position=(0, p0))
            t_su = c.inc(t_mm, "sp")
            st["vtm_free"][cc] = t_su
            sflat = bass.AP(S_sb, 0, [[512, 128], [1, 512]])
            c.wait(ve, [t_su, t_num])
            st["S_sb_ok"] = c.inc(ve.tensor_tensor(out=sflat, in0=sflat,
                                                   in1=ps[:, 7, 0:512], op=OP.add), "sv")
            return st["S_sb_ok"]

        def scan_chunk(g, cc, is_last_main, t_q, t_k, t_ktm, t_v):
            for hi, h in enumerate(list(range(0, H, 2)) + list(range(1, H, 2))):
                p0 = (h % 2) * 64
                dc = h // 2
                if hi == 0:
                    c.wait(te, [t_q, t_k, st["sc03_free"], st["b07_free"]])
                t_mm = te.matmul(ps[:, h // 4, (h % 4) * 128:(h % 4 + 1) * 128],
                                 kf[p0:p0 + 64, dc, cc * 128:(cc + 1) * 128],
                                 qf[p0:p0 + 64, dc, cc * 128:(cc + 1) * 128],
                                 start=(h % 4 == 0), stop=(h % 4 == 3))
            t_sc = c.inc(t_mm, "sp")
            c.wait(ve, [t_k, st["kcum_free"], st["z_ok"]])
            t_scan = None
            for dc in range(8):
                t_scan = ve.tensor_tensor_scan(out=kcum[:, dc, :],
                                               data0=kf[:, dc, cc * 128:(cc + 1) * 128],
                                               data1=zeros128[:], initial=z_sb[:, dc:dc + 1],
                                               op0=OP.add, op1=OP.add)
            t_scan = c.inc(t_scan, "sv")
            c.wait(ve, [t_scan, t_q, st["pbuf_free"]])
            t_p0 = c.inc(ve.tensor_tensor(out=pbuf[:], in0=qf[:, :, cc * 128:(cc + 1) * 128],
                                          in1=kcum[:], op=OP.mult), "sv")
            c.wait(ve, t_p0)
            t_z = c.inc(ve.tensor_copy(out=bass.AP(z_sb, 0, [[8, 128], [1, 8], [1, 1]]),
                                       in_=kcum[:, :, 127:128]), "sv")
            st["z_ok"] = t_z
            t_p = t_z
            for dc in range(8):
                if dc == 0:
                    c.wait(te, [t_p, st["b6_free"]])
                t_mm = te.matmul(ps[0:16, 6, 0:128], selden[:, dc, :], pbuf[:, dc, :],
                                 start=(dc == 0), stop=(dc == 7))
            t_den = c.inc(t_mm, "sp")
            st["kcum_free"] = t_scan
            st["pbuf_free"] = t_den
            if DEBUG_SCRATCH and g == 0:
                st["pbuf_free"] = store(env["dbg5"], pbuf[:], "Sdbg", pre=[t_p, t_den])
            c.wait(ve, t_den)
            t_clamp = c.inc(ve.tensor_single_scalar(out=denr[:], in_=ps[0:16, 6, 0:128],
                                                    scalar=1e-30, op=OP.max), "sv")
            c.wait(ve, t_clamp)
            t_r = c.inc(ve.reciprocal(out=dinv[:], in_=denr[:]), "sv")
            c.wait(ve, t_r)
            t_dinv = c.inc(ve.tensor_copy(out=denr[:], in_=dinv[:]), "sv")
            st["b6_free"] = t_dinv
            if DEBUG_SCRATCH and g == 0:
                store(env["dbg6"][:, 0, :], denr[:], "Sdbg", pre=t_dinv)
                store(env["dbg6"][:, 1, :], dinv[:], "Sdbg", pre=t_dinv)
            for dc in range(8):
                if dc == 0:
                    c.wait(te, [t_dinv, st["b45_free"]])
                t_mm = te.matmul(ps[:, 4 + dc // 4, (dc % 4) * 128:(dc % 4 + 1) * 128],
                                 selbc[:, dc, :], dinv[:],
                                 start=(dc % 4 == 0), stop=(dc % 4 == 3))
            t_bc = c.inc(t_mm, "sp")
            c.wait(sc, [t_bc, st["dbc_free"]])
            t_dbc = c.inc(sc.copy(out=dbc_sb[:],
                                  in_=bass.AP(ps, 4 * 512, [[4096, 128], [128, 8], [1, 128]])), "ss")
            st["b45_free"] = t_dbc
            c.wait(ve, [t_sc, st["ssb_free"]])
            t_ssb = c.inc(ve.tensor_tensor(
                out=ssb[:], in0=bass.AP(ps, 0, [[4096, 128], [128, 16], [1, 128]]),
                in1=mask[:], op=OP.mult), "sv")
            st["sc03_free"] = t_ssb
            if DEBUG_SCRATCH and g == 0:
                st["ssb_free"] = store(env["dbg7"], ssb[:], "Sdbg", pre=t_ssb)
            for hi, h in enumerate(list(range(0, H, 2)) + list(range(1, H, 2))):
                p0 = (h % 2) * 64
                dc = h // 2
                if hi == 0:
                    c.wait(te, [t_ssb, st["S_sb_ok"], st["b45_free"]])
                te.matmul(ps[p0:p0 + 64, 4 + dc // 4, (dc % 4) * 128:(dc % 4 + 1) * 128],
                          vtm[:, cc, h * 64:(h + 1) * 64], ssb[:, h, :],
                          start=(h % 8 <= 1), stop=False, tile_position=(0, p0))
                t_mm = te.matmul(ps[p0:p0 + 64, 4 + dc // 4, (dc % 4) * 128:(dc % 4 + 1) * 128],
                                 S_sb[p0:p0 + 64, dc, :],
                                 qf[p0:p0 + 64, dc, cc * 128:(cc + 1) * 128],
                                 start=False, stop=(h % 8 >= 6), tile_position=(p0, p0))
            t_num = c.inc(t_mm, "sp")
            st["ssb_free"] = t_num
            sl = g % 2
            c.wait(ve, [t_num, t_dbc, st["outst_free"][sl]])
            t_oev = c.inc(ve.tensor_tensor(
                out=outst[:, sl, :, :],
                in0=bass.AP(ps, 4 * 512, [[4096, 128], [128, 8], [1, 128]]),
                in1=dbc_sb[:], op=OP.mult), "sv")
            st["b45_free"] = t_oev
            st["dbc_free"] = t_oev
            st["outst_free"][sl] = store(att_v[:, :, g * 128:(g + 1) * 128],
                                         outst[:, sl, :, :], f"Satt{sl}", pre=t_oev)
            s_update(cc, t_ktm, t_v, t_num)

        pf_tiles = ([(i * 256, 256) for i in range(15)] + [(3840, 128)])[:A1_PF_TILES]
        t_su = None
        for ti_, (t0, tw) in enumerate(pf_tiles):
            b = ti_ % 2
            tl = load(xin[:, :, 0:tw], xp_v[:, :, t0:t0 + tw], f"Lxin{b}", pre=st["xin_free"])
            t_hl = cast_hl(xin, tw, tl)
            t_k = proj_qk(WkTh, WkTl, bk_s, kf, tw, t_hl)
            if A1_STAGES & 2:
                c.wait(ve, t_k)
                t_k = c.inc(ve.tensor_single_scalar(out=kf[:, :, 0:tw], in_=kf[:, :, 0:tw],
                                                    scalar=gate_s[:, 0:1], op=OP.mult), "sv")
                c.wait(ve, t_k)
                t_zr = c.inc(ve.tensor_reduce(out=ztmp[:], in_=kf[:, :, 0:tw],
                                              axis=AX.X, op=OP.add), "sv")
                c.wait(ve, t_zr)
                c.inc(ve.tensor_tensor(out=z_sb[:], in0=z_sb[:], in1=ztmp[:], op=OP.add), "sv")
            t_vs = [proj_v(cc, t_hl) if A1_STAGES & 4 else None
                    for cc in range(tw // 128)]
            for cc in range(tw // 128):
                t_ktm = ktm_make(cc, t_k) if A1_STAGES & 8 else None
                if A1_STAGES & 16:
                    t_su = s_update(cc, t_ktm, t_vs[cc], None)
            st["xin_free"] = t_hl
            st["xh_free"] = c.tok("sp")

        for ti_ in range(A1_MAIN_TILES):
            b = ti_ % 2
            t0 = ti_ * 256
            tl = load(xin[:, :, 0:256], xw_v[:, :, t0:t0 + 256], f"Lxin{b}", pre=st["xin_free"])
            if A1_STAGES & 128:
                c.wait(ve, tl)
                ve.tensor_single_scalar(out=qf[:], in_=xin[:, :, 0:256], scalar=1.0, op=OP.max)
                t_q = t_k = c.inc(ve.tensor_single_scalar(out=kf[:], in_=xin[:, :, 0:256],
                                                          scalar=1.0, op=OP.max), "sv")
            else:
                t_hl = cast_hl(xin, 256, tl)
                t_q = proj_qk(WqTh, WqTl, bq_s, qf, 256, t_hl)
                t_k = proj_qk(WkTh, WkTl, bk_s, kf, 256, t_hl)
            if ti_ == 0:
                c.wait(ve, [t_q, t_k])
                ve.tensor_single_scalar(out=qf[:, :, 0:128], in_=qf[:, :, 0:128],
                                        scalar=gate_s[:, 0:1], op=OP.mult)
                t_q = t_k = c.inc(
                    ve.tensor_single_scalar(out=kf[:, :, 0:128], in_=kf[:, :, 0:128],
                                            scalar=gate_s[:, 0:1], op=OP.mult), "sv")
            ncc = 1 if A1_SCAN & 65536 else 2
            t_vs = [proj_v(cc, t_hl) for cc in range(ncc)]
            for cc in range(ncc):
                g = ti_ * 2 + cc
                t_ktm = ktm_make(cc, t_k)
                scan_chunk(g, cc, g == NWCH - 1, t_q, t_k, t_ktm, t_vs[cc])
            st["xin_free"] = t_hl
            st["xh_free"] = c.tok("sp")


def row_absmax(nc, c, env, srcap, tw, t_src, bank_t, bank_r, pre_te):
    sy, ve, sc, te, gp = nc.sync, nc.vector, nc.scalar, nc.tensor, nc.gpsimd
    ps, ident, tcol = env["ps"], env["ident"], env["tcol"]
    rwa = env["rwa"]
    nblk = tw // 128
    t_red = None
    for j in range(nblk):
        c.wait(te, [t_src, pre_te] if j == 0 else t_red)
        t_tr = c.inc(te.transpose(ps[:, bank_t, 0:128], srcap[:, j * 128:(j + 1) * 128],
                                  ident[:]), "sp")
        c.wait(ve, t_tr)
        t_red = c.inc(ve.tensor_reduce(out=tcol[:, j:j + 1], in_=ps[:, bank_t, 0:128],
                                       axis=AX.X, op=OP.max), "sv")
    for j in range(nblk):
        c.wait(te, t_red if j == 0 else None)
        t_tr2 = c.inc(te.transpose(ps[0:1, bank_r, j * 128:(j + 1) * 128],
                                   tcol[:, j:j + 1], ident[:]), "sp")
    c.wait(ve, t_tr2)
    t_row = c.inc(ve.tensor_single_scalar(out=rwa[0:1, 0:tw], in_=ps[0:1, bank_r, 0:tw],
                                          scalar=1e-5, op=OP.max), "sv")
    return t_row


def emit_a2(nc, c, env):
    sy, ve, sc, te, gp = nc.sync, nc.vector, nc.scalar, nc.tensor, nc.gpsimd
    ps = env["ps"]
    Wo, att_v, xw_v = env["Wo"], env["att_v"], env["xw_v"]
    x2_v, xq1_v, st1_scr = env["x2_v"], env["xq1_v"], env["st1_scr"]
    bo_s, ones1, onescol, ident = env["bo_s"], env["ones1"], env["onescol"], env["ident"]
    gate_s, gater_s = env["gate_s"], env["gater_s"]
    rwa, rwb, rwc, rwd = env["rwa"], env["rwb"], env["rwc"], env["rwd"]
    load, store = env["load"], env["store"]
    env["full_barrier"]()

    with (
        nc.sbuf_tensor("WoTh", [128, DC, D], BF16) as WoTh,
        nc.sbuf_tensor("WoTl", [128, DC, D], BF16) as WoTl,
    ):
        with nc.sbuf_tensor("wstg3", [128, 1024], F32) as wstg3:
            t_stg_free = None
            tr_evac = [None] * 4
            for m in range(8):
                tl = load(wstg3[:], Wo[m * 128:(m + 1) * 128, :], "Lws3", pre=t_stg_free)
                for kc in range(8):
                    qb_ = kc % 4
                    c.wait(te, [tl, tr_evac[qb_]])
                    t_tr = c.inc(te.transpose(ps[:, 4 + qb_, 0:128],
                                              wstg3[:, kc * 128:(kc + 1) * 128], ident[:]), "sp")
                    c.wait(sc, t_tr)
                    t_hi = c.inc(sc.copy(out=WoTh[:, kc, m * 128:(m + 1) * 128],
                                         in_=ps[:, 4 + qb_, 0:128]), "ss")
                    c.wait(ve, t_hi)
                    tr_evac[qb_] = c.inc(ve.tensor_tensor(
                        out=WoTl[:, kc, m * 128:(m + 1) * 128],
                        in0=ps[:, 4 + qb_, 0:128],
                        in1=WoTh[:, kc, m * 128:(m + 1) * 128],
                        op=OP.subtract), "sv")
                t_stg_free = c.tok("sp")
        t_woT = c.tok("sv")

        import contextlib
        _es2 = contextlib.ExitStack()
        ath = _es2.enter_context(nc.sbuf_tensor("ath", [128, DC, 512], BF16))
        atl = _es2.enter_context(nc.sbuf_tensor("atl", [128, DC, 512], BF16))
        attl = _es2.enter_context(nc.sbuf_tensor("attl", [128, DC, 512], F32))
        xl = _es2.enter_context(nc.sbuf_tensor("xl", [128, DC, 512], F32))
        x2b = _es2.enter_context(nc.sbuf_tensor("x2b", [128, DC, 512], F32))
        sqb = _es2.enter_context(nc.sbuf_tensor("sqb", [128, 2, 512], F32))
        yb = _es2.enter_context(nc.sbuf_tensor("yb", [128, DC, 512], F32))
        tib = _es2.enter_context(nc.sbuf_tensor("tib", [128, DC, 512], I32))
        ab = _es2.enter_context(nc.sbuf_tensor("ab", [128, DC, 512], F32))
        bb = _es2.enter_context(nc.sbuf_tensor("bb", [128, DC, 512], F32))
        xq1b = _es2.enter_context(nc.sbuf_tensor("xq1b", [128, DC, 512], BF16))
        qt1 = _es2.enter_context(nc.sbuf_tensor("qt1", [128, 1, 512], F32))
        qt2 = _es2.enter_context(nc.sbuf_tensor("qt2", [128, 1, 512], F32))

        tiles = [(i * 512, 512) for i in range(8)] + [(4096, 256)]
        t_ps_free = t_woT
        t_ath_free = None
        t_bufs_free = None
        t_round_prev = None
        t_comb_prev = None
        t_st1w_prev = None
        t_al_prev = None
        for ti_, (t0, tw) in enumerate(tiles):
            tl1 = load(attl[:, :, 0:tw], att_v[:, :, t0:t0 + tw], "Lattl", pre=t_al_prev)
            tl2 = load(xl[:, :, 0:tw], xw_v[:, :, t0:t0 + tw], "Lxl", pre=t_bufs_free)
            c.wait(sc, [tl1, t_ath_free])
            t_ah = c.inc(sc.copy(out=ath[:, :, 0:tw], in_=attl[:, :, 0:tw]), "ss")
            c.wait(ve, t_ah)
            t_al = c.inc(ve.tensor_tensor(out=atl[:, :, 0:tw], in0=attl[:, :, 0:tw],
                                          in1=ath[:, :, 0:tw], op=OP.subtract), "sv")
            t_al_prev = t_al
            for m in range(8):
                for kc in range(8):
                    if m == 0 and kc == 0:
                        c.wait(te, [t_al, t_ps_free])
                    te.matmul(ps[:, m, 0:tw], WoTh[:, kc, m * 128:(m + 1) * 128],
                              ath[:, kc, 0:tw], start=(kc == 0), stop=False)
                    te.matmul(ps[:, m, 0:tw], WoTh[:, kc, m * 128:(m + 1) * 128],
                              atl[:, kc, 0:tw], start=False, stop=False)
                    t_mm = te.matmul(ps[:, m, 0:tw], WoTl[:, kc, m * 128:(m + 1) * 128],
                                     ath[:, kc, 0:tw], start=False, stop=(kc == 7))
            t_mm = c.inc(t_mm, "sp")
            t_ath_free = c.tok("sp")
            psw = bass.AP(ps, 0, [[4096, 128], [512, 8], [1, tw]])
            c.wait(ve, [t_mm, tl2, t_bufs_free])
            ve.tensor_tensor(out=x2b[:, :, 0:tw], in0=psw, in1=xl[:, :, 0:tw], op=OP.add)
            t_r1 = c.inc(ve.tensor_tensor(
                out=x2b[:, :, 0:tw], in0=x2b[:, :, 0:tw],
                in1=bass.AP(bo_s, 0, [[8, 128], [1, 8], [0, tw]]), op=OP.add), "sv")
            t_sq = [None, None]
            for dc in range(8):
                sl = dc % 2
                c.wait(sc, [t_r1, t_sq[sl]])
                t_s = c.inc(sc.activation(out=sqb[:, sl, 0:tw], in_=x2b[:, dc, 0:tw],
                                          func=AF.Square), "ss")
                c.wait(te, [t_r1, t_s])
                te.matmul(ps[0:1, 0, 0:tw], onescol[:], x2b[:, dc, 0:tw],
                          start=(dc == 0), stop=(dc == 7))
                t_sq[sl] = c.inc(te.matmul(ps[0:1, 1, 0:tw], onescol[:], sqb[:, sl, 0:tw],
                                           start=(dc == 0), stop=(dc == 7)), "sp")
            c.wait(ve, t_sq[1])
            ve.tensor_single_scalar(out=rwa[0:1, 0:tw], in_=ps[0:1, 0, 0:tw],
                                    scalar=1.0 / D, op=OP.mult)
            ve.tensor_single_scalar(out=rwb[0:1, 0:tw], in_=ps[0:1, 1, 0:tw],
                                    scalar=1.0 / D, op=OP.mult)
            ve.tensor_tensor(out=rwc[0:1, 0:tw], in0=rwa[0:1, 0:tw], in1=rwa[0:1, 0:tw],
                             op=OP.mult)
            t_var = c.inc(ve.tensor_tensor(out=rwb[0:1, 0:tw], in0=rwb[0:1, 0:tw],
                                           in1=rwc[0:1, 0:tw], op=OP.subtract), "sv")
            c.wait(sc, t_var)
            t_std = c.inc(sc.activation(out=rwb[0:1, 0:tw], in_=rwb[0:1, 0:tw],
                                        func=AF.Sqrt, bias=np.float32(1e-5)), "ss")
            c.wait(ve, t_std)
            t_rstd = c.inc(ve.reciprocal(out=rwc[0:1, 0:tw], in_=rwb[0:1, 0:tw]), "sv")
            c.wait(te, [t_rstd, t_r1])
            te.matmul(ps[:, 2, 0:tw], ones1[:], rwa[0:1, 0:tw], start=True, stop=True)
            t_bc = c.inc(te.matmul(ps[:, 3, 0:tw], ones1[:], rwc[0:1, 0:tw],
                                   start=True, stop=True), "sp")
            c.wait(ve, t_bc)
            ve.tensor_tensor(out=x2b[:, :, 0:tw], in0=x2b[:, :, 0:tw],
                             in1=bass.AP(ps, 2 * 512, [[4096, 128], [0, 8], [1, tw]]),
                             op=OP.subtract)
            t_x2 = c.inc(ve.tensor_tensor(out=x2b[:, :, 0:tw], in0=x2b[:, :, 0:tw],
                                          in1=bass.AP(ps, 3 * 512, [[4096, 128], [0, 8], [1, tw]]),
                                          op=OP.mult), "sv")
            t_st_x2 = store(x2_v[:, :, t0:t0 + tw], x2b[:, :, 0:tw], "Sx2", pre=t_x2)

            x2v = bass.AP(x2b, 0, [[8 * 512, 128], [1, tw], [512, 8]])
            c.wait(ve, [t_x2, t_comb_prev])
            t_rd = c.inc(ve.tensor_reduce(out=qt1[:, 0, 0:tw], in_=x2v, op=OP.max,
                                          axis=mybir.AxisListType.X), "sv")
            c.wait(ve, t_rd)
            t_rd2 = c.inc(ve.tensor_reduce(out=qt2[:, 0, 0:tw], in_=x2v, op=OP.min,
                                           axis=mybir.AxisListType.X), "sv")
            c.wait(ve, t_rd2)
            ve.tensor_single_scalar(out=qt2[:, 0, 0:tw], in_=qt2[:, 0, 0:tw],
                                    scalar=-1.0, op=OP.mult)
            t_am = c.inc(ve.tensor_tensor(out=qt1[:, 0, 0:tw], in0=qt1[:, 0, 0:tw],
                                          in1=qt2[:, 0, 0:tw], op=OP.max), "sv")
            t_comb_prev = t_am
            t_row = row_absmax(nc, c, env, qt1[:, 0, :], tw, t_am, 4, 5, t_x2)
            c.wait(ve, [t_row, t_st1w_prev])
            t_rc = c.inc(ve.reciprocal(out=rwb[0:1, 0:tw], in_=rwa[0:1, 0:tw]), "sv")
            c.wait(ve, t_rc)
            t_st1 = c.inc(ve.tensor_single_scalar(out=rwd[0:1, 0:tw], in_=rwb[0:1, 0:tw],
                                                  scalar=127.0, op=OP.mult), "sv")
            t_st1w_prev = store(st1_scr[0:1, t0:t0 + tw], rwd[0:1, 0:tw], "Sst1", pre=t_st1)
            c.wait(te, [t_st1, t_x2])
            t_bc2 = c.inc(te.matmul(ps[:, 2, 0:tw], ones1[:], rwd[0:1, 0:tw],
                                    start=True, stop=True), "sp")
            c.wait(ve, [t_bc2, t_st_x2])
            t_y = c.inc(ve.tensor_tensor(out=yb[:, :, 0:tw], in0=x2b[:, :, 0:tw],
                                         in1=bass.AP(ps, 2 * 512, [[4096, 128], [0, 8], [1, tw]]),
                                         op=OP.mult), "sv")
            t_ps_free = t_y
            c.wait(sc, [t_y, t_round_prev])
            t_ti = c.inc(sc.activation(out=tib[:, :, 0:tw], in_=yb[:, :, 0:tw],
                                       func=AF.Identity), "ss")
            c.wait(ve, t_ti)
            t_a = c.inc(ve.scalar_tensor_tensor(out=ab[:, :, 0:tw], in0=tib[:, :, 0:tw],
                                                scalar=0.5, in1=yb[:, :, 0:tw],
                                                op0=OP.add, op1=OP.is_le), "sv")
            ve.scalar_tensor_tensor(out=bb[:, :, 0:tw], in0=tib[:, :, 0:tw],
                                    scalar=0.5, in1=yb[:, :, 0:tw],
                                    op0=OP.subtract, op1=OP.is_gt)
            ve.scalar_tensor_tensor(out=ab[:, :, 0:tw], in0=bb[:, :, 0:tw], scalar=-1.0,
                                    in1=ab[:, :, 0:tw], op0=OP.mult, op1=OP.add)
            t_xq1 = c.inc(ve.tensor_tensor(out=xq1b[:, :, 0:tw], in0=ab[:, :, 0:tw],
                                           in1=tib[:, :, 0:tw], op=OP.add), "sv")
            t_round_prev = t_xq1
            if t0 <= 127 < t0 + tw:
                j = 127 - t0
                t_xq1 = c.inc(ve.tensor_single_scalar(out=xq1b[:, :, j:j + 1],
                                                      in_=xq1b[:, :, j:j + 1],
                                                      scalar=gate_s[:, 0:1], op=OP.mult), "sv")
            if t0 <= 4224 < t0 + tw:
                j = 4224 - t0
                t_xq1 = c.inc(ve.tensor_single_scalar(out=xq1b[:, :, j:j + 1],
                                                      in_=xq1b[:, :, j:j + 1],
                                                      scalar=gater_s[:, 0:1], op=OP.mult), "sv")
            store(xq1_v[:, :, t0:t0 + tw], xq1b[:, :, 0:tw], "Sxq1", pre=t_xq1)
            t_bufs_free = [c.tok("sp"), c.tok("sv"), c.tok("ss"), c.tok("sg"), c.tok("Sx2"), c.tok("Sxq1"), c.tok("Sst1")]
        _es2.close()


def emit_f1(nc, c, env):
    sy, ve, sc, te, gp = nc.sync, nc.vector, nc.scalar, nc.tensor, nc.gpsimd
    ps = env["ps"]
    t1t_v, xq1_v = env["t1t_v"], env["xq1_v"]
    st1_scr, xq2_v, cs2_scr = env["st1_scr"], env["xq2_v"], env["cs2_scr"]
    cw_s, cb_s, ident = env["cw_s"], env["cb_s"], env["ident"]
    ones1, rsw1, rsw2 = env["ones1"], env["rsw1"], env["rsw2"]
    rwa, rwb, rwc, rwd = env["rwa"], env["rwb"], env["rwc"], env["rwd"]
    load, store = env["load"], env["store"]
    env["full_barrier"]()

    with (
        nc.sbuf_tensor("t1t", [128, DC, FF], BF16) as t1t,
        nc.sbuf_tensor("xq1w", [128, DC, 258], BF16) as xq1w,
        nc.sbuf_tensor("hb", [128, FC, 258], F32) as hb,
        nc.sbuf_tensor("cacc", [128, FC, 256], F32) as cacc,
        nc.sbuf_tensor("cs1bc", [128, 258], F32) as cs1bc,
        nc.sbuf_tensor("ti2", [128, 8, 256], I32) as ti2,
        nc.sbuf_tensor("a2", [128, 8, 256], F32) as a2,
        nc.sbuf_tensor("b2", [128, 8, 256], F32) as b2,
        nc.sbuf_tensor("xq2b", [128, FC, 256], BF16) as xq2b,
        nc.sbuf_tensor("st1l", [1, 2, 258], F32) as st1l,
    ):
        tl_t1 = load(t1t[:], t1t_v[:, :, :], "Lt1t", pre=None)
        t_ps_free = None
        t_cs1b_prev = None
        t_ev_pair = [None, None]
        t_hb_free = None
        t_cacc_free = None
        t_row_free = [None, None]
        t_group_prev = None
        t_xq2b_free = None
        t_bc_prev = None
        t_cs2w_prev = None
        t_mn_prev = None
        for ti_ in range(16):
            b = ti_ % 2
            t0 = ti_ * 256
            w0 = PADL + t0 - 1
            tl_x = load(xq1w[:], xq1_v[:, :, w0:w0 + 258], "Lxq1w",
                        pre=c.tok("sp") if ti_ >= 1 else None)
            tl_s = load(st1l[0:1, b, :], st1_scr[0:1, w0:w0 + 258], f"Lst1{b}", pre=t_row_free[b])
            c.wait(ve, [tl_s, t_bc_prev])
            t_rc = c.inc(ve.reciprocal(out=rwa[0:1, 0:258], in_=st1l[0:1, b, :]), "sv")
            c.wait(ve, t_rc)
            t_cs1 = c.inc(ve.tensor_single_scalar(out=rwa[0:1, 0:258], in_=rwa[0:1, 0:258],
                                                  scalar=rsw1[0:1, 0:1], op=OP.mult), "sv")
            t_row_free[b] = t_cs1
            if DEBUG_SCRATCH and ti_ == 0:
                store(env["dbg9"][0:1, 0, :], st1l[0:1, b, :], "Sdbg", pre=tl_s)
                t_row_free[b] = store(env["dbg9"][0:1, 1, :], rwa[0:1, 0:258], "Sdbg",
                                      pre=t_cs1)
            c.wait(te, [t_cs1, t_cs1b_prev])
            t_bc = c.inc(te.matmul(ps[:, 0, 0:258], ones1[:], rwa[0:1, 0:258],
                                   start=True, stop=True), "sp")
            t_bc_prev = t_bc
            c.wait(ve, t_bc)
            t_cs1b = c.inc(ve.tensor_copy(out=cs1bc[:], in_=ps[:, 0, 0:258]), "sv")
            t_cs1b_prev = t_cs1b

            for r in range(16):
                pair = r % 2
                for f2 in range(2):
                    fc = r * 2 + f2
                    bank = 4 + pair * 2 + f2
                    for dc in range(8):
                        if f2 == 0 and dc == 0:
                            c.wait(te, [tl_x, tl_t1] if r == 0 else t_ev_pair[pair])
                        t_mm = te.matmul(ps[:, bank, 0:258],
                                         t1t[:, dc, fc * 128:(fc + 1) * 128],
                                         xq1w[:, dc, :],
                                         start=(dc == 0), stop=(dc == 7))
                t_round = c.inc(t_mm, "sp")
                c.wait(ve, [t_round] + ([t_hb_free] if r == 0 else []))
                t_ev_pair[pair] = c.inc(ve.tensor_tensor(
                    out=hb[:, r * 2:(r + 1) * 2, :],
                    in0=bass.AP(ps, (4 + pair * 2) * 512, [[4096, 128], [512, 2], [1, 258]]),
                    in1=bass.AP(cs1bc, 0, [[258, 128], [0, 2], [1, 258]]),
                    op=OP.mult), "sv")
            t_ps_free = t_ev_pair[1]
            t_h = c.tok("sv")
            if DEBUG_SCRATCH and ti_ == 0:
                t_h = store(env["dbg8"], hb[:], "Sdbg", pre=t_ps_free)
                store(env["dbg9"][0:1, 2, :], cs1bc[0:1, :], "Sdbg", pre=t_cs1b)
                store(env["dbg9"][0:1, 3, :], cs1bc[64:65, :], "Sdbg", pre=t_cs1b)

            c.wait(ve, [t_h, t_cacc_free])
            for fc in range(FC):
                ve.tensor_scalar(out=cacc[:, fc, :], in0=hb[:, fc, 1:257],
                                 scalar1=cw_s[:, fc, 1:2], scalar2=cb_s[:, fc:fc + 1],
                                 op0=OP.mult, op1=OP.add)
            for fc in range(FC):
                ve.scalar_tensor_tensor(out=cacc[:, fc, :], in0=hb[:, fc, 0:256],
                                        scalar=cw_s[:, fc, 0:1], in1=cacc[:, fc, :],
                                        op0=OP.mult, op1=OP.add)
            for fc in range(FC):
                ve.scalar_tensor_tensor(out=cacc[:, fc, :], in0=hb[:, fc, 2:258],
                                        scalar=cw_s[:, fc, 2:3], in1=cacc[:, fc, :],
                                        op0=OP.mult, op1=OP.add)
            t_cr = c.inc(ve.nop(), "sv")
            c.wait(sc, t_cr)
            t_hs = c.inc(sc.activation(out=hb[:, :, 0:256], in_=cacc[:], func=AF.Silu), "ss")
            c.wait(sc, t_hs)
            t_abs = c.inc(sc.activation(out=cacc[:], in_=hb[:, :, 0:256], func=AF.Abs), "ss")

            c.wait(ve, t_abs)
            ve.tensor_tensor(out=cacc[:, 0:16, :], in0=cacc[:, 0:16, :],
                             in1=cacc[:, 16:32, :], op=OP.max)
            ve.tensor_tensor(out=cacc[:, 0:8, :], in0=cacc[:, 0:8, :],
                             in1=cacc[:, 8:16, :], op=OP.max)
            ve.tensor_tensor(out=cacc[:, 0:4, :], in0=cacc[:, 0:4, :],
                             in1=cacc[:, 4:8, :], op=OP.max)
            ve.tensor_tensor(out=cacc[:, 0:2, :], in0=cacc[:, 0:2, :],
                             in1=cacc[:, 2:4, :], op=OP.max)
            t_am = c.inc(ve.tensor_tensor(out=cacc[:, 0, :], in0=cacc[:, 0, :],
                                          in1=cacc[:, 1, :], op=OP.max), "sv")
            t_row = row_absmax(nc, c, env, cacc[:, 0, :], 256, t_am, 1, 2, t_h)
            c.wait(ve, [t_row, t_cs2w_prev])
            t_rc = c.inc(ve.reciprocal(out=rwb[0:1, 0:256], in_=rwa[0:1, 0:256]), "sv")
            c.wait(ve, t_rc)
            t_st2 = c.inc(ve.tensor_single_scalar(out=rwd[0:1, 0:256], in_=rwb[0:1, 0:256],
                                                  scalar=127.0, op=OP.mult), "sv")
            c.wait(ve, t_st2)
            t_rc = c.inc(ve.reciprocal(out=rwb[0:1, 0:256], in_=rwd[0:1, 0:256]), "sv")
            c.wait(ve, t_rc)
            t_cs2 = c.inc(ve.tensor_single_scalar(out=rwb[0:1, 0:256], in_=rwb[0:1, 0:256],
                                                  scalar=rsw2[0:1, 0:1], op=OP.mult), "sv")
            t_cs2w_prev = store(cs2_scr[0:1, t0:t0 + 256], rwb[0:1, 0:256], "Scs2", pre=t_cs2)
            c.wait(te, t_st2)
            t_bc2 = c.inc(te.matmul(ps[:, 3, 0:256], ones1[:], rwd[0:1, 0:256],
                                    start=True, stop=True), "sp")
            c.wait(ve, t_bc2)
            t_y = c.inc(ve.tensor_tensor(out=cacc[:], in0=hb[:, :, 0:256],
                                         in1=bass.AP(ps, 3 * 512, [[4096, 128], [0, 32], [1, 256]]),
                                         op=OP.mult), "sv")
            t_ps_free = t_y
            t_hb_free = t_y
            t_last = None
            for gq in range(4):
                slc = cacc[:, gq * 8:(gq + 1) * 8, :]
                c.wait(sc, [t_y, t_group_prev])
                t_ti = c.inc(sc.activation(out=ti2[:], in_=slc, func=AF.Identity), "ss")
                c.wait(ve, t_ti)
                t_a = c.inc(ve.scalar_tensor_tensor(out=a2[:], in0=ti2[:], scalar=0.5,
                                                    in1=slc, op0=OP.add, op1=OP.is_le), "sv")
                ve.scalar_tensor_tensor(out=b2[:], in0=ti2[:], scalar=0.5,
                                        in1=slc, op0=OP.subtract, op1=OP.is_gt)
                c.wait(ve, t_xq2b_free)
                ve.scalar_tensor_tensor(out=a2[:], in0=b2[:], scalar=-1.0, in1=a2[:],
                                        op0=OP.mult, op1=OP.add)
                t_last = c.inc(ve.tensor_tensor(out=xq2b[:, gq * 8:(gq + 1) * 8, :],
                                                in0=a2[:], in1=ti2[:], op=OP.add), "sv")
                t_group_prev = t_last
            t_cacc_free = t_last
            t_st = store(xq2_v[:, :, t0:t0 + 256], xq2b[:], "Sxq2", pre=t_last)
            t_xq2b_free = t_st


def emit_f2(nc, c, env):
    sy, ve, sc, te, gp = nc.sync, nc.vector, nc.scalar, nc.tensor, nc.gpsimd
    ps = env["ps"]
    t2t_v, xq2_v, cs2_scr = env["t2t_v"], env["xq2_v"], env["cs2_scr"]
    x2_v, y_v = env["x2_v"], env["y_v"]
    ones1, onescol = env["ones1"], env["onescol"]
    rwa, rwb, rwc, rwd = env["rwa"], env["rwb"], env["rwc"], env["rwd"]
    load, store = env["load"], env["store"]
    env["full_barrier"]()

    with (
        nc.sbuf_tensor("t2t", [128, FC, D], BF16) as t2t,
        nc.sbuf_tensor("xq2l", [128, FC, 512], BF16) as xq2l,
        nc.sbuf_tensor("x2l", [128, DC, 512], F32) as x2l,
        nc.sbuf_tensor("r2b", [128, DC, 512], F32) as r2b,
        nc.sbuf_tensor("sq2", [128, 2, 512], F32) as sq2,
        nc.sbuf_tensor("outb", [128, DC, 512], F32) as outb,
        nc.sbuf_tensor("cs2l", [1, 2, 512], F32) as cs2l,
    ):
        tl_t2 = load(t2t[:], t2t_v[:, :, :], "Lt2t", pre=None)
        t_ps_free = None
        t_ev_g = [None, None]
        t_bufs_free = None
        t_row_free = [None, None]
        t_sy_prev = None
        for ti_ in range(8):
            b = ti_ % 2
            t0 = ti_ * 512
            tl_x = load(xq2l[:], xq2_v[:, :, t0:t0 + 512], "Lxq2l",
                        pre=c.tok("sp") if ti_ >= 1 else None)
            tl_c = load(cs2l[0:1, b, :], cs2_scr[0:1, t0:t0 + 512], f"Lcs2{b}", pre=t_row_free[b])
            tl_x2 = load(x2l[:], x2_v[:, :, PADL + t0:PADL + t0 + 512], "Lx2l", pre=t_bufs_free)
            for grp in range(2):
                for mi in range(4):
                    m = grp * 4 + mi
                    for fc in range(FC):
                        if mi == 0 and fc == 0:
                            c.wait(te, [tl_x, tl_t2, t_ev_g[1 - grp]])
                        t_mm = te.matmul(ps[:, 4 + mi, 0:512],
                                         t2t[:, fc, m * 128:(m + 1) * 128],
                                         xq2l[:, fc, :], start=(fc == 0), stop=(fc == FC - 1))
                t_g = c.inc(t_mm, "sp")
                c.wait(ve, [t_g, t_bufs_free])
                t_ev_g[grp] = c.inc(ve.tensor_copy(
                    out=r2b[:, grp * 4:(grp + 1) * 4, :],
                    in_=bass.AP(ps, 4 * 512, [[4096, 128], [512, 4], [1, 512]])), "sv")
            c.wait(te, [t_ev_g[1], tl_c])
            t_bc = c.inc(te.matmul(ps[:, 0, 0:512], ones1[:], cs2l[0:1, b, :],
                                   start=True, stop=True), "sp")
            c.wait(ve, [t_bc, tl_x2])
            ve.tensor_tensor(out=r2b[:], in0=r2b[:],
                             in1=bass.AP(ps, 0, [[4096, 128], [0, 8], [1, 512]]),
                             op=OP.mult)
            t_r2 = c.inc(ve.tensor_tensor(out=r2b[:], in0=r2b[:], in1=x2l[:], op=OP.add), "sv")
            t_sq = [None, None]
            for dc in range(8):
                sl = dc % 2
                c.wait(sc, [t_r2, t_sq[sl]])
                t_s = c.inc(sc.activation(out=sq2[:, sl, :], in_=r2b[:, dc, :],
                                          func=AF.Square), "ss")
                c.wait(te, [t_r2, t_s])
                te.matmul(ps[0:1, 1, 0:512], onescol[:], r2b[:, dc, :],
                          start=(dc == 0), stop=(dc == 7))
                t_sq[sl] = c.inc(te.matmul(ps[0:1, 2, 0:512], onescol[:], sq2[:, sl, :],
                                           start=(dc == 0), stop=(dc == 7)), "sp")
            c.wait(ve, t_sq[1])
            ve.tensor_single_scalar(out=rwa[0:1, 0:512], in_=ps[0:1, 1, 0:512],
                                    scalar=1.0 / D, op=OP.mult)
            ve.tensor_single_scalar(out=rwb[0:1, 0:512], in_=ps[0:1, 2, 0:512],
                                    scalar=1.0 / D, op=OP.mult)
            ve.tensor_tensor(out=rwc[0:1, 0:512], in0=rwa[0:1, 0:512], in1=rwa[0:1, 0:512],
                             op=OP.mult)
            t_var = c.inc(ve.tensor_tensor(out=rwb[0:1, 0:512], in0=rwb[0:1, 0:512],
                                           in1=rwc[0:1, 0:512], op=OP.subtract), "sv")
            c.wait(sc, t_var)
            t_std = c.inc(sc.activation(out=rwb[0:1, 0:512], in_=rwb[0:1, 0:512],
                                        func=AF.Sqrt, bias=np.float32(1e-5)), "ss")
            c.wait(ve, t_std)
            t_rstd = c.inc(ve.reciprocal(out=rwc[0:1, 0:512], in_=rwb[0:1, 0:512]), "sv")
            c.wait(te, [t_rstd])
            te.matmul(ps[:, 3, 0:512], ones1[:], rwa[0:1, 0:512], start=True, stop=True)
            t_bc2 = c.inc(te.matmul(ps[:, 4, 0:512], ones1[:], rwc[0:1, 0:512],
                                    start=True, stop=True), "sp")
            c.wait(ve, [t_bc2, t_sy_prev])
            ve.tensor_tensor(out=outb[:], in0=r2b[:],
                             in1=bass.AP(ps, 3 * 512, [[4096, 128], [0, 8], [1, 512]]),
                             op=OP.subtract)
            t_out = c.inc(ve.tensor_tensor(out=outb[:], in0=outb[:],
                                           in1=bass.AP(ps, 4 * 512, [[4096, 128], [0, 8], [1, 512]]),
                                           op=OP.mult), "sv")
            t_ps_free = t_out
            t_st = store(y_v[:, :, t0:t0 + 512], outb[:], "Sy", pre=t_out)
            t_sy_prev = t_st
            t_bufs_free = [t_st, c.tok("sv"), c.tok("ss")]
            t_row_free[b] = t_out


def _stage_inputs(inputs):
    x = np.asarray(inputs["L0_attn_x"], dtype=np.float32)
    Wq_ = np.ascontiguousarray(inputs["L0_attn_Wq"], dtype=np.float32)
    Wk_ = np.ascontiguousarray(inputs["L0_attn_Wk"], dtype=np.float32)
    Wv_ = np.ascontiguousarray(inputs["L0_attn_Wv"], dtype=np.float32)
    Wo_ = np.ascontiguousarray(inputs["L0_attn_Wo"], dtype=np.float32)
    bq = np.asarray(inputs["L0_attn_bq"], dtype=np.float32)
    bk = np.asarray(inputs["L0_attn_bk"], dtype=np.float32)
    bv = np.asarray(inputs["L0_attn_bv"], dtype=np.float32)
    bo = np.asarray(inputs["L0_attn_bo"], dtype=np.float32)
    w1_ = np.ascontiguousarray(inputs["L0_ffn_w1"], dtype=np.float32)
    w2_ = np.ascontiguousarray(inputs["L0_ffn_w2"], dtype=np.float32)
    cw = np.asarray(inputs["L0_ffn_conv_w"], dtype=np.float32).reshape(FF, 3)
    cb = np.asarray(inputs["L0_ffn_conv_b"], dtype=np.float32)

    bq_s = np.ascontiguousarray(bq.reshape(8, 128).T)
    bk_s = np.ascontiguousarray(bk.reshape(8, 128).T)
    bo_s = np.ascontiguousarray(bo.reshape(8, 128).T)
    bv_r = np.ascontiguousarray(bv.reshape(1, D))
    cw_s = np.ascontiguousarray(cw.reshape(FC, 128, 3).transpose(1, 0, 2))
    cb_s = np.ascontiguousarray(cb.reshape(FC, 128).T)

    ident = np.eye(128, dtype=np.float32)
    m0 = np.triu(np.ones((128, 128), dtype=np.float32))
    maskc = np.ascontiguousarray(np.broadcast_to(m0[:, None, :], (128, H, 128)))
    selden = np.zeros((128, DC, H), dtype=np.float32)
    selbc = np.zeros((16, DC, 128), dtype=np.float32)
    for dc in range(DC):
        for p in range(128):
            h = 2 * dc + (1 if p >= 64 else 0)
            selden[p, dc, h] = 1.0
            selbc[h, dc, p] = 1.0
    ones1 = np.ones((1, 128), dtype=np.float32)
    onescol = np.ones((128, 1), dtype=np.float32)

    shared = dict(Wq=Wq_, Wk=Wk_, Wv=Wv_, Wo=Wo_, bq_i=bq_s, bk_i=bk_s, bv_i=bv_r,
                  bo_i=bo_s, w1=w1_, w2=w2_, cw_i=cw_s, cb_i=cb_s, id_i=ident,
                  mask_i=maskc, sd_i=selden, sb_i=selbc, o1_i=ones1, oc_i=onescol)

    in_maps = []
    for core in range(8):
        bi, half = core // 2, core % 2
        xb = x[bi]
        xwin = np.zeros((D, NW), dtype=np.float32)
        if half == 0:
            xwin[:, PADL:] = xb[0:4224].T
            xpre = np.zeros((D, NP), dtype=np.float32)
            g, gr = 0.0, 1.0
        else:
            xwin[:, 0:4224] = xb[3968:8192].T
            xpre = np.ascontiguousarray(xb[0:NP].T)
            g, gr = 1.0, 0.0
        m = dict(shared)
        m["xw"] = np.ascontiguousarray(xwin)
        m["xp"] = np.ascontiguousarray(xpre)
        m["gate"] = np.full((128, 1), g, dtype=np.float32)
        m["gater"] = np.full((128, 1), gr, dtype=np.float32)
        in_maps.append(m)
    return in_maps


def kernel(**inputs):
    in_maps = _stage_inputs(inputs)
    nc = build_program()
    res = run_bass_kernel_spmd(nc, in_maps, core_ids=list(range(8)))
    out = np.empty((4, 8192, D), dtype=np.float32)
    for core in range(8):
        bi, half = core // 2, core % 2
        out[bi, half * NO:(half + 1) * NO, :] = res.results[core]["yT"].T
    return out


# version 08 (final)
# speedup vs baseline: 1.0781x; 1.0109x over previous
import numpy as np
import ml_dtypes
import concourse.bass as bass
import concourse.mybir as mybir
from concourse.bass_utils import run_bass_kernel_spmd

F32 = mybir.dt.float32
BF16 = mybir.dt.bfloat16
I32 = mybir.dt.int32
AF = mybir.ActivationFunctionType
OP = mybir.AluOpType
AX = mybir.AxisListType

D = 1024
DC = 8
H = 16
DH = 64
FF = 4096
FC = 32
NO = 4096
NW = 4352
NWCH = 34
NP = 3968
PADL = 128

N_PHASES = 5
DEBUG_SCRATCH = False
RACE_DETECT = False
A1_PF_TILES = 16
A1_MAIN_TILES = 17
A1_STAGES = 63
A1_SCAN = 63
USE_BARRIER = True
SKIP_TERN = False
SKIP_PHASES = frozenset()
SKIP_WT = False


class Ctx:
    def __init__(self, nc):
        self.nc = nc
        self.sem = {}
        self.cnt = {}
        self.last_wait = {}

    def add_sem(self, name, handle):
        self.sem[name] = handle
        self.cnt[name] = 0

    def wait(self, eng, tok):
        if tok is None:
            return
        if isinstance(tok, list):
            for t in tok:
                self.wait(eng, t)
            return
        s, v = tok
        if v <= 0:
            return
        key = (id(eng), s)
        if self.last_wait.get(key, 0) >= v:
            return
        eng.wait_ge(self.sem[s], v)
        self.last_wait[key] = v

    def inc(self, instr, s, amount=1):
        instr.then_inc(self.sem[s], amount)
        self.cnt[s] += amount
        return (s, self.cnt[s])

    def tok(self, s):
        return (s, self.cnt[s])


def build_program():
    nc = bass.Bass("TRN2", target_bir_lowering=False, detect_race_conditions=RACE_DETECT)
    c = Ctx(nc)

    xw = nc.dram_tensor("xw", [D, NW], F32, kind="ExternalInput").ap()
    xp = nc.dram_tensor("xp", [D, NP], F32, kind="ExternalInput").ap()
    gate = nc.dram_tensor("gate", [128, 1], F32, kind="ExternalInput").ap()
    gater = nc.dram_tensor("gater", [128, 1], F32, kind="ExternalInput").ap()
    Wq = nc.dram_tensor("Wq", [D, D], F32, kind="ExternalInput").ap()
    Wk = nc.dram_tensor("Wk", [D, D], F32, kind="ExternalInput").ap()
    Wv = nc.dram_tensor("Wv", [D, D], F32, kind="ExternalInput").ap()
    Wo = nc.dram_tensor("Wo", [D, D], F32, kind="ExternalInput").ap()
    bq_i = nc.dram_tensor("bq_i", [128, 8], F32, kind="ExternalInput").ap()
    bk_i = nc.dram_tensor("bk_i", [128, 8], F32, kind="ExternalInput").ap()
    bv_i = nc.dram_tensor("bv_i", [1, D], F32, kind="ExternalInput").ap()
    bo_i = nc.dram_tensor("bo_i", [128, 8], F32, kind="ExternalInput").ap()
    w1 = nc.dram_tensor("w1", [FF, D], F32, kind="ExternalInput").ap()
    w2 = nc.dram_tensor("w2", [D, FF], F32, kind="ExternalInput").ap()
    cw_i = nc.dram_tensor("cw_i", [128, FC, 3], F32, kind="ExternalInput").ap()
    cb_i = nc.dram_tensor("cb_i", [128, FC], F32, kind="ExternalInput").ap()
    id_i = nc.dram_tensor("id_i", [128, 128], F32, kind="ExternalInput").ap()
    mask_i = nc.dram_tensor("mask_i", [128, H, 128], F32, kind="ExternalInput").ap()
    sd_i = nc.dram_tensor("sd_i", [128, DC, H], F32, kind="ExternalInput").ap()
    sb_i = nc.dram_tensor("sb_i", [16, DC, 128], F32, kind="ExternalInput").ap()
    o1_i = nc.dram_tensor("o1_i", [1, 128], F32, kind="ExternalInput").ap()
    oc_i = nc.dram_tensor("oc_i", [128, 1], F32, kind="ExternalInput").ap()

    yT = nc.dram_tensor("yT", [D, NO], F32, kind="ExternalOutput").ap()

    sk = "ExternalOutput" if DEBUG_SCRATCH else "Internal"
    att_scr = nc.dram_tensor("att_scr", [D, NW], F32, kind=sk).ap()
    x2_scr = nc.dram_tensor("x2_scr", [D, NW], F32, kind=sk).ap()
    xq1_scr = nc.dram_tensor("xq1_scr", [D, NW], BF16, kind=sk).ap()
    st1_scr = nc.dram_tensor("st1_scr", [1, NW], F32, kind=sk).ap()
    t1t_scr = nc.dram_tensor("t1t_scr", [D, FF], BF16, kind=sk).ap()
    t2t_scr = nc.dram_tensor("t2t_scr", [FF, D], BF16, kind=sk).ap()
    xq2_scr = nc.dram_tensor("xq2_scr", [FF, NO], BF16, kind=sk).ap()
    dbg1 = nc.dram_tensor("dbg1", [128, 8], F32, kind=sk).ap()
    dbg2 = nc.dram_tensor("dbg2", [128, 1024], F32, kind=sk).ap()
    dbg3 = nc.dram_tensor("dbg3", [128, 1024], F32, kind=sk).ap()
    dbg4 = nc.dram_tensor("dbg4", [128, 1024], F32, kind=sk).ap()
    dbg5 = nc.dram_tensor("dbg5", [128, 8, 128], F32, kind=sk).ap()
    dbg6 = nc.dram_tensor("dbg6", [16, 2, 128], F32, kind=sk).ap()
    dbg7 = nc.dram_tensor("dbg7", [128, 16, 128], F32, kind=sk).ap()
    dbg8 = nc.dram_tensor("dbg8", [128, 32, 258], F32, kind=sk).ap()
    dbg9 = nc.dram_tensor("dbg9", [1, 4, 258], F32, kind=sk).ap()
    cs2_scr = nc.dram_tensor("cs2_scr", [1, NO], F32, kind=sk).ap()

    env = {}
    env["att_v"] = att_scr.rearrange("(dc p) t -> p dc t", p=128)
    env["x2_v"] = x2_scr.rearrange("(dc p) t -> p dc t", p=128)
    env["xq1_v"] = xq1_scr.rearrange("(dc p) t -> p dc t", p=128)
    env["xw_v"] = xw.rearrange("(dc p) t -> p dc t", p=128)
    env["xp_v"] = xp.rearrange("(dc p) t -> p dc t", p=128)
    env["t1t_v"] = t1t_scr.rearrange("(dc p) f -> p dc f", p=128)
    env["t2t_v"] = t2t_scr.rearrange("(fc p) d -> p fc d", p=128)
    env["xq2_v"] = xq2_scr.rearrange("(fc p) t -> p fc t", p=128)
    env["y_v"] = yT.rearrange("(dc p) t -> p dc t", p=128)
    env["dbg5"], env["dbg6"], env["dbg7"] = None, None, None
    env["dbg8"], env["dbg9"] = None, None
    env["st1_scr"] = st1_scr
    env["cs2_scr"] = cs2_scr
    env["Wo"] = Wo

    sy, ve, sc, te, gp = nc.sync, nc.vector, nc.scalar, nc.tensor, nc.gpsimd

    import contextlib
    _es = contextlib.ExitStack()
    if True:
        ident = _es.enter_context(nc.sbuf_tensor("ident", [128, 128], F32))
        mask = _es.enter_context(nc.sbuf_tensor("mask", [128, H, 128], F32))
        selden = _es.enter_context(nc.sbuf_tensor("selden", [128, DC, H], F32))
        selbc = _es.enter_context(nc.sbuf_tensor("selbc", [16, DC, 128], F32))
        ones1 = _es.enter_context(nc.sbuf_tensor("ones1", [1, 128], F32))
        onescol = _es.enter_context(nc.sbuf_tensor("onescol", [128, 1], F32))
        gate_s = _es.enter_context(nc.sbuf_tensor("gate_s", [128, 1], F32))
        gater_s = _es.enter_context(nc.sbuf_tensor("gater_s", [128, 1], F32))
        bq_s = _es.enter_context(nc.sbuf_tensor("bq_s", [128, 8], F32))
        bk_s = _es.enter_context(nc.sbuf_tensor("bk_s", [128, 8], F32))
        bo_s = _es.enter_context(nc.sbuf_tensor("bo_s", [128, 8], F32))
        bv_row = _es.enter_context(nc.sbuf_tensor("bv_row", [1, D], F32))
        cw_s = _es.enter_context(nc.sbuf_tensor("cw_s", [128, FC, 3], F32))
        cb_s = _es.enter_context(nc.sbuf_tensor("cb_s", [128, FC], F32))
        rsw1 = _es.enter_context(nc.sbuf_tensor("rsw1", [1, 1], F32))
        rsw2 = _es.enter_context(nc.sbuf_tensor("rsw2", [1, 1], F32))
        S_sb = _es.enter_context(nc.sbuf_tensor("S_sb", [128, DC, DH], F32))
        z_sb = _es.enter_context(nc.sbuf_tensor("z_sb", [128, DC], F32))
        zeros128 = _es.enter_context(nc.sbuf_tensor("zeros128", [128, 128], F32))
        tcol = _es.enter_context(nc.sbuf_tensor("tcol", [128, 4], F32))
        rwa = _es.enter_context(nc.sbuf_tensor("rwa", [1, 544], F32))
        rwb = _es.enter_context(nc.sbuf_tensor("rwb", [1, 544], F32))
        rwc = _es.enter_context(nc.sbuf_tensor("rwc", [1, 544], F32))
        rwd = _es.enter_context(nc.sbuf_tensor("rwd", [1, 544], F32))
        ps = _es.enter_context(nc.psum_tensor("ps", [128, 8, 512], F32))
        _s1 = _es.enter_context(nc.semaphore("si"))
        _s2 = _es.enter_context(nc.semaphore("so"))
        _s3 = _es.enter_context(nc.semaphore("sp"))
        _s4 = _es.enter_context(nc.semaphore("sv"))
        _s5 = _es.enter_context(nc.semaphore("ss"))
        _s6 = _es.enter_context(nc.semaphore("sg"))
        _blk = _es.enter_context(nc.Block())
    if True:
        for nm, h in [("si", _s1), ("so", _s2), ("sp", _s3), ("sv", _s4),
                      ("ss", _s5), ("sg", _s6)]:
            c.add_sem(nm, h)

        def dsem(name):
            if name not in c.sem:
                c.add_sem(name, _es.enter_context(nc.semaphore(name)))
            return name

        def load(dst, src, sem, pre=None):
            s = dsem(sem)
            c.wait(sy, pre)
            return c.inc(sy.dma_start(out=dst, in_=src), s, 16)

        def store(dst, src, sem, pre=None):
            s = dsem(sem)
            c.wait(gp, pre)
            return c.inc(gp.dma_start(out=dst, in_=src), s, 16)

        def full_barrier():
            if not USE_BARRIER:
                return
            for eng in (te, ve, sc, gp, sy):
                for s in list(c.sem):
                    c.wait(eng, c.tok(s))

        env["full_barrier"] = full_barrier
        env["dbg5"], env["dbg6"], env["dbg7"] = dbg5, dbg6, dbg7
        env["dbg8"], env["dbg9"] = dbg8, dbg9
        env.update(dict(ps=ps, ident=ident, mask=mask, selden=selden, selbc=selbc,
                        ones1=ones1, onescol=onescol, gate_s=gate_s, gater_s=gater_s,
                        bq_s=bq_s, bk_s=bk_s, bo_s=bo_s, bv_row=bv_row, cw_s=cw_s,
                        cb_s=cb_s, rsw1=rsw1, rsw2=rsw2, S_sb=S_sb, z_sb=z_sb,
                        zeros128=zeros128, tcol=tcol, rwa=rwa, rwb=rwb, rwc=rwc,
                        rwd=rwd, load=load, store=store))

        for dst, src in [(ident[:], id_i), (mask[:], mask_i), (selden[:], sd_i),
                         (selbc[:], sb_i), (ones1[:], o1_i), (onescol[:], oc_i),
                         (gate_s[:], gate), (gater_s[:], gater), (bq_s[:], bq_i),
                         (bk_s[:], bk_i), (bo_s[:], bo_i), (bv_row[:], bv_i),
                         (cw_s[:], cw_i), (cb_s[:], cb_i)]:
            load(dst, src, "Lconst")
        tok_consts = c.tok("Lconst")
        c.wait(ve, tok_consts)
        c.wait(te, tok_consts)
        c.wait(sc, tok_consts)
        c.wait(gp, tok_consts)
        ve.memset(zeros128[:], 0.0)
        ve.memset(S_sb[:], 0.0)
        c.inc(ve.memset(z_sb[:], 0.0), "sv")

        with (
            nc.sbuf_tensor("wstage", [128, 2, 4096], F32) as wstage,
            nc.sbuf_tensor("qy", [128, 2, 1024], F32) as qy,
            nc.sbuf_tensor("qti", [128, 2, 1024], I32) as qti,
            nc.sbuf_tensor("qa", [128, 2, 1024], F32) as qa,
            nc.sbuf_tensor("qb", [128, 2, 1024], F32) as qb,
            nc.sbuf_tensor("tern", [128, 2, 1024], F32) as tern,
            nc.sbuf_tensor("absac", [128, 40], F32) as absac,
            nc.sbuf_tensor("abscol", [128, 2], F32) as abscol,
            nc.sbuf_tensor("swcol", [128, 2], F32) as swcol,
        ):
            t_ws_slot = [None, None]
            _wlist = [] if SKIP_TERN else [(w1, 32, 1024), (w2, 8, 4096)]
            gi = 0
            for wi, (wt, ntile, tw) in enumerate(_wlist):
                for i in range(ntile):
                    sl = gi % 2
                    gi += 1
                    tl = load(wstage[:, sl, 0:tw], wt[i * 128:(i + 1) * 128, :],
                              f"Lws{sl}", pre=t_ws_slot[sl])
                    col = i if wi == 0 else 32 + i
                    c.wait(ve, tl)
                    t_ws_slot[sl] = c.inc(
                        ve.tensor_reduce(out=absac[:, col:col + 1], in_=wstage[:, sl, 0:tw],
                                         axis=AX.X, op=OP.add, apply_absolute_value=True),
                        "sv")
            t_ws_free = [t_ws_slot[0], t_ws_slot[1]]
            c.wait(ve, t_ws_free)
            ve.tensor_reduce(out=abscol[:, 0:1], in_=absac[:, 0:32], axis=AX.X, op=OP.add)
            t_red = c.inc(ve.tensor_reduce(out=abscol[:, 1:2], in_=absac[:, 32:40],
                                           axis=AX.X, op=OP.add), "sv")
            if SKIP_TERN:
                c.wait(ve, c.tok("Lconst"))
                ve.memset(rsw1[:], 1.0)
                c.inc(ve.memset(rsw2[:], 1.0), "sv")
            c.wait(te, t_red)
            t_mm = c.inc(te.matmul(ps[0:1, 0, 0:2], onescol[:], abscol[:],
                                   start=True, stop=True), "sp")
            c.wait(ve, t_mm)
            t_f = c.inc(ve.tensor_scalar(out=rwa[0:1, 0:2], in0=ps[0:1, 0, 0:2],
                                         scalar1=1.0 / (FF * D), scalar2=1e-5,
                                         op0=OP.mult, op1=OP.max), "sv")
            c.wait(ve, t_f)
            t_f = c.inc(ve.reciprocal(out=rwb[0:1, 0:2], in_=rwa[0:1, 0:2]), "sv")
            c.wait(ve, t_f)
            t_f = c.inc(ve.reciprocal(out=rwc[0:1, 0:2], in_=rwb[0:1, 0:2]), "sv")
            c.wait(ve, t_f)
            ve.tensor_copy(out=rsw1[:], in_=rwc[0:1, 0:1])
            t_rw = c.inc(ve.tensor_copy(out=rsw2[:], in_=rwc[0:1, 1:2]), "sv")
            c.wait(te, t_rw)
            t_mm = c.inc(te.matmul(ps[:, 0, 0:2], ones1[:], rwb[0:1, 0:2],
                                   start=True, stop=True), "sp")
            c.wait(ve, t_mm)
            t_sw = c.inc(ve.tensor_copy(out=swcol[:], in_=ps[:, 0, 0:2]), "sv")
            if DEBUG_SCRATCH:
                store(dbg1[:, 0:2], swcol[:], "Sdbg", pre=t_sw)
                store(dbg1[:, 2:4], abscol[:], "Sdbg", pre=t_red)
                store(dbg1[0:1, 4:6], rwa[0:1, 0:2], "Sdbg", pre=t_rw)
                store(dbg1[0:1, 6:8], rwb[0:1, 0:2], "Sdbg", pre=t_rw)

            t_spill = None
            t_ws_slot2 = [t_ws_free, t_ws_free]
            t_y_slot = [None, None]
            t_ti_slot = [None, None]
            t_tern_slot = [None, None]
            gi = 0
            for wi, (wt, ntile, tw) in enumerate(_wlist):
                tsb_shape = [128, DC, FF] if wi == 0 else [128, FC, D]
                with nc.sbuf_tensor(f"tts{wi}", tsb_shape, BF16) as tts:
                    c.wait(sc, t_spill)
                    tr_evac = [None] * 4
                    for i in range(ntile):
                        wl = gi % 2
                        gi += 1
                        tl = load(wstage[:, wl, 0:tw], wt[i * 128:(i + 1) * 128, :],
                                  f"Lws{wl}", pre=t_ws_slot2[wl])
                        for s4 in range(tw // 1024):
                            sl = s4 % 2
                            wsl = wstage[:, wl, s4 * 1024:(s4 + 1) * 1024]
                            c.wait(ve, [tl, t_sw, t_ti_slot[sl], t_tern_slot[sl]])
                            t_y = c.inc(ve.tensor_single_scalar(
                                out=qy[:, sl, :], in_=wsl, scalar=swcol[:, wi:wi + 1],
                                op=OP.mult), "sv")
                            t_y_slot[sl] = t_y
                            c.wait(sc, [t_y, t_tern_slot[sl]])
                            t_ti = c.inc(sc.activation(out=qti[:, sl, :], in_=qy[:, sl, :],
                                                       func=AF.Identity), "ss")
                            t_ti_slot[sl] = t_ti
                            c.wait(ve, t_ti)
                            t_a = c.inc(ve.scalar_tensor_tensor(
                                out=qa[:, sl, :], in0=qti[:, sl, :], scalar=0.5,
                                in1=qy[:, sl, :], op0=OP.add, op1=OP.is_le), "sv")
                            ve.scalar_tensor_tensor(
                                out=qb[:, sl, :], in0=qti[:, sl, :], scalar=0.5,
                                in1=qy[:, sl, :], op0=OP.subtract, op1=OP.is_gt)
                            ve.scalar_tensor_tensor(out=qa[:, sl, :], in0=qb[:, sl, :],
                                                    scalar=-1.0, in1=qa[:, sl, :],
                                                    op0=OP.mult, op1=OP.add)
                            ve.tensor_tensor(out=qa[:, sl, :], in0=qa[:, sl, :],
                                             in1=qti[:, sl, :], op=OP.add)
                            t_tern = c.inc(ve.tensor_scalar(out=tern[:, sl, :],
                                                            in0=qa[:, sl, :],
                                                            scalar1=-1.0, scalar2=1.0,
                                                            op0=OP.max, op1=OP.min), "sv")
                            if DEBUG_SCRATCH and wi == 0 and i == 0:
                                store(dbg2[:], tern[:, sl, :], "Sdbg", pre=t_tern)
                            if DEBUG_SCRATCH and wi == 0 and i == 31:
                                store(dbg3[:], tern[:, sl, :], "Sdbg", pre=t_tern)
                                store(dbg4[:], qy[:, sl, :], "Sdbg", pre=t_y)
                            for dcb in range(8):
                                qb_ = dcb % 4
                                c.wait(te, [t_tern, tr_evac[qb_]])
                                t_tr = c.inc(te.transpose(
                                    ps[:, 4 + qb_, 0:128],
                                    tern[:, sl, dcb * 128:(dcb + 1) * 128], ident[:]), "sp")
                                c.wait(sc, t_tr)
                                if wi == 0:
                                    dst = tts[:, dcb, i * 128:(i + 1) * 128]
                                else:
                                    dst = tts[:, s4 * 8 + dcb, i * 128:(i + 1) * 128]
                                tr_evac[qb_] = c.inc(sc.copy(out=dst,
                                                             in_=ps[:, 4 + qb_, 0:128]), "ss")
                            t_tern_slot[sl] = c.tok("sp")
                        t_ws_slot2[wl] = c.tok("sv")
                    t_spill = store(env["t1t_v"][:, :, :] if wi == 0 else env["t2t_v"][:, :, :],
                                    tts[:], f"Stts{wi}", pre=[c.tok("ss")])

        full_barrier()
        with (
            nc.sbuf_tensor("WqTh", [128, DC, D], BF16) as WqTh,
            nc.sbuf_tensor("WqTl", [128, DC, D], BF16) as WqTl,
            nc.sbuf_tensor("WkTh", [128, DC, D], BF16) as WkTh,
            nc.sbuf_tensor("WkTl", [128, DC, D], BF16) as WkTl,
            nc.sbuf_tensor("WvTh", [128, DC, D], BF16) as WvTh,
            nc.sbuf_tensor("WvTl", [128, DC, D], BF16) as WvTl,
        ):
            with nc.sbuf_tensor("wstg2", [128, 2, 1024], F32) as wstg2:
                t_stg_free = [None, None]
                tr_evac = [None] * 4
                mi_ = 0
                for wt, wdh, wdl in ([] if SKIP_WT else
                                     [(Wq, WqTh, WqTl), (Wk, WkTh, WkTl),
                                      (Wv, WvTh, WvTl)]):
                    for m in range(8):
                        sl_ = mi_ % 2
                        mi_ += 1
                        tl = load(wstg2[:, sl_, :], wt[m * 128:(m + 1) * 128, :],
                                  f"Lws2{sl_}", pre=t_stg_free[sl_])
                        for kc in range(8):
                            qb_ = kc % 4
                            c.wait(te, [tl, tr_evac[qb_]])
                            t_tr = c.inc(te.transpose(ps[:, 4 + qb_, 0:128],
                                                      wstg2[:, sl_, kc * 128:(kc + 1) * 128],
                                                      ident[:]), "sp")
                            c.wait(sc, t_tr)
                            t_hi = c.inc(sc.copy(out=wdh[:, kc, m * 128:(m + 1) * 128],
                                                 in_=ps[:, 4 + qb_, 0:128]), "ss")
                            c.wait(ve, t_hi)
                            tr_evac[qb_] = c.inc(ve.tensor_tensor(
                                out=wdl[:, kc, m * 128:(m + 1) * 128],
                                in0=ps[:, 4 + qb_, 0:128],
                                in1=wdh[:, kc, m * 128:(m + 1) * 128],
                                op=OP.subtract), "sv")
                        t_stg_free[sl_] = c.tok("sp")
            env.update(dict(WqTh=WqTh, WqTl=WqTl, WkTh=WkTh, WkTl=WkTl,
                            WvTh=WvTh, WvTl=WvTl))
            env["tok_wT"] = c.tok("sv")

            if N_PHASES >= 2 and 2 not in SKIP_PHASES:
                emit_a1(nc, c, env)

        if N_PHASES >= 3 and 3 not in SKIP_PHASES:
            emit_a2(nc, c, env)
        if N_PHASES >= 4 and 4 not in SKIP_PHASES:
            emit_f1(nc, c, env)
        if N_PHASES >= 5 and 5 not in SKIP_PHASES:
            emit_f2(nc, c, env)

        for s in list(c.sem):
            c.wait(sy, c.tok(s))

    _es.close()
    return nc


def emit_a1(nc, c, env):
    sy, ve, sc, te, gp = nc.sync, nc.vector, nc.scalar, nc.tensor, nc.gpsimd
    ps = env["ps"]
    WqTh, WqTl = env["WqTh"], env["WqTl"]
    WkTh, WkTl = env["WkTh"], env["WkTl"]
    WvTh, WvTl = env["WvTh"], env["WvTl"]
    xw_v, xp_v, att_v = env["xw_v"], env["xp_v"], env["att_v"]
    bq_s, bk_s, bv_row = env["bq_s"], env["bk_s"], env["bv_row"]
    ones1, ident, mask = env["ones1"], env["ident"], env["mask"]
    selden, selbc = env["selden"], env["selbc"]
    gate_s, zeros128 = env["gate_s"], env["zeros128"]
    S_sb, z_sb = env["S_sb"], env["z_sb"]
    load, store = env["load"], env["store"]
    env["full_barrier"]()

    with (
        nc.sbuf_tensor("xin", [128, DC, 256], F32) as xin,
        nc.sbuf_tensor("qf", [128, DC, 256], F32) as qf,
        nc.sbuf_tensor("kf", [128, DC, 256], F32) as kf,
        nc.sbuf_tensor("vtm", [128, 2, D], F32) as vtm,
        nc.sbuf_tensor("ktm", [128, DC, 128], F32) as ktm,
        nc.sbuf_tensor("kcum", [128, DC, 128], F32) as kcum,
        nc.sbuf_tensor("pbuf", [128, DC, 128], F32) as pbuf,
        nc.sbuf_tensor("ssb", [128, H, 128], F32) as ssb,
        nc.sbuf_tensor("denr", [16, 128], F32) as denr,
        nc.sbuf_tensor("dbc_sb", [128, DC, 128], F32) as dbc_sb,
        nc.sbuf_tensor("dinv", [16, 128], F32) as dinv,
        nc.sbuf_tensor("outst", [128, 2, DC, 128], F32) as outst,
        nc.sbuf_tensor("ph1", [128, 2, 256], F32) as ph1,
        nc.sbuf_tensor("ph2", [128, 2, 256], F32) as ph2,
        nc.sbuf_tensor("ph3", [128, 2, 256], F32) as ph3,
        nc.sbuf_tensor("ztmp", [128, DC], F32) as ztmp,
        nc.sbuf_tensor("xhh", [128, DC, 256], BF16) as xh,
        nc.sbuf_tensor("xll", [128, DC, 256], BF16) as xl,
    ):
        st = {
            "xin_free": None,
            "xh_free": None,
            "b07_free": None,
            "b45_free": None,
            "b6_free": None,
            "S_started": [False, False],
            "phi_free": [None, None],
            "outst_free": [None, None],
            "vtm_free": [None, None],
            "S_sb_ok": None,
            "ssb_free": None,
            "kcum_free": None,
            "dbc_free": None,
            "pbuf_free": None,
            "z_ok": None,
            "sc03_free": None,
        }

        def cast_hl(xbuf, tw, t_x):
            c.wait(sc, [t_x, st["xh_free"]])
            t_xh = c.inc(sc.copy(out=xh[:, :, 0:tw], in_=xbuf[:, :, 0:tw]), "ss")
            c.wait(ve, t_xh)
            return c.inc(ve.tensor_tensor(out=xl[:, :, 0:tw], in0=xbuf[:, :, 0:tw],
                                          in1=xh[:, :, 0:tw], op=OP.subtract), "sv")

        def proj_qk(wTh, wTl, bias, dstf, tw, t_x):
            t_e1 = None
            for m in range(8):
                reg = ps[:, m, 0:tw]
                for kc in range(8):
                    if kc == 0 and m == 0:
                        c.wait(te, [t_x, st["b07_free"], st["sc03_free"],
                                    st["b45_free"], st["b6_free"], st["S_sb_ok"]])
                    te.matmul(reg, wTh[:, kc, m * 128:(m + 1) * 128],
                              xh[:, kc, 0:tw], start=(kc == 0), stop=False)
                    te.matmul(reg, wTh[:, kc, m * 128:(m + 1) * 128],
                              xl[:, kc, 0:tw], start=False, stop=False)
                    t_mm = te.matmul(reg, wTl[:, kc, m * 128:(m + 1) * 128],
                                     xh[:, kc, 0:tw], start=False, stop=(kc == 7))
                t_mm = c.inc(t_mm, "sp")
                s = m % 2
                c.wait(sc, [t_mm, st["phi_free"][s]])
                t_e1 = c.inc(sc.activation(out=ph1[:, s, 0:tw], in_=reg, func=AF.Identity,
                                           bias=bias[:, m:m + 1]), "ss")
                c.wait(gp, t_e1)
                t_mn = c.inc(gp.tensor_single_scalar(out=ph2[:, s, 0:tw],
                                                     in_=ph1[:, s, 0:tw],
                                                     scalar=0.0, op=OP.min), "sg")
                c.wait(sc, t_mn)
                t_ex = c.inc(sc.activation(out=ph3[:, s, 0:tw], in_=ph2[:, s, 0:tw],
                                           func=AF.Exp), "ss")
                c.wait(ve, t_ex)
                st["phi_free"][s] = c.inc(
                    ve.scalar_tensor_tensor(out=dstf[:, m, 0:tw], in0=ph1[:, s, 0:tw],
                                            scalar=0.0, in1=ph3[:, s, 0:tw],
                                            op0=OP.max, op1=OP.add), "sv")
            t_all_evac = c.tok("ss")
            st["b07_free"] = t_all_evac
            st["sc03_free"] = t_all_evac
            st["b45_free"] = t_all_evac
            st["b6_free"] = t_all_evac
            return c.tok("sv")

        def proj_v(cc, t_x):
            for half in range(2):
                reg = ps[:, 4 + half, 0:512]
                for kc in range(8):
                    if kc == 0 and half == 0:
                        c.wait(te, [t_x, st["b45_free"]])
                    te.matmul(reg, xh[:, kc, cc * 128:(cc + 1) * 128],
                              WvTh[:, kc, half * 512:(half + 1) * 512],
                              start=(kc == 0), stop=False)
                    te.matmul(reg, xl[:, kc, cc * 128:(cc + 1) * 128],
                              WvTh[:, kc, half * 512:(half + 1) * 512],
                              start=False, stop=False)
                    te.matmul(reg, xh[:, kc, cc * 128:(cc + 1) * 128],
                              WvTl[:, kc, half * 512:(half + 1) * 512],
                              start=False, stop=False)
                t_mm = c.inc(te.matmul(reg, ones1[0:1, 0:128],
                                       bv_row[0:1, half * 512:(half + 1) * 512],
                                       start=False, stop=True), "sp")
            c.wait(sc, [t_mm, st["vtm_free"][cc]])
            t_ev = c.inc(sc.copy(out=vtm[:, cc, :], in_=ps[:, 4:6, 0:512]), "ss")
            st["b45_free"] = t_ev
            return t_ev

        def ktm_make(cc, t_k):
            tr_ev = [None, None]
            for dc in range(8):
                bnk = 4 + (dc % 2)
                c.wait(te, [t_k, st["b45_free"]] if dc == 0 else tr_ev[dc % 2])
                t_tr = c.inc(te.transpose(ps[:, bnk, 0:128],
                                          kf[:, dc, cc * 128:(cc + 1) * 128], ident[:]), "sp")
                c.wait(sc, t_tr)
                tr_ev[dc % 2] = c.inc(sc.copy(out=ktm[:, dc, :],
                                              in_=ps[:, bnk, 0:128]), "ss")
            st["b45_free"] = c.tok("ss")
            return c.tok("ss")

        def s_update(cc, t_ktm, t_v, t_num):
            for hi, h in enumerate(list(range(0, H, 2)) + list(range(1, H, 2))):
                p0 = (h % 2) * 64
                dc = h // 2
                if hi == 0:
                    c.wait(te, [t_ktm, t_v, st["S_sb_ok"]])
                t_mm = te.matmul(ps[p0:p0 + 64, 7, dc * 64:(dc + 1) * 64],
                                 ktm[:, dc, (h % 2) * 64:(h % 2) * 64 + 64],
                                 vtm[:, cc, h * 64:(h + 1) * 64],
                                 start=(h <= 1), stop=(hi % 8 == 7),
                                 tile_position=(0, p0))
            t_su = c.inc(t_mm, "sp")
            st["vtm_free"][cc] = t_su
            sflat = bass.AP(S_sb, 0, [[512, 128], [1, 512]])
            c.wait(ve, [t_su, t_num])
            st["S_sb_ok"] = c.inc(ve.tensor_tensor(out=sflat, in0=sflat,
                                                   in1=ps[:, 7, 0:512], op=OP.add), "sv")
            return st["S_sb_ok"]

        def scan_chunk(g, cc, is_last_main, t_q, t_k, t_ktm, t_v):
            for hi, h in enumerate(list(range(0, H, 2)) + list(range(1, H, 2))):
                p0 = (h % 2) * 64
                dc = h // 2
                if hi == 0:
                    c.wait(te, [t_q, t_k, st["sc03_free"], st["b07_free"]])
                t_mm = te.matmul(ps[:, h // 4, (h % 4) * 128:(h % 4 + 1) * 128],
                                 kf[p0:p0 + 64, dc, cc * 128:(cc + 1) * 128],
                                 qf[p0:p0 + 64, dc, cc * 128:(cc + 1) * 128],
                                 start=(h % 4 == 0), stop=(h % 4 == 3))
            t_sc = c.inc(t_mm, "sp")
            c.wait(ve, [t_k, st["kcum_free"], st["z_ok"]])
            t_scan = None
            for dc in range(8):
                t_scan = ve.tensor_tensor_scan(out=kcum[:, dc, :],
                                               data0=kf[:, dc, cc * 128:(cc + 1) * 128],
                                               data1=zeros128[:], initial=z_sb[:, dc:dc + 1],
                                               op0=OP.add, op1=OP.add)
            t_scan = c.inc(t_scan, "sv")
            c.wait(ve, [t_scan, t_q, st["pbuf_free"]])
            t_p0 = c.inc(ve.tensor_tensor(out=pbuf[:], in0=qf[:, :, cc * 128:(cc + 1) * 128],
                                          in1=kcum[:], op=OP.mult), "sv")
            c.wait(ve, t_p0)
            t_z = c.inc(ve.tensor_copy(out=bass.AP(z_sb, 0, [[8, 128], [1, 8], [1, 1]]),
                                       in_=kcum[:, :, 127:128]), "sv")
            st["z_ok"] = t_z
            t_p = t_z
            for dc in range(8):
                if dc == 0:
                    c.wait(te, [t_p, st["b6_free"]])
                t_mm = te.matmul(ps[0:16, 6, 0:128], selden[:, dc, :], pbuf[:, dc, :],
                                 start=(dc == 0), stop=(dc == 7))
            t_den = c.inc(t_mm, "sp")
            st["kcum_free"] = t_scan
            st["pbuf_free"] = t_den
            if DEBUG_SCRATCH and g == 0:
                st["pbuf_free"] = store(env["dbg5"], pbuf[:], "Sdbg", pre=[t_p, t_den])
            c.wait(ve, t_den)
            t_clamp = c.inc(ve.tensor_single_scalar(out=denr[:], in_=ps[0:16, 6, 0:128],
                                                    scalar=1e-30, op=OP.max), "sv")
            c.wait(ve, t_clamp)
            t_r = c.inc(ve.reciprocal(out=dinv[:], in_=denr[:]), "sv")
            c.wait(ve, t_r)
            t_dinv = c.inc(ve.tensor_copy(out=denr[:], in_=dinv[:]), "sv")
            st["b6_free"] = t_dinv
            if DEBUG_SCRATCH and g == 0:
                store(env["dbg6"][:, 0, :], denr[:], "Sdbg", pre=t_dinv)
                store(env["dbg6"][:, 1, :], dinv[:], "Sdbg", pre=t_dinv)
            for dc in range(8):
                if dc == 0:
                    c.wait(te, [t_dinv, st["b45_free"]])
                t_mm = te.matmul(ps[:, 4 + dc // 4, (dc % 4) * 128:(dc % 4 + 1) * 128],
                                 selbc[:, dc, :], dinv[:],
                                 start=(dc % 4 == 0), stop=(dc % 4 == 3))
            t_bc = c.inc(t_mm, "sp")
            c.wait(sc, [t_bc, st["dbc_free"]])
            t_dbc = c.inc(sc.copy(out=dbc_sb[:],
                                  in_=bass.AP(ps, 4 * 512, [[4096, 128], [128, 8], [1, 128]])), "ss")
            st["b45_free"] = t_dbc
            c.wait(ve, [t_sc, st["ssb_free"]])
            t_ssb = c.inc(ve.tensor_tensor(
                out=ssb[:], in0=bass.AP(ps, 0, [[4096, 128], [128, 16], [1, 128]]),
                in1=mask[:], op=OP.mult), "sv")
            st["sc03_free"] = t_ssb
            if DEBUG_SCRATCH and g == 0:
                st["ssb_free"] = store(env["dbg7"], ssb[:], "Sdbg", pre=t_ssb)
            for hi, h in enumerate(list(range(0, H, 2)) + list(range(1, H, 2))):
                p0 = (h % 2) * 64
                dc = h // 2
                if hi == 0:
                    c.wait(te, [t_ssb, st["S_sb_ok"], st["b45_free"]])
                te.matmul(ps[p0:p0 + 64, 4 + dc // 4, (dc % 4) * 128:(dc % 4 + 1) * 128],
                          vtm[:, cc, h * 64:(h + 1) * 64], ssb[:, h, :],
                          start=(h % 8 <= 1), stop=False, tile_position=(0, p0))
                t_mm = te.matmul(ps[p0:p0 + 64, 4 + dc // 4, (dc % 4) * 128:(dc % 4 + 1) * 128],
                                 S_sb[p0:p0 + 64, dc, :],
                                 qf[p0:p0 + 64, dc, cc * 128:(cc + 1) * 128],
                                 start=False, stop=(h % 8 >= 6), tile_position=(p0, p0))
            t_num = c.inc(t_mm, "sp")
            st["ssb_free"] = t_num
            sl = g % 2
            c.wait(ve, [t_num, t_dbc, st["outst_free"][sl]])
            t_oev = c.inc(ve.tensor_tensor(
                out=outst[:, sl, :, :],
                in0=bass.AP(ps, 4 * 512, [[4096, 128], [128, 8], [1, 128]]),
                in1=dbc_sb[:], op=OP.mult), "sv")
            st["b45_free"] = t_oev
            st["dbc_free"] = t_oev
            st["outst_free"][sl] = store(att_v[:, :, g * 128:(g + 1) * 128],
                                         outst[:, sl, :, :], f"Satt{sl}", pre=t_oev)
            s_update(cc, t_ktm, t_v, t_num)

        pf_tiles = ([(i * 256, 256) for i in range(15)] + [(3840, 128)])[:A1_PF_TILES]
        t_su = None
        for ti_, (t0, tw) in enumerate(pf_tiles):
            b = ti_ % 2
            tl = load(xin[:, :, 0:tw], xp_v[:, :, t0:t0 + tw], f"Lxin{b}", pre=st["xin_free"])
            t_hl = cast_hl(xin, tw, tl)
            t_k = proj_qk(WkTh, WkTl, bk_s, kf, tw, t_hl)
            if A1_STAGES & 2:
                c.wait(ve, t_k)
                t_k = c.inc(ve.tensor_single_scalar(out=kf[:, :, 0:tw], in_=kf[:, :, 0:tw],
                                                    scalar=gate_s[:, 0:1], op=OP.mult), "sv")
                c.wait(ve, t_k)
                t_zr = c.inc(ve.tensor_reduce(out=ztmp[:], in_=kf[:, :, 0:tw],
                                              axis=AX.X, op=OP.add), "sv")
                c.wait(ve, t_zr)
                c.inc(ve.tensor_tensor(out=z_sb[:], in0=z_sb[:], in1=ztmp[:], op=OP.add), "sv")
            t_vs = [proj_v(cc, t_hl) if A1_STAGES & 4 else None
                    for cc in range(tw // 128)]
            for cc in range(tw // 128):
                t_ktm = ktm_make(cc, t_k) if A1_STAGES & 8 else None
                if A1_STAGES & 16:
                    t_su = s_update(cc, t_ktm, t_vs[cc], None)
            st["xin_free"] = t_hl
            st["xh_free"] = c.tok("sp")

        for ti_ in range(A1_MAIN_TILES):
            b = ti_ % 2
            t0 = ti_ * 256
            tl = load(xin[:, :, 0:256], xw_v[:, :, t0:t0 + 256], f"Lxin{b}", pre=st["xin_free"])
            if A1_STAGES & 128:
                c.wait(ve, tl)
                ve.tensor_single_scalar(out=qf[:], in_=xin[:, :, 0:256], scalar=1.0, op=OP.max)
                t_q = t_k = c.inc(ve.tensor_single_scalar(out=kf[:], in_=xin[:, :, 0:256],
                                                          scalar=1.0, op=OP.max), "sv")
            else:
                t_hl = cast_hl(xin, 256, tl)
                t_q = proj_qk(WqTh, WqTl, bq_s, qf, 256, t_hl)
                t_k = proj_qk(WkTh, WkTl, bk_s, kf, 256, t_hl)
            if ti_ == 0:
                c.wait(ve, [t_q, t_k])
                ve.tensor_single_scalar(out=qf[:, :, 0:128], in_=qf[:, :, 0:128],
                                        scalar=gate_s[:, 0:1], op=OP.mult)
                t_q = t_k = c.inc(
                    ve.tensor_single_scalar(out=kf[:, :, 0:128], in_=kf[:, :, 0:128],
                                            scalar=gate_s[:, 0:1], op=OP.mult), "sv")
            ncc = 1 if A1_SCAN & 65536 else 2
            t_vs = [proj_v(cc, t_hl) for cc in range(ncc)]
            for cc in range(ncc):
                g = ti_ * 2 + cc
                t_ktm = ktm_make(cc, t_k)
                scan_chunk(g, cc, g == NWCH - 1, t_q, t_k, t_ktm, t_vs[cc])
            st["xin_free"] = t_hl
            st["xh_free"] = c.tok("sp")


def row_absmax(nc, c, env, srcap, tw, t_src, bank_t, bank_r, pre_te):
    sy, ve, sc, te, gp = nc.sync, nc.vector, nc.scalar, nc.tensor, nc.gpsimd
    ps, ident, tcol = env["ps"], env["ident"], env["tcol"]
    rwa = env["rwa"]
    nblk = tw // 128
    t_red = None
    for j in range(nblk):
        c.wait(te, [t_src, pre_te] if j == 0 else t_red)
        t_tr = c.inc(te.transpose(ps[:, bank_t, 0:128], srcap[:, j * 128:(j + 1) * 128],
                                  ident[:]), "sp")
        c.wait(ve, t_tr)
        t_red = c.inc(ve.tensor_reduce(out=tcol[:, j:j + 1], in_=ps[:, bank_t, 0:128],
                                       axis=AX.X, op=OP.max), "sv")
    for j in range(nblk):
        c.wait(te, t_red if j == 0 else None)
        t_tr2 = c.inc(te.transpose(ps[0:1, bank_r, j * 128:(j + 1) * 128],
                                   tcol[:, j:j + 1], ident[:]), "sp")
    c.wait(ve, t_tr2)
    t_row = c.inc(ve.tensor_single_scalar(out=rwa[0:1, 0:tw], in_=ps[0:1, bank_r, 0:tw],
                                          scalar=1e-5, op=OP.max), "sv")
    return t_row


def emit_a2(nc, c, env):
    sy, ve, sc, te, gp = nc.sync, nc.vector, nc.scalar, nc.tensor, nc.gpsimd
    ps = env["ps"]
    Wo, att_v, xw_v = env["Wo"], env["att_v"], env["xw_v"]
    x2_v, xq1_v, st1_scr = env["x2_v"], env["xq1_v"], env["st1_scr"]
    bo_s, ones1, onescol, ident = env["bo_s"], env["ones1"], env["onescol"], env["ident"]
    gate_s, gater_s = env["gate_s"], env["gater_s"]
    rwa, rwb, rwc, rwd = env["rwa"], env["rwb"], env["rwc"], env["rwd"]
    load, store = env["load"], env["store"]
    env["full_barrier"]()

    with (
        nc.sbuf_tensor("WoTh", [128, DC, D], BF16) as WoTh,
        nc.sbuf_tensor("WoTl", [128, DC, D], BF16) as WoTl,
    ):
        with nc.sbuf_tensor("wstg3", [128, 2, 1024], F32) as wstg3:
            t_stg_free = [None, None]
            tr_evac = [None] * 4
            for m in range(8):
                sl_ = m % 2
                tl = load(wstg3[:, sl_, :], Wo[m * 128:(m + 1) * 128, :],
                          f"Lws3{sl_}", pre=t_stg_free[sl_])
                for kc in range(8):
                    qb_ = kc % 4
                    c.wait(te, [tl, tr_evac[qb_]])
                    t_tr = c.inc(te.transpose(ps[:, 4 + qb_, 0:128],
                                              wstg3[:, sl_, kc * 128:(kc + 1) * 128], ident[:]), "sp")
                    c.wait(sc, t_tr)
                    t_hi = c.inc(sc.copy(out=WoTh[:, kc, m * 128:(m + 1) * 128],
                                         in_=ps[:, 4 + qb_, 0:128]), "ss")
                    c.wait(ve, t_hi)
                    tr_evac[qb_] = c.inc(ve.tensor_tensor(
                        out=WoTl[:, kc, m * 128:(m + 1) * 128],
                        in0=ps[:, 4 + qb_, 0:128],
                        in1=WoTh[:, kc, m * 128:(m + 1) * 128],
                        op=OP.subtract), "sv")
                t_stg_free[sl_] = c.tok("sp")
        t_woT = c.tok("sv")

        import contextlib
        _es2 = contextlib.ExitStack()
        ath = _es2.enter_context(nc.sbuf_tensor("ath", [128, DC, 512], BF16))
        atl = _es2.enter_context(nc.sbuf_tensor("atl", [128, DC, 512], BF16))
        attl = _es2.enter_context(nc.sbuf_tensor("attl", [128, DC, 512], F32))
        xl = _es2.enter_context(nc.sbuf_tensor("xl", [128, DC, 512], F32))
        x2b = _es2.enter_context(nc.sbuf_tensor("x2b", [128, DC, 512], F32))
        sqb = _es2.enter_context(nc.sbuf_tensor("sqb", [128, 2, 512], F32))
        yb = _es2.enter_context(nc.sbuf_tensor("yb", [128, DC, 512], F32))
        tib = _es2.enter_context(nc.sbuf_tensor("tib", [128, DC, 512], I32))
        ab = _es2.enter_context(nc.sbuf_tensor("ab", [128, DC, 512], F32))
        bb = _es2.enter_context(nc.sbuf_tensor("bb", [128, DC, 512], F32))
        xq1b = _es2.enter_context(nc.sbuf_tensor("xq1b", [128, DC, 512], BF16))
        qt1 = _es2.enter_context(nc.sbuf_tensor("qt1", [128, 1, 512], F32))
        qt2 = _es2.enter_context(nc.sbuf_tensor("qt2", [128, 1, 512], F32))

        tiles = [(i * 512, 512) for i in range(8)] + [(4096, 256)]
        t_ps_free = t_woT
        t_ath_free = None
        t_bufs_free = None
        t_round_prev = None
        t_comb_prev = None
        t_st1w_prev = None
        t_al_prev = None
        for ti_, (t0, tw) in enumerate(tiles):
            tl1 = load(attl[:, :, 0:tw], att_v[:, :, t0:t0 + tw], "Lattl", pre=t_al_prev)
            tl2 = load(xl[:, :, 0:tw], xw_v[:, :, t0:t0 + tw], "Lxl", pre=t_bufs_free)
            c.wait(sc, [tl1, t_ath_free])
            t_ah = c.inc(sc.copy(out=ath[:, :, 0:tw], in_=attl[:, :, 0:tw]), "ss")
            c.wait(ve, t_ah)
            t_al = c.inc(ve.tensor_tensor(out=atl[:, :, 0:tw], in0=attl[:, :, 0:tw],
                                          in1=ath[:, :, 0:tw], op=OP.subtract), "sv")
            t_al_prev = t_al
            for m in range(8):
                for kc in range(8):
                    if m == 0 and kc == 0:
                        c.wait(te, [t_al, t_ps_free])
                    te.matmul(ps[:, m, 0:tw], WoTh[:, kc, m * 128:(m + 1) * 128],
                              ath[:, kc, 0:tw], start=(kc == 0), stop=False)
                    te.matmul(ps[:, m, 0:tw], WoTh[:, kc, m * 128:(m + 1) * 128],
                              atl[:, kc, 0:tw], start=False, stop=False)
                    t_mm = te.matmul(ps[:, m, 0:tw], WoTl[:, kc, m * 128:(m + 1) * 128],
                                     ath[:, kc, 0:tw], start=False, stop=(kc == 7))
            t_mm = c.inc(t_mm, "sp")
            t_ath_free = c.tok("sp")
            psw = bass.AP(ps, 0, [[4096, 128], [512, 8], [1, tw]])
            c.wait(ve, [t_mm, tl2, t_bufs_free])
            ve.tensor_tensor(out=x2b[:, :, 0:tw], in0=psw, in1=xl[:, :, 0:tw], op=OP.add)
            t_r1 = c.inc(ve.tensor_tensor(
                out=x2b[:, :, 0:tw], in0=x2b[:, :, 0:tw],
                in1=bass.AP(bo_s, 0, [[8, 128], [1, 8], [0, tw]]), op=OP.add), "sv")
            t_sq = [None, None]
            for dc in range(8):
                sl = dc % 2
                c.wait(sc, [t_r1, t_sq[sl]])
                t_s = c.inc(sc.activation(out=sqb[:, sl, 0:tw], in_=x2b[:, dc, 0:tw],
                                          func=AF.Square), "ss")
                c.wait(te, [t_r1, t_s])
                te.matmul(ps[0:1, 0, 0:tw], onescol[:], x2b[:, dc, 0:tw],
                          start=(dc == 0), stop=(dc == 7))
                t_sq[sl] = c.inc(te.matmul(ps[0:1, 1, 0:tw], onescol[:], sqb[:, sl, 0:tw],
                                           start=(dc == 0), stop=(dc == 7)), "sp")
            c.wait(ve, t_sq[1])
            ve.tensor_single_scalar(out=rwa[0:1, 0:tw], in_=ps[0:1, 0, 0:tw],
                                    scalar=1.0 / D, op=OP.mult)
            ve.tensor_single_scalar(out=rwb[0:1, 0:tw], in_=ps[0:1, 1, 0:tw],
                                    scalar=1.0 / D, op=OP.mult)
            ve.tensor_tensor(out=rwc[0:1, 0:tw], in0=rwa[0:1, 0:tw], in1=rwa[0:1, 0:tw],
                             op=OP.mult)
            t_var = c.inc(ve.tensor_tensor(out=rwb[0:1, 0:tw], in0=rwb[0:1, 0:tw],
                                           in1=rwc[0:1, 0:tw], op=OP.subtract), "sv")
            c.wait(sc, t_var)
            t_std = c.inc(sc.activation(out=rwb[0:1, 0:tw], in_=rwb[0:1, 0:tw],
                                        func=AF.Sqrt, bias=np.float32(1e-5)), "ss")
            c.wait(ve, t_std)
            t_rstd = c.inc(ve.reciprocal(out=rwc[0:1, 0:tw], in_=rwb[0:1, 0:tw]), "sv")
            c.wait(te, [t_rstd, t_r1])
            te.matmul(ps[:, 2, 0:tw], ones1[:], rwa[0:1, 0:tw], start=True, stop=True)
            t_bc = c.inc(te.matmul(ps[:, 3, 0:tw], ones1[:], rwc[0:1, 0:tw],
                                   start=True, stop=True), "sp")
            c.wait(ve, t_bc)
            ve.tensor_tensor(out=x2b[:, :, 0:tw], in0=x2b[:, :, 0:tw],
                             in1=bass.AP(ps, 2 * 512, [[4096, 128], [0, 8], [1, tw]]),
                             op=OP.subtract)
            t_x2 = c.inc(ve.tensor_tensor(out=x2b[:, :, 0:tw], in0=x2b[:, :, 0:tw],
                                          in1=bass.AP(ps, 3 * 512, [[4096, 128], [0, 8], [1, tw]]),
                                          op=OP.mult), "sv")
            t_st_x2 = store(x2_v[:, :, t0:t0 + tw], x2b[:, :, 0:tw], "Sx2", pre=t_x2)

            x2v = bass.AP(x2b, 0, [[8 * 512, 128], [1, tw], [512, 8]])
            c.wait(ve, [t_x2, t_comb_prev])
            t_rd = c.inc(ve.tensor_reduce(out=qt1[:, 0, 0:tw], in_=x2v, op=OP.max,
                                          axis=mybir.AxisListType.X), "sv")
            c.wait(ve, t_rd)
            t_rd2 = c.inc(ve.tensor_reduce(out=qt2[:, 0, 0:tw], in_=x2v, op=OP.min,
                                           axis=mybir.AxisListType.X), "sv")
            c.wait(ve, t_rd2)
            ve.tensor_single_scalar(out=qt2[:, 0, 0:tw], in_=qt2[:, 0, 0:tw],
                                    scalar=-1.0, op=OP.mult)
            t_am = c.inc(ve.tensor_tensor(out=qt1[:, 0, 0:tw], in0=qt1[:, 0, 0:tw],
                                          in1=qt2[:, 0, 0:tw], op=OP.max), "sv")
            t_comb_prev = t_am
            t_row = row_absmax(nc, c, env, qt1[:, 0, :], tw, t_am, 4, 5, t_x2)
            c.wait(ve, [t_row, t_st1w_prev])
            t_rc = c.inc(ve.reciprocal(out=rwb[0:1, 0:tw], in_=rwa[0:1, 0:tw]), "sv")
            c.wait(ve, t_rc)
            t_st1 = c.inc(ve.tensor_single_scalar(out=rwd[0:1, 0:tw], in_=rwb[0:1, 0:tw],
                                                  scalar=127.0, op=OP.mult), "sv")
            t_st1w_prev = store(st1_scr[0:1, t0:t0 + tw], rwd[0:1, 0:tw], "Sst1", pre=t_st1)
            c.wait(te, [t_st1, t_x2])
            t_bc2 = c.inc(te.matmul(ps[:, 2, 0:tw], ones1[:], rwd[0:1, 0:tw],
                                    start=True, stop=True), "sp")
            c.wait(ve, [t_bc2, t_st_x2])
            t_y = c.inc(ve.tensor_tensor(out=yb[:, :, 0:tw], in0=x2b[:, :, 0:tw],
                                         in1=bass.AP(ps, 2 * 512, [[4096, 128], [0, 8], [1, tw]]),
                                         op=OP.mult), "sv")
            t_ps_free = t_y
            c.wait(sc, [t_y, t_round_prev])
            t_ti = c.inc(sc.activation(out=tib[:, :, 0:tw], in_=yb[:, :, 0:tw],
                                       func=AF.Identity), "ss")
            c.wait(ve, t_ti)
            t_a = c.inc(ve.scalar_tensor_tensor(out=ab[:, :, 0:tw], in0=tib[:, :, 0:tw],
                                                scalar=0.5, in1=yb[:, :, 0:tw],
                                                op0=OP.add, op1=OP.is_le), "sv")
            ve.scalar_tensor_tensor(out=bb[:, :, 0:tw], in0=tib[:, :, 0:tw],
                                    scalar=0.5, in1=yb[:, :, 0:tw],
                                    op0=OP.subtract, op1=OP.is_gt)
            ve.scalar_tensor_tensor(out=ab[:, :, 0:tw], in0=bb[:, :, 0:tw], scalar=-1.0,
                                    in1=ab[:, :, 0:tw], op0=OP.mult, op1=OP.add)
            t_xq1 = c.inc(ve.tensor_tensor(out=xq1b[:, :, 0:tw], in0=ab[:, :, 0:tw],
                                           in1=tib[:, :, 0:tw], op=OP.add), "sv")
            t_round_prev = t_xq1
            if t0 <= 127 < t0 + tw:
                j = 127 - t0
                t_xq1 = c.inc(ve.tensor_single_scalar(out=xq1b[:, :, j:j + 1],
                                                      in_=xq1b[:, :, j:j + 1],
                                                      scalar=gate_s[:, 0:1], op=OP.mult), "sv")
            if t0 <= 4224 < t0 + tw:
                j = 4224 - t0
                t_xq1 = c.inc(ve.tensor_single_scalar(out=xq1b[:, :, j:j + 1],
                                                      in_=xq1b[:, :, j:j + 1],
                                                      scalar=gater_s[:, 0:1], op=OP.mult), "sv")
            store(xq1_v[:, :, t0:t0 + tw], xq1b[:, :, 0:tw], "Sxq1", pre=t_xq1)
            t_bufs_free = [c.tok("sp"), c.tok("sv"), c.tok("ss"), c.tok("sg"), c.tok("Sx2"), c.tok("Sxq1"), c.tok("Sst1")]
        _es2.close()


def emit_f1(nc, c, env):
    sy, ve, sc, te, gp = nc.sync, nc.vector, nc.scalar, nc.tensor, nc.gpsimd
    ps = env["ps"]
    t1t_v, xq1_v = env["t1t_v"], env["xq1_v"]
    st1_scr, xq2_v, cs2_scr = env["st1_scr"], env["xq2_v"], env["cs2_scr"]
    cw_s, cb_s, ident = env["cw_s"], env["cb_s"], env["ident"]
    ones1, rsw1, rsw2 = env["ones1"], env["rsw1"], env["rsw2"]
    rwa, rwb, rwc, rwd = env["rwa"], env["rwb"], env["rwc"], env["rwd"]
    load, store = env["load"], env["store"]
    env["full_barrier"]()

    with (
        nc.sbuf_tensor("t1t", [128, DC, FF], BF16) as t1t,
        nc.sbuf_tensor("xq1w", [128, DC, 258], BF16) as xq1w,
        nc.sbuf_tensor("hb", [128, FC, 258], F32) as hb,
        nc.sbuf_tensor("cacc", [128, FC, 256], F32) as cacc,
        nc.sbuf_tensor("cs1bc", [128, 258], F32) as cs1bc,
        nc.sbuf_tensor("ti2", [128, 8, 256], I32) as ti2,
        nc.sbuf_tensor("a2", [128, 8, 256], F32) as a2,
        nc.sbuf_tensor("b2", [128, 8, 256], F32) as b2,
        nc.sbuf_tensor("xq2b", [128, FC, 256], BF16) as xq2b,
        nc.sbuf_tensor("st1l", [1, 2, 258], F32) as st1l,
    ):
        tl_t1 = load(t1t[:], t1t_v[:, :, :], "Lt1t", pre=None)
        t_ps_free = None
        t_cs1b_prev = None
        t_ev_pair = [None, None]
        t_hb_free = None
        t_cacc_free = None
        t_row_free = [None, None]
        t_group_prev = None
        t_xq2b_free = None
        t_bc_prev = None
        t_cs2w_prev = None
        t_mn_prev = None
        for ti_ in range(16):
            b = ti_ % 2
            t0 = ti_ * 256
            w0 = PADL + t0 - 1
            tl_x = load(xq1w[:], xq1_v[:, :, w0:w0 + 258], "Lxq1w",
                        pre=c.tok("sp") if ti_ >= 1 else None)
            tl_s = load(st1l[0:1, b, :], st1_scr[0:1, w0:w0 + 258], f"Lst1{b}", pre=t_row_free[b])
            c.wait(ve, [tl_s, t_bc_prev])
            t_rc = c.inc(ve.reciprocal(out=rwa[0:1, 0:258], in_=st1l[0:1, b, :]), "sv")
            c.wait(ve, t_rc)
            t_cs1 = c.inc(ve.tensor_single_scalar(out=rwa[0:1, 0:258], in_=rwa[0:1, 0:258],
                                                  scalar=rsw1[0:1, 0:1], op=OP.mult), "sv")
            t_row_free[b] = t_cs1
            if DEBUG_SCRATCH and ti_ == 0:
                store(env["dbg9"][0:1, 0, :], st1l[0:1, b, :], "Sdbg", pre=tl_s)
                t_row_free[b] = store(env["dbg9"][0:1, 1, :], rwa[0:1, 0:258], "Sdbg",
                                      pre=t_cs1)
            c.wait(te, [t_cs1, t_cs1b_prev])
            t_bc = c.inc(te.matmul(ps[:, 0, 0:258], ones1[:], rwa[0:1, 0:258],
                                   start=True, stop=True), "sp")
            t_bc_prev = t_bc
            c.wait(ve, t_bc)
            t_cs1b = c.inc(ve.tensor_copy(out=cs1bc[:], in_=ps[:, 0, 0:258]), "sv")
            t_cs1b_prev = t_cs1b

            for r in range(16):
                pair = r % 2
                for f2 in range(2):
                    fc = r * 2 + f2
                    bank = 4 + pair * 2 + f2
                    for dc in range(8):
                        if f2 == 0 and dc == 0:
                            c.wait(te, [tl_x, tl_t1] if r == 0 else t_ev_pair[pair])
                        t_mm = te.matmul(ps[:, bank, 0:258],
                                         t1t[:, dc, fc * 128:(fc + 1) * 128],
                                         xq1w[:, dc, :],
                                         start=(dc == 0), stop=(dc == 7))
                t_round = c.inc(t_mm, "sp")
                c.wait(ve, [t_round] + ([t_hb_free] if r == 0 else []))
                t_ev_pair[pair] = c.inc(ve.tensor_tensor(
                    out=hb[:, r * 2:(r + 1) * 2, :],
                    in0=bass.AP(ps, (4 + pair * 2) * 512, [[4096, 128], [512, 2], [1, 258]]),
                    in1=bass.AP(cs1bc, 0, [[258, 128], [0, 2], [1, 258]]),
                    op=OP.mult), "sv")
            t_ps_free = t_ev_pair[1]
            t_h = c.tok("sv")
            if DEBUG_SCRATCH and ti_ == 0:
                t_h = store(env["dbg8"], hb[:], "Sdbg", pre=t_ps_free)
                store(env["dbg9"][0:1, 2, :], cs1bc[0:1, :], "Sdbg", pre=t_cs1b)
                store(env["dbg9"][0:1, 3, :], cs1bc[64:65, :], "Sdbg", pre=t_cs1b)

            c.wait(ve, [t_h, t_cacc_free])
            for fc in range(FC):
                ve.tensor_scalar(out=cacc[:, fc, :], in0=hb[:, fc, 1:257],
                                 scalar1=cw_s[:, fc, 1:2], scalar2=cb_s[:, fc:fc + 1],
                                 op0=OP.mult, op1=OP.add)
            for fc in range(FC):
                ve.scalar_tensor_tensor(out=cacc[:, fc, :], in0=hb[:, fc, 0:256],
                                        scalar=cw_s[:, fc, 0:1], in1=cacc[:, fc, :],
                                        op0=OP.mult, op1=OP.add)
            for fc in range(FC):
                ve.scalar_tensor_tensor(out=cacc[:, fc, :], in0=hb[:, fc, 2:258],
                                        scalar=cw_s[:, fc, 2:3], in1=cacc[:, fc, :],
                                        op0=OP.mult, op1=OP.add)
            t_cr = c.inc(ve.nop(), "sv")
            c.wait(sc, t_cr)
            t_hs = c.inc(sc.activation(out=hb[:, :, 0:256], in_=cacc[:], func=AF.Silu), "ss")
            c.wait(sc, t_hs)
            t_abs = c.inc(sc.activation(out=cacc[:], in_=hb[:, :, 0:256], func=AF.Abs), "ss")

            c.wait(ve, t_abs)
            ve.tensor_tensor(out=cacc[:, 0:16, :], in0=cacc[:, 0:16, :],
                             in1=cacc[:, 16:32, :], op=OP.max)
            ve.tensor_tensor(out=cacc[:, 0:8, :], in0=cacc[:, 0:8, :],
                             in1=cacc[:, 8:16, :], op=OP.max)
            ve.tensor_tensor(out=cacc[:, 0:4, :], in0=cacc[:, 0:4, :],
                             in1=cacc[:, 4:8, :], op=OP.max)
            ve.tensor_tensor(out=cacc[:, 0:2, :], in0=cacc[:, 0:2, :],
                             in1=cacc[:, 2:4, :], op=OP.max)
            t_am = c.inc(ve.tensor_tensor(out=cacc[:, 0, :], in0=cacc[:, 0, :],
                                          in1=cacc[:, 1, :], op=OP.max), "sv")
            t_row = row_absmax(nc, c, env, cacc[:, 0, :], 256, t_am, 1, 2, t_h)
            c.wait(ve, [t_row, t_cs2w_prev])
            t_rc = c.inc(ve.reciprocal(out=rwb[0:1, 0:256], in_=rwa[0:1, 0:256]), "sv")
            c.wait(ve, t_rc)
            t_st2 = c.inc(ve.tensor_single_scalar(out=rwd[0:1, 0:256], in_=rwb[0:1, 0:256],
                                                  scalar=127.0, op=OP.mult), "sv")
            c.wait(ve, t_st2)
            t_rc = c.inc(ve.reciprocal(out=rwb[0:1, 0:256], in_=rwd[0:1, 0:256]), "sv")
            c.wait(ve, t_rc)
            t_cs2 = c.inc(ve.tensor_single_scalar(out=rwb[0:1, 0:256], in_=rwb[0:1, 0:256],
                                                  scalar=rsw2[0:1, 0:1], op=OP.mult), "sv")
            t_cs2w_prev = store(cs2_scr[0:1, t0:t0 + 256], rwb[0:1, 0:256], "Scs2", pre=t_cs2)
            c.wait(te, t_st2)
            t_bc2 = c.inc(te.matmul(ps[:, 3, 0:256], ones1[:], rwd[0:1, 0:256],
                                    start=True, stop=True), "sp")
            c.wait(ve, t_bc2)
            t_y = c.inc(ve.tensor_tensor(out=cacc[:], in0=hb[:, :, 0:256],
                                         in1=bass.AP(ps, 3 * 512, [[4096, 128], [0, 32], [1, 256]]),
                                         op=OP.mult), "sv")
            t_ps_free = t_y
            t_hb_free = t_y
            t_last = None
            for gq in range(4):
                slc = cacc[:, gq * 8:(gq + 1) * 8, :]
                c.wait(sc, [t_y, t_group_prev])
                t_ti = c.inc(sc.activation(out=ti2[:], in_=slc, func=AF.Identity), "ss")
                c.wait(ve, t_ti)
                t_a = c.inc(ve.scalar_tensor_tensor(out=a2[:], in0=ti2[:], scalar=0.5,
                                                    in1=slc, op0=OP.add, op1=OP.is_le), "sv")
                ve.scalar_tensor_tensor(out=b2[:], in0=ti2[:], scalar=0.5,
                                        in1=slc, op0=OP.subtract, op1=OP.is_gt)
                c.wait(ve, t_xq2b_free)
                ve.scalar_tensor_tensor(out=a2[:], in0=b2[:], scalar=-1.0, in1=a2[:],
                                        op0=OP.mult, op1=OP.add)
                t_last = c.inc(ve.tensor_tensor(out=xq2b[:, gq * 8:(gq + 1) * 8, :],
                                                in0=a2[:], in1=ti2[:], op=OP.add), "sv")
                t_group_prev = t_last
            t_cacc_free = t_last
            t_st = store(xq2_v[:, :, t0:t0 + 256], xq2b[:], "Sxq2", pre=t_last)
            t_xq2b_free = t_st


def emit_f2(nc, c, env):
    sy, ve, sc, te, gp = nc.sync, nc.vector, nc.scalar, nc.tensor, nc.gpsimd
    ps = env["ps"]
    t2t_v, xq2_v, cs2_scr = env["t2t_v"], env["xq2_v"], env["cs2_scr"]
    x2_v, y_v = env["x2_v"], env["y_v"]
    ones1, onescol = env["ones1"], env["onescol"]
    rwa, rwb, rwc, rwd = env["rwa"], env["rwb"], env["rwc"], env["rwd"]
    load, store = env["load"], env["store"]
    env["full_barrier"]()

    with (
        nc.sbuf_tensor("t2t", [128, FC, D], BF16) as t2t,
        nc.sbuf_tensor("xq2l", [128, FC, 512], BF16) as xq2l,
        nc.sbuf_tensor("x2l", [128, DC, 512], F32) as x2l,
        nc.sbuf_tensor("r2b", [128, DC, 512], F32) as r2b,
        nc.sbuf_tensor("sq2", [128, 2, 512], F32) as sq2,
        nc.sbuf_tensor("outb", [128, DC, 512], F32) as outb,
        nc.sbuf_tensor("cs2l", [1, 2, 512], F32) as cs2l,
    ):
        tl_t2 = load(t2t[:], t2t_v[:, :, :], "Lt2t", pre=None)
        t_ps_free = None
        t_ev_g = [None, None]
        t_bufs_free = None
        t_row_free = [None, None]
        t_sy_prev = None
        for ti_ in range(8):
            b = ti_ % 2
            t0 = ti_ * 512
            tl_x = load(xq2l[:], xq2_v[:, :, t0:t0 + 512], "Lxq2l",
                        pre=c.tok("sp") if ti_ >= 1 else None)
            tl_c = load(cs2l[0:1, b, :], cs2_scr[0:1, t0:t0 + 512], f"Lcs2{b}", pre=t_row_free[b])
            tl_x2 = load(x2l[:], x2_v[:, :, PADL + t0:PADL + t0 + 512], "Lx2l", pre=t_bufs_free)
            for grp in range(2):
                for mi in range(4):
                    m = grp * 4 + mi
                    for fc in range(FC):
                        if mi == 0 and fc == 0:
                            c.wait(te, [tl_x, tl_t2, t_ev_g[1 - grp]])
                        t_mm = te.matmul(ps[:, 4 + mi, 0:512],
                                         t2t[:, fc, m * 128:(m + 1) * 128],
                                         xq2l[:, fc, :], start=(fc == 0), stop=(fc == FC - 1))
                t_g = c.inc(t_mm, "sp")
                c.wait(ve, [t_g, t_bufs_free])
                t_ev_g[grp] = c.inc(ve.tensor_copy(
                    out=r2b[:, grp * 4:(grp + 1) * 4, :],
                    in_=bass.AP(ps, 4 * 512, [[4096, 128], [512, 4], [1, 512]])), "sv")
            c.wait(te, [t_ev_g[1], tl_c])
            t_bc = c.inc(te.matmul(ps[:, 0, 0:512], ones1[:], cs2l[0:1, b, :],
                                   start=True, stop=True), "sp")
            c.wait(ve, [t_bc, tl_x2])
            ve.tensor_tensor(out=r2b[:], in0=r2b[:],
                             in1=bass.AP(ps, 0, [[4096, 128], [0, 8], [1, 512]]),
                             op=OP.mult)
            t_r2 = c.inc(ve.tensor_tensor(out=r2b[:], in0=r2b[:], in1=x2l[:], op=OP.add), "sv")
            t_sq = [None, None]
            for dc in range(8):
                sl = dc % 2
                c.wait(sc, [t_r2, t_sq[sl]])
                t_s = c.inc(sc.activation(out=sq2[:, sl, :], in_=r2b[:, dc, :],
                                          func=AF.Square), "ss")
                c.wait(te, [t_r2, t_s])
                te.matmul(ps[0:1, 1, 0:512], onescol[:], r2b[:, dc, :],
                          start=(dc == 0), stop=(dc == 7))
                t_sq[sl] = c.inc(te.matmul(ps[0:1, 2, 0:512], onescol[:], sq2[:, sl, :],
                                           start=(dc == 0), stop=(dc == 7)), "sp")
            c.wait(ve, t_sq[1])
            ve.tensor_single_scalar(out=rwa[0:1, 0:512], in_=ps[0:1, 1, 0:512],
                                    scalar=1.0 / D, op=OP.mult)
            ve.tensor_single_scalar(out=rwb[0:1, 0:512], in_=ps[0:1, 2, 0:512],
                                    scalar=1.0 / D, op=OP.mult)
            ve.tensor_tensor(out=rwc[0:1, 0:512], in0=rwa[0:1, 0:512], in1=rwa[0:1, 0:512],
                             op=OP.mult)
            t_var = c.inc(ve.tensor_tensor(out=rwb[0:1, 0:512], in0=rwb[0:1, 0:512],
                                           in1=rwc[0:1, 0:512], op=OP.subtract), "sv")
            c.wait(sc, t_var)
            t_std = c.inc(sc.activation(out=rwb[0:1, 0:512], in_=rwb[0:1, 0:512],
                                        func=AF.Sqrt, bias=np.float32(1e-5)), "ss")
            c.wait(ve, t_std)
            t_rstd = c.inc(ve.reciprocal(out=rwc[0:1, 0:512], in_=rwb[0:1, 0:512]), "sv")
            c.wait(te, [t_rstd])
            te.matmul(ps[:, 3, 0:512], ones1[:], rwa[0:1, 0:512], start=True, stop=True)
            t_bc2 = c.inc(te.matmul(ps[:, 4, 0:512], ones1[:], rwc[0:1, 0:512],
                                    start=True, stop=True), "sp")
            c.wait(ve, [t_bc2, t_sy_prev])
            ve.tensor_tensor(out=outb[:], in0=r2b[:],
                             in1=bass.AP(ps, 3 * 512, [[4096, 128], [0, 8], [1, 512]]),
                             op=OP.subtract)
            t_out = c.inc(ve.tensor_tensor(out=outb[:], in0=outb[:],
                                           in1=bass.AP(ps, 4 * 512, [[4096, 128], [0, 8], [1, 512]]),
                                           op=OP.mult), "sv")
            t_ps_free = t_out
            t_st = store(y_v[:, :, t0:t0 + 512], outb[:], "Sy", pre=t_out)
            t_sy_prev = t_st
            t_bufs_free = [t_st, c.tok("sv"), c.tok("ss")]
            t_row_free[b] = t_out


def _stage_inputs(inputs):
    x = np.asarray(inputs["L0_attn_x"], dtype=np.float32)
    Wq_ = np.ascontiguousarray(inputs["L0_attn_Wq"], dtype=np.float32)
    Wk_ = np.ascontiguousarray(inputs["L0_attn_Wk"], dtype=np.float32)
    Wv_ = np.ascontiguousarray(inputs["L0_attn_Wv"], dtype=np.float32)
    Wo_ = np.ascontiguousarray(inputs["L0_attn_Wo"], dtype=np.float32)
    bq = np.asarray(inputs["L0_attn_bq"], dtype=np.float32)
    bk = np.asarray(inputs["L0_attn_bk"], dtype=np.float32)
    bv = np.asarray(inputs["L0_attn_bv"], dtype=np.float32)
    bo = np.asarray(inputs["L0_attn_bo"], dtype=np.float32)
    w1_ = np.ascontiguousarray(inputs["L0_ffn_w1"], dtype=np.float32)
    w2_ = np.ascontiguousarray(inputs["L0_ffn_w2"], dtype=np.float32)
    cw = np.asarray(inputs["L0_ffn_conv_w"], dtype=np.float32).reshape(FF, 3)
    cb = np.asarray(inputs["L0_ffn_conv_b"], dtype=np.float32)

    bq_s = np.ascontiguousarray(bq.reshape(8, 128).T)
    bk_s = np.ascontiguousarray(bk.reshape(8, 128).T)
    bo_s = np.ascontiguousarray(bo.reshape(8, 128).T)
    bv_r = np.ascontiguousarray(bv.reshape(1, D))
    cw_s = np.ascontiguousarray(cw.reshape(FC, 128, 3).transpose(1, 0, 2))
    cb_s = np.ascontiguousarray(cb.reshape(FC, 128).T)

    ident = np.eye(128, dtype=np.float32)
    m0 = np.triu(np.ones((128, 128), dtype=np.float32))
    maskc = np.ascontiguousarray(np.broadcast_to(m0[:, None, :], (128, H, 128)))
    selden = np.zeros((128, DC, H), dtype=np.float32)
    selbc = np.zeros((16, DC, 128), dtype=np.float32)
    for dc in range(DC):
        for p in range(128):
            h = 2 * dc + (1 if p >= 64 else 0)
            selden[p, dc, h] = 1.0
            selbc[h, dc, p] = 1.0
    ones1 = np.ones((1, 128), dtype=np.float32)
    onescol = np.ones((128, 1), dtype=np.float32)

    shared = dict(Wq=Wq_, Wk=Wk_, Wv=Wv_, Wo=Wo_, bq_i=bq_s, bk_i=bk_s, bv_i=bv_r,
                  bo_i=bo_s, w1=w1_, w2=w2_, cw_i=cw_s, cb_i=cb_s, id_i=ident,
                  mask_i=maskc, sd_i=selden, sb_i=selbc, o1_i=ones1, oc_i=onescol)

    in_maps = []
    for core in range(8):
        bi, half = core // 2, core % 2
        xb = x[bi]
        xwin = np.zeros((D, NW), dtype=np.float32)
        if half == 0:
            xwin[:, PADL:] = xb[0:4224].T
            xpre = np.zeros((D, NP), dtype=np.float32)
            g, gr = 0.0, 1.0
        else:
            xwin[:, 0:4224] = xb[3968:8192].T
            xpre = np.ascontiguousarray(xb[0:NP].T)
            g, gr = 1.0, 0.0
        m = dict(shared)
        m["xw"] = np.ascontiguousarray(xwin)
        m["xp"] = np.ascontiguousarray(xpre)
        m["gate"] = np.full((128, 1), g, dtype=np.float32)
        m["gater"] = np.full((128, 1), gr, dtype=np.float32)
        in_maps.append(m)
    return in_maps


def kernel(**inputs):
    in_maps = _stage_inputs(inputs)
    nc = build_program()
    res = run_bass_kernel_spmd(nc, in_maps, core_ids=list(range(8)))
    out = np.empty((4, 8192, D), dtype=np.float32)
    for core in range(8):
        bi, half = core // 2, core % 2
        out[bi, half * NO:(half + 1) * NO, :] = res.results[core]["yT"].T
    return out
